# version 38
# baseline (speedup 1.0000x reference)
import numpy as np
import ml_dtypes
import concourse.bass as bass
import concourse.mybir as mybir
from concourse.bass_utils import run_bass_kernel_spmd
import sys

DEBUG_TAGS = False
_TAGS = []

F32 = mybir.dt.float32
BF16 = mybir.dt.bfloat16
AF = mybir.ActivationFunctionType
ALU = mybir.AluOpType
AX = mybir.AxisListType

B, S, D = 4, 4096, 1024
NOWN = 2048
NEG = -float(2.0 ** 99)
WIN = 3584
KV = {0: 128, 1: 768}
Q0 = {0: 640, 1: 1280}
X1AA = 766
SOWN = 1408
NHALO = SOWN - X1AA
NSOWN = WIN - SOWN
HID = 2730
NHC = 22
PADTOP = 1536
DEBUG_SPILL = False
NKV = {l: WIN - KV[l] for l in (0, 1)}
NQ = {l: WIN - Q0[l] for l in (0, 1)}
NQC = {l: NQ[l] // 128 for l in (0, 1)}
NKVC = {l: NKV[l] // 128 for l in (0, 1)}

WO_BLKS = {0: [(766, 322), (1088, 320), (1408, 512), (1920, 512), (2432, 512), (2944, 512), (3456, 128)],
           1: [(1280, 128), (1408, 512), (1920, 512), (2432, 512), (2944, 512), (3456, 128)]}
MLP_BLKS = {0: [(767, 257), (1024, 256), (1280, 128)]
               + [(1408 + 256 * i, 256) for i in range(8)] + [(3456, 128)],
            1: [(1408 + 256 * i, 256) for i in range(8)] + [(3456, 128)]}
MLPW = 257


def _hc_w(hc):
    return 128 if hc < NHC - 1 else HID - 128 * (NHC - 1)


def _blocks(total, step=512):
    res, o = [], 0
    while o < total:
        res.append((o, min(step, total - o)))
        o += step
    return res


def _rotary_tables(abs_t0):
    DH, WINDOW, QK = 64, 512, 8.0
    half = np.arange(0, DH, 2, dtype=np.float64)
    inv = 1.0 / 10000.0 ** (half / DH)
    t = abs_t0 + np.arange(WIN, dtype=np.float64)
    fr = t[:, None] * inv[None, :]
    fr = np.concatenate([fr, fr], -1)
    base = (half + 0.4 * DH) / (1.4 * DH)
    power = np.clip((t - S // 2) / float(WINDOW // 2), -12.0, 12.0)
    sc = base[None, :] ** power[:, None]
    sc = np.concatenate([sc, sc], -1)
    cos, sin = np.cos(fr), np.sin(fr)
    sgn = np.concatenate([-np.ones(32), np.ones(32)])[None, :]
    outs = [(cos * sc) * QK, (sin * sc) * sgn * QK, cos / sc, (sin / sc) * sgn]

    def dup(a):
        a = a.T.astype(np.float32)
        return np.ascontiguousarray(np.concatenate([a, a], 0))
    return [dup(o) for o in outs]


def _host_consts():
    c = {}
    c["iden"] = np.eye(128, dtype=np.float32)
    c["ones_row"] = np.ones((1, 640), dtype=ml_dtypes.bfloat16)
    c["ones_col"] = np.ones((128, 1), dtype=ml_dtypes.bfloat16)
    s = np.arange(128)[:, None]
    r = np.arange(128)[None, :]
    c["mask_first"] = np.where(s >= r, 0.0, NEG).astype(np.float32)
    c["mask_last"] = np.where(s <= r, 0.0, NEG).astype(np.float32)
    l2 = np.zeros((128, 32), dtype=np.float32)
    for i in range(4):
        l2[0:64, 8 * i + 2 * i] = 1
        l2[64:128, 8 * i + 2 * i + 1] = 1
    c["l2ones"] = l2.astype(ml_dtypes.bfloat16)
    rp = np.zeros((8, 512), dtype=np.float32)
    for i in range(4):
        rp[2 * i, 128 * i:128 * i + 64] = 1
        rp[2 * i + 1, 128 * i + 64:128 * i + 128] = 1
    c["rep8"] = rp.astype(ml_dtypes.bfloat16)
    c["ones1f"] = np.ones((1, 128), dtype=np.float32)
    return c


_CONST = _host_consts()
_TABS = {0: _rotary_tables(-1536), 1: _rotary_tables(512)}
_PROG = [None]

ENGS = ("sync", "gpsimd", "tensor", "vector", "scalar")
SEMS = ("dX", "dW", "dS", "sT", "sV", "sS", "dL0", "dL1", "dL2", "dL3", "dLC", "dU", "dT0", "dT1", "dWa", "dWb", "dO0", "dO1", "dSa", "dSb", "dWc", "dWd")


class Sched:
    def __init__(self):
        self.ops = {e: [] for e in ENGS}
        self.cnt = {s: 0 for s in SEMS}

    def op(self, eng, fn, waits=(), inc=None, incn=1):
        w = [t for t in waits if t is not None and t[1] > 0]
        self.ops[eng].append((w, inc, incn, fn))
        if inc is not None:
            self.cnt[inc] += incn

    def tok(self, sem):
        return (sem, self.cnt[sem])

    def all_toks(self):
        return [self.tok(s) for s in SEMS]


def build_program():
    nc = bass.Bass("TRN2", target_bir_lowering=False)
    sch = Sched()
    semh = {s: nc.alloc_semaphore(s) for s in SEMS}

    def din(name, shape, dt=F32):
        return nc.dram_tensor(name, list(shape), dt, kind="ExternalInput").ap()

    xseq = din("xseq", (S, D))
    xwin = din("xwin", (WIN, D))
    padpen_d = din("padpen", (1, WIN), BF16)
    padedge_d = din("padedge", (128, 1))
    wq_d = din("wq", (D, D)); wk_d = din("wk", (D, D)); wv_d = din("wv", (D, D))
    wo_a_d = din("wo_a", (D, D))
    bq_d = din("bq", (D,)); bk_d = din("bk", (D,)); bv_d = din("bv", (D,)); bo_d = din("bo", (D,))
    wqkv_d = din("wqkv", (2, D, 1536))
    qs_d = din("qscale", (2, 64)); ks_d = din("kscale", (2, 64))
    wof_d = din("wo_f", (2, 512, D))
    lng_d = din("ln_g", (2, D)); lnb_d = din("ln_b", (2, D))
    w1_d = din("w1", (2, D, 2 * HID)); b1_d = din("b1", (2, 2 * HID))
    w2_d = din("w2", (2, HID, D)); b2_d = din("b2", (2, D))
    iden_d = din("iden", (128, 128))
    onesr_d = din("ones_row", (1, 640), BF16)
    onesc_d = din("ones_col", (128, 1), BF16)
    maskf_d = din("mask_first", (128, 128))
    maskl_d = din("mask_last", (128, 128))
    l2o_d = din("l2ones", (128, 32), BF16)
    rep8_d = din("rep8", (8, 512), BF16)
    ones1f_d = din("ones1f", (1, 128))
    tab_d = [din(f"tab{i}", (128, WIN)) for i in range(4)]
    out_d = nc.dram_tensor("out", [NOWN, D], F32, kind="ExternalOutput").ap()
    spill_d = nc.dram_tensor("attn_spill", [128, 8 * NOWN], BF16, kind="ExternalOutput" if DEBUG_SPILL else "Internal").ap()
    w2bf_d = nc.dram_tensor("w2bf", [2 * 8 * 128 * 2816], BF16, kind="Internal").ap()

    free = [[17024, 229376]]
    live = {}

    def salloc(name, shape, dt, high=False):
        al = 64
        bpp = (int(np.prod(shape[1:])) * mybir.dt.size(dt) + al - 1) // al * al
        segs = sorted(free, reverse=high)
        for seg in segs:
            if seg[1] - seg[0] < bpp:
                continue
            off = ((seg[1] - bpp) // al * al) if high else seg[0]
            if off < seg[0]:
                continue
            t = nc.alloc_sbuf_tensor_at(name, list(shape), dt, offset=off)
            live[name] = [off, off + bpp]
            free.remove(seg)
            if seg[0] < off:
                free.append([seg[0], off])
            if off + bpp < seg[1]:
                free.append([off + bpp, seg[1]])
            return t
        raise RuntimeError(f"SBUF OOM {name}: need {bpp}, free={sorted(free)}, live={ {k: (v[1]-v[0]) for k, v in sorted(live.items(), key=lambda x: -x[1][1]+x[1][0])} }")

    def sfree(*names):
        for n in names:
            free.append(live.pop(n))
        free.sort()
        m = [free[0]]
        for seg in free[1:]:
            if seg[0] <= m[-1][1]:
                m[-1][1] = max(m[-1][1], seg[1])
            else:
                m.append(seg)
        free[:] = m

    ps_t = nc.alloc_psum_tensor("ps", [128, 4096], F32)

    def bank(i, p0=0, p1=128, c0=0, cw=512):
        return ps_t[p0:p1, 512 * i + c0:512 * i + c0 + cw]

    def pscols(c0, cw, p0=0, p1=128):
        return ps_t[p0:p1, c0:c0 + cw]

    def dma(queue, sem, outap, inap, waits=(), slow=False):
        sch.op(queue, lambda e, o=outap, i=inap, s=slow: e.dma_start(
            o, i, allow_slow_non_contiguous=s), waits, sem, 16)
        return sch.tok(sem)

    def mm(outap, lhsT, rhs, start, stop, waits=(), inc="sT", tp=None):
        if DEBUG_TAGS:
            _TAGS.append(sys._getframe(1).f_lineno)
        sch.op("tensor",
               lambda e, o=outap, l=lhsT, r=rhs, st=start, sp=stop, t=tp:
               e.matmul(o, l, r, start=st, stop=sp, tile_position=t), waits, inc)
        return sch.tok("sT")

    def tsp(outap, in_ap, waits=()):
        if DEBUG_TAGS:
            _TAGS.append(-sys._getframe(1).f_lineno)
        sch.op("tensor", lambda e, o=outap, i=in_ap: e.transpose(o, i, c_iden[:]), waits, "sT")
        return sch.tok("sT")

    def vo(fn, waits=(), inc="sV"):
        sch.op("vector", fn, waits, inc)
        return sch.tok("sV")

    def so(fn, waits=(), inc="sS"):
        sch.op("scalar", fn, waits, inc)
        return sch.tok("sS")

    def sb_ap(t, part0, nparts, col0, ncols):
        return bass.AP(t, part0 * t.shape[1] + col0, [[t.shape[1], nparts], [1, ncols]])

    c_iden = salloc("c_iden", [128, 128], F32)
    c_onesr = salloc("c_onesr", [1, 128], BF16)
    c_onesc = salloc("c_onesc", [128, 1], BF16)
    c_maskf = salloc("c_maskf", [128, 128], BF16)
    c_maskl = salloc("c_maskl", [128, 128], BF16)
    c_idenb = salloc("c_idenb", [128, 128], BF16)
    c_l2o = salloc("c_l2o", [128, 32], BF16)
    c_rep8 = salloc("c_rep8", [8, 512], BF16)
    c_1f = salloc("c_1f", [1, 128], F32)
    qkvb = salloc("qkvb", [64, 48], F32)
    b2c = salloc("b2c", [128, 16], F32)
    lgpp = salloc("lgpp", [128, 16], F32)
    lbpp = salloc("lbpp", [128, 16], F32)
    b1a = salloc("b1a", [128, 44], F32)
    b1g = salloc("b1g", [128, 44], F32)
    qspp = salloc("qspp", [128, 8], F32)
    st_rs = salloc("st_rs", [128, 60], F32)
    st_nb = salloc("st_nb", [128, 60], F32)
    padedge = salloc("padedge", [128, 1], F32)
    Mb = salloc("Mb", [128, 4], F32)
    cflt = salloc("cflt", [128, 1], F32)
    tok_cf = vo(lambda e: e.memset(cflt[:], 1.0e-5))
    nc.const_aps.register(F32, 1.0e-5, cflt[:, 0:1])

    dma("gpsimd", "dW", c_iden[:], iden_d)
    dma("gpsimd", "dW", c_onesr[:], onesr_d[0:1, 0:128])
    dma("gpsimd", "dW", c_onesc[:], onesc_d)
    dma("gpsimd", "dW", c_maskf[:], maskf_d)
    dma("gpsimd", "dW", c_maskl[:], maskl_d)
    dma("gpsimd", "dW", c_idenb[:], iden_d)
    dma("gpsimd", "dW", c_l2o[:], l2o_d)
    dma("gpsimd", "dW", c_rep8[:], rep8_d)
    dma("gpsimd", "dW", c_1f[:], ones1f_d)
    dma("gpsimd", "dW", padedge[:], padedge_d)
    for j, bsrc in enumerate((bq_d, bk_d, bv_d)):
        dma("gpsimd", "dW", qkvb[:, 16 * j:16 * j + 16],
            bass.AP(bsrc.tensor, 0, [[1, 64], [64, 16]]), slow=True)
    for l in range(2):
        dma("gpsimd", "dW", b2c[:, 8 * l:8 * l + 8],
            bass.AP(b2_d.tensor, l * D, [[1, 128], [128, 8]]), slow=True)
        dma("gpsimd", "dW", lgpp[:, 8 * l:8 * l + 8],
            bass.AP(lng_d.tensor, 1024 * l, [[1, 128], [128, 8]]), slow=True)
        dma("gpsimd", "dW", lbpp[:, 8 * l:8 * l + 8],
            bass.AP(lnb_d.tensor, 1024 * l, [[1, 128], [128, 8]]), slow=True)
        for (dst, off) in ((b1a, 0), (b1g, HID)):
            dma("gpsimd", "dW", dst[:, 22 * l:22 * l + 21],
                bass.AP(b1_d.tensor, l * 2 * HID + off, [[1, 128], [128, 21]]), slow=True)
            dma("gpsimd", "dW", dst[0:42, 22 * l + 21:22 * l + 22],
                bass.AP(b1_d.tensor, l * 2 * HID + off + 2688, [[1, 42], [42, 1]]), slow=True)
        for half in range(2):
            p0 = 64 * half
            for (cc, src, o1, o2) in ((0, qs_d, 0, 32), (2, ks_d, 0, 32)):
                dma("gpsimd", "dW", qspp[p0:p0 + 64, 4 * l + cc:4 * l + cc + 1],
                    bass.AP(src.tensor, 64 * l, [[1, 64], [64, 1]]), slow=True)
                dma("gpsimd", "dW", qspp[p0:p0 + 32, 4 * l + cc + 1:4 * l + cc + 2],
                    bass.AP(src.tensor, 64 * l + 32, [[1, 32], [32, 1]]), slow=True)
                dma("gpsimd", "dW", qspp[p0 + 32:p0 + 64, 4 * l + cc + 1:4 * l + cc + 2],
                    bass.AP(src.tensor, 64 * l, [[1, 32], [32, 1]]), slow=True)
    tok_consts = sch.tok("dW")
    tok_bq = vo(lambda e: e.tensor_scalar_mul(qkvb[:, 0:16], qkvb[:, 0:16], 0.125),
                waits=[tok_consts])

    xnseq = salloc("xnseq", [128, 8 * S], BF16)
    xtm = salloc("xtm", [128, 4 * 1024], F32)
    st_sum = salloc("st_sum", [128, 60], F32)
    st_sq = salloc("st_sq", [128, 60], F32)
    xn_tm = salloc("xn_tm", [128, 1024], F32)

    CH = [("seq", i) for i in range(32)] + [("win", i) for i in range(28)]

    def xsrc(which, i):
        src = xseq if which == "seq" else xwin
        return src[128 * i:128 * i + 128, :]

    lc_state = dict(xfree=[], reg=0, regfree=[None, None], xtm=xtm, xntm=None)

    xtm_free = [[], [], [], []]
    for ci, (which, i) in enumerate(CH):
        sl = ci % 4
        tl = dma("sync", f"dL{sl}", xtm[:, 1024 * sl:1024 * sl + 1024], xsrc(which, i),
                 waits=xtm_free[sl])
        tv = vo(lambda e, s=sl, c=ci: e.reduce_sum(
            st_sum[:, c:c + 1], xtm[:, 1024 * s:1024 * s + 1024], AX.X), waits=[tl])
        tq = so(lambda e, s=sl, c=ci: e.activation(
            xtm[:, 1024 * s:1024 * s + 1024], xtm[:, 1024 * s:1024 * s + 1024], AF.Square,
            accum_out=st_sq[:, c:c + 1]), waits=[tl, tok_cf, tv])
        xtm_free[sl] = [tv, tq]
    t1 = so(lambda e: e.activation(st_sum[:], st_sum[:], AF.Copy, scale=1.0 / 1024),
            waits=[sch.tok("sV")])
    t2 = so(lambda e: e.activation(st_nb[:], st_sum[:], AF.Square), waits=[t1])
    t3 = vo(lambda e: e.scalar_tensor_tensor(st_sq[:], st_sq[:], 1.0 / 1024, st_nb[:],
                                             ALU.mult, ALU.subtract), waits=[t2])
    t4 = so(lambda e: e.activation(st_sq[:], st_sq[:], AF.Ln, bias=1.0e-5), waits=[t3])
    t5 = so(lambda e: e.activation(st_rs[:], st_sq[:], AF.Exp, scale=-0.5), waits=[t4])
    tok_stats = vo(lambda e: e.scalar_tensor_tensor(st_nb[:], st_sum[:], -1.0, st_rs[:],
                                                    ALU.mult, ALU.mult), waits=[t5])

    def ln_chunk(which, i, statcol, dst_bf=None, bfcol=0, dst_f32=None, f32col=0, gate=()):
        xt, xnt = lc_state["xtm"], lc_state["xntm"]
        tl = dma("sync", "dLC", xt[:, 0:1024], xsrc(which, i), waits=list(lc_state["xfree"]) + list(gate))
        ta = so(lambda e, c=statcol, x1=xt, x2=xnt: e.activation(
            x2[:], x1[:, 0:1024], AF.Identity,
            scale=st_rs[:, c:c + 1], bias=st_nb[:, c:c + 1]), waits=[tl, tok_stats])
        lc_state["xfree"] = [ta]
        if lc_state.get("regpin"):
            reg = 0
        else:
            reg = lc_state["reg"]
            lc_state["reg"] ^= 1
        tt = None
        for t in range(8):
            tt = tsp(pscols(1024 * reg + 128 * t, 128),
                     lc_state["xntm"][:, 128 * t:128 * t + 128],
                     waits=[ta, lc_state["regfree"][reg]] + list(gate))
        ev = None
        for (dd, dc) in ((dst_bf, bfcol), (dst_f32, f32col)):
            if dd is None:
                continue
            sd = dd.shape[1] // 8
            ev = so(lambda e, r=reg, d=dd, c=dc, s=sd: e.activation(
                bass.AP(d, c, [[d.shape[1], 128], [s, 8], [1, 128]]),
                bass.AP(ps_t, 1024 * r, [[4096, 128], [128, 8], [1, 128]]),
                AF.Copy), waits=[tt])
        lc_state["regfree"][reg] = ev

    lc_state["xntm"] = xn_tm
    lc_state["xfree"] = [sch.tok("sS"), sch.tok("sV")]
    for i in range(32):
        ln_chunk("seq", i, i, dst_bf=xnseq, bfcol=128 * i)
    tok_xn = sch.all_toks()
    sfree("st_sum", "st_sq", "xtm", "xn_tm")

    wqb = salloc("wqb", [128, 8 * 1024], BF16)
    wkb = salloc("wkb", [128, 8 * 1024], BF16)
    wvb = salloc("wvb", [128, 8 * 1024], BF16)
    BARB = [sch.tok("sT"), sch.tok("sV"), sch.tok("sS")]
    for kt in range(8):
        dma("gpsimd", "dW", wqb[:, 1024 * kt:1024 * kt + 1024], wq_d[128 * kt:128 * kt + 128, :], waits=BARB)
        dma("gpsimd", "dW", wkb[:, 1024 * kt:1024 * kt + 1024], wk_d[128 * kt:128 * kt + 128, :], waits=BARB)
        dma("gpsimd", "dW", wvb[:, 1024 * kt:1024 * kt + 1024], wv_d[128 * kt:128 * kt + 128, :], waits=BARB)
    tok_wqkv = sch.tok("dW")

    attn_out = salloc("attn_out", [128, 2 * NOWN], BF16)
    repsG = salloc("repsG", [64, 512], F32)
    oddst = salloc("oddst", [64, NOWN], BF16)
    ptb = salloc("ptb", [128, 5 * 512], BF16)
    lnrecB = salloc("lnrecB", [33, 512], F32)
    recG = lnrecB[0:1, :]
    lnG = lnrecB[32:33, :]
    qstg4 = [[salloc(f"qstg{i}{j}", [65, NOWN], BF16) for j in range(2)] for i in range(2)]
    kstg4 = [[salloc(f"kstg{i}{j}", [65, S], BF16) for j in range(2)] for i in range(2)]
    vb2 = [salloc(f"vb{i}", [128, 32 * 132], BF16) for i in range(2)]
    sqscr = salloc("sqscr", [64, 2048], BF16)
    nrmx = salloc("nrmx", [1, 32], F32)
    cch = salloc("cch", [1, 16], F32)

    tko4 = [None, None]
    for i in range(2):
        t0k = vo(lambda e, a=i: e.memset(kstg4[a][0][64:65, :], 1.0),
                 waits=BARB if i == 0 else [])
        tko4[i] = vo(lambda e, a=i: e.memset(kstg4[a][1][64:65, :], 1.0), waits=[t0k])

    gstate = {"p1": [None, None, None, None],
              "st": [None, None, None, None],
              "pt": [None, None, None, None, None],
              "pv": [None, None],
              "rep": None, "vb": [None, None],
              "tmu_prev": None}
    nmst = {"sq": [None, None, None, None], "nrm": [None, None], "cch": [None, None]}
    spill_tok = [None, None]
    hargs = {}

    def gA2(pairt):
        H0, H1 = 2 * pairt, 2 * pairt + 1
        qstg2 = qstg4[pairt % 2]
        kstg2 = kstg4[pairt % 2]
        vb = vb2[pairt % 2]
        gT = sch.tok("sT")
        tms = vo(lambda e, v=vb: e.memset(
            bass.AP(v, 64, [[v.shape[1], 128], [132, 32], [66, 2]]), 1.0),
            waits=[gT, tok_xn[4], tok_xn[5], gstate["vb"][pairt % 2]])
        for tc in range(32):
            pb = tc % 2
            tmm = None
            for kt in range(8):
                tmm = mm(bank(pb, 0, 128, 0, 128),
                         xnseq[:, 4096 * kt + 128 * tc:4096 * kt + 128 * tc + 128],
                         wvb[:, 1024 * kt + 128 * pairt:1024 * kt + 128 * pairt + 128],
                         start=(kt == 0), stop=(kt == 7),
                         waits=([tok_wqkv, tok_xn[4], tok_xn[5], gstate["p1"][pb], gT]
                                if kt == 0 else []))
            ev = vo(lambda e, b=pb, t=tc, v=vb: e.tensor_copy(
                bass.AP(v, 132 * t, [[v.shape[1], 128], [66, 2], [1, 64]]),
                bass.AP(ps_t, 512 * b, [[4096, 128], [64, 2], [1, 64]])),
                waits=[tmm, tms])
            gstate["p1"][pb] = ev
            if tc % 4 == 3:
                yield
        hargs[("v", pairt)] = sch.tok("sV")
        evq = [None, None]
        evk = [None, None]
        for src_w, dstl, bias0, scl, nn, tag in (
                (wqb, qstg2, 0, 0.125, NOWN, "q"), (wkb, kstg2, 16, 1.0, S, "k")):
            for cb, (c0, w) in enumerate(_blocks(nn)):
                pb = cb % 2
                tmm = None
                for kt in range(8):
                    for par in range(2):
                        tmm = mm(bank(pb, 64 * par, 64 * par + 64, 0, w),
                                 src_w[:, 1024 * kt + 64 * (H0 + par):1024 * kt + 64 * (H0 + par) + 64],
                                 xnseq[:, 4096 * kt + c0:4096 * kt + c0 + w],
                                 start=(kt == 0), stop=(kt == 7), tp=(0, 64 * par),
                                 waits=([gstate["p1"][pb], tok_bq,
                                         hargs.get(("stgfree", pairt % 2))] if kt == 0 else []))
                ev = None
                for par in range(2):
                    ev = vo(lambda e, b=pb, c=c0, ww=w, p=par, d=dstl, bc=bias0, s=scl:
                            e.tensor_scalar(d[p][0:64, c:c + ww],
                                            bank(b, 64 * p, 64 * p + 64, 0, ww),
                                            s, qkvb[0:64, bc + H0 + p:bc + H0 + p + 1],
                                            ALU.mult, ALU.add),
                            waits=[tmm])
                gstate["p1"][pb] = ev
                if cb % 2 == 1:
                    yield
        tko1 = tko4[pairt % 2]
        tok_stg = sch.tok("sV")
        tkon = [tko1, tko1]
        for par in range(2):
            blks = [(qstg2[par], c) for c in range(0, NOWN, 512)] + \
                   [(kstg2[par], c) for c in range(0, S, 512)]
            nb0 = 16 * par
            sqfree = nmst["sq"]
            tred = None
            for bi, (stg, c0) in enumerate(blks):
                sl = bi % 4
                nb = bi % 2
                tsq = vo(lambda e, t=stg, c=c0, s=sl: e.tensor_tensor(
                    sqscr[:, 512 * s:512 * s + 512], t[0:64, c:c + 512],
                    t[0:64, c:c + 512], ALU.mult),
                    waits=[tok_stg, sqfree[sl]])
                tnmm = mm(bank(nb, 0, 1, 0, 512), c_onesc[0:64, 0:1],
                          sqscr[0:64, 512 * sl:512 * sl + 512],
                          start=True, stop=True, waits=[tsq, gstate["p1"][nb]])
                sqfree[sl] = tnmm
                tred = vo(lambda e, b=bi, n=nb, p=par: e.tensor_reduce(
                    nrmx[0:1, nb0 + b:nb0 + b + 1], bank(n, 0, 1, 0, 512), AX.X, ALU.max),
                    waits=[tnmm, nmst["nrm"][par] if bi < 2 else None])
                gstate["p1"][nb] = tred
                yield
            cb = 8 * par
            tq = vo(lambda e: e.tensor_reduce(
                cch[0:1, cb + 0:cb + 1], nrmx[0:1, nb0:nb0 + 4], AX.X, ALU.max),
                waits=[tred, nmst["cch"][par]])
            tk = vo(lambda e: e.tensor_reduce(
                cch[0:1, cb + 1:cb + 2], nrmx[0:1, nb0 + 4:nb0 + 12], AX.X, ALU.max),
                waits=[tq])
            nmst["nrm"][par] = tk
            tl1 = so(lambda e: e.activation(cch[0:1, cb + 2:cb + 3],
                                            cch[0:1, cb + 0:cb + 1], AF.Ln), waits=[tq])
            tl2 = so(lambda e: e.activation(cch[0:1, cb + 3:cb + 4],
                                            cch[0:1, cb + 1:cb + 2], AF.Ln), waits=[tk])
            tadd = vo(lambda e: e.tensor_tensor(cch[0:1, cb + 4:cb + 5],
                                                cch[0:1, cb + 2:cb + 3],
                                                cch[0:1, cb + 3:cb + 4], ALU.add),
                      waits=[tl1, tl2])
            tC = so(lambda e: e.activation(cch[0:1, cb + 5:cb + 6],
                                           cch[0:1, cb + 4:cb + 5], AF.Exp, scale=0.5),
                    waits=[tadd])
            tneg = vo(lambda e: e.tensor_scalar(cch[0:1, cb + 6:cb + 7],
                                                cch[0:1, cb + 5:cb + 6], -1.0, None,
                                                ALU.mult),
                      waits=[tC])
            trow = vo(lambda e, p=par: e.tensor_scalar(
                qstg2[p][64:65, 0:NOWN], kstg2[p][64:65, 0:NOWN],
                cch[0:1, cb + 6:cb + 7], None, ALU.mult),
                waits=[tneg, tkon[par], hargs.get(("stgfree", pairt % 2))])
            nmst["cch"][par] = trow
            hargs[("tnm", H0 + par)] = trow
            yield

    def gB(H):
        par = H % 2
        pairt, odd = H // 2, H % 2
        qstg, kstg = qstg4[pairt % 2][par], kstg4[pairt % 2][par]
        vb = vb2[pairt % 2]
        tnm = hargs.pop(("tnm", H))
        tokv = hargs.get(("v", pairt))
        tmu = gstate["tmu_prev"]
        for qg in range(4):
            pvb = qg % 2
            pend = []
            tpv = None
            for kc in range(32):
                stb = kc % 4
                sbk = (4, 5, 2, 3)[stb]
                tmm = mm(bank(sbk), kstg[0:65, 128 * kc:128 * kc + 128],
                         qstg[0:65, 512 * qg:512 * qg + 512],
                         start=True, stop=True,
                         waits=[tnm, tokv, gstate["st"][stb]] +
                               ([gstate["pv"][pvb]] if kc == 0 else []))
                texp = so(lambda e, b=sbk, p=kc % 5: e.activation(
                    ptb[:, 512 * p:512 * p + 512], bank(b), AF.Exp),
                    waits=[tmm, gstate["pt"][kc % 5]])
                gstate["st"][stb] = texp
                pend.append((kc, texp))
                while len(pend) > 4:
                    k0, te0 = pend.pop(0)
                    tpv = mm(bank(6 + pvb, 0, 65, 0, 512),
                             vb[:, 132 * k0 + 66 * odd:132 * k0 + 66 * odd + 65],
                             ptb[:, 512 * (k0 % 5):512 * (k0 % 5) + 512],
                             start=(k0 == 0), stop=False, waits=[te0])
                    gstate["pt"][k0 % 5] = tpv
                if kc % 2 == 1:
                    yield
            for (k0, te0) in pend:
                tpv = mm(bank(6 + pvb, 0, 65, 0, 512),
                         vb[:, 132 * k0 + 66 * odd:132 * k0 + 66 * odd + 65],
                         ptb[:, 512 * (k0 % 5):512 * (k0 % 5) + 512],
                         start=(k0 == 0), stop=(k0 == 31), waits=[te0])
                gstate["pt"][k0 % 5] = tpv
            tl1 = so(lambda e, b=pvb: e.activation(lnG[:], bank(6 + b, 64, 65, 0, 512), AF.Ln),
                     waits=[tpv])
            tl2 = so(lambda e: e.activation(recG[:], lnG[:], AF.Exp, scale=-1.0),
                     waits=[tl1, gstate["rep"]])
            rb = 6 + (1 - pvb)
            trep = mm(bank(rb, 0, 64, 0, 512), c_1f[0:1, 0:64], recG[:],
                      start=True, stop=True, waits=[tl2, gstate["pv"][1 - pvb]])
            gstate["rep"] = trep
            tev = vo(lambda e, r=rb: e.tensor_copy(repsG[:], bank(r, 0, 64, 0, 512)),
                     waits=[trep, tmu])
            dst = attn_out if odd == 0 else oddst
            dc = (2048 * (pairt % 2) + 512 * qg) if odd == 0 else (512 * qg)
            tmu = vo(lambda e, b=pvb, d=dst, c=dc: e.tensor_mul(
                sb_ap(d, 0, 64, c, 512), bank(6 + b, 0, 64, 0, 512), repsG[:]),
                waits=[tev, sch.tok("dX") if qg == 0 else None])
            gstate["pv"][1 - pvb] = tev
            tbv = so(lambda e, d=dst, c=dc, hh=H: e.activation(
                sb_ap(d, 0, 64, c, 512), sb_ap(d, 0, 64, c, 512),
                AF.Identity, bias=qkvb[0:64, 32 + hh:33 + hh]), waits=[tmu])
            gstate["pv"][pvb] = tbv
            yield
        gstate["tmu_prev"] = tmu
        hargs[("stgfree", pairt % 2)] = sch.tok("sS")
        if odd:
            todd = dma("sync", "dX", sb_ap(attn_out, 64, 64, 2048 * (pairt % 2), 2048),
                       oddst[:], waits=[sch.tok("sS")])
            spill_tok[pairt % 2] = dma(
                "sync", "dX", spill_d[:, 2048 * pairt:2048 * pairt + 2048],
                attn_out[:, 2048 * (pairt % 2):2048 * (pairt % 2) + 2048],
                waits=[todd, sch.tok("sS")])
            gstate["vb"][pairt % 2] = sch.tok("sT")

    def run_interleaved(g1, g2, r1=1, r2=1):
        gens = [[g, r] for g, r in ((g1, r1), (g2, r2)) if g is not None]
        while gens:
            for item in list(gens):
                try:
                    for _ in range(item[1]):
                        next(item[0])
                except StopIteration:
                    gens.remove(item)

    def chainB(p):
        yield from gB(2 * p)
        yield from gB(2 * p + 1)

    bufs = {}
    prevB = None
    for p in range(8):
        run_interleaved(prevB, gA2(p), 3, 1)
        prevB = chainB(p)
    sfree("xnseq", "wqb", "wkb", "wvb", "sqscr", "nrmx", "cch")
    tok_g7 = [sch.tok("sT"), sch.tok("sV"), sch.tok("sS")]
    xnwin_pre = salloc("xnwin", [128, 8 * WIN], BF16)
    lc_state["xtm"] = salloc("xtm2a", [128, 1024], F32)
    lc_state["xntm"] = salloc("xntm2a", [128, 1024], F32)
    lc_state["xfree"] = list(tok_g7)
    lc_state["regfree"] = [None, None]
    lc_state["regpin"] = True

    def gen_lnwin():
        for i in range(28):
            ln_chunk("win", i, 32 + i, dst_bf=xnwin_pre, bfcol=128 * i, gate=tok_g7)
            yield
    run_interleaved(prevB, gen_lnwin(), 5, 1)
    lc_state["regpin"] = False
    bufs["xnwin"] = xnwin_pre
    tok_spill = spill_tok[1]
    sfree("attn_out", "oddst", "ptb", "repsG", "lnrecB",
          "qstg00", "qstg01", "qstg10", "qstg11", "kstg00", "kstg01", "kstg10", "kstg11",
          "vb0", "vb1", "qkvb", "Mb")


    def stream_rd_bf(t, w0, w, dstap_fn, waits):
        toks = []
        if w0 < SOWN:
            bb = min(w0 + w, SOWN)
            toks.append(vo(lambda e, a=w0, b=bb, t_=t, hh=bufs["halo"]: e.tensor_copy(
                dstap_fn(0, b - a),
                hh[:, NHALO * t_ + (a - X1AA):NHALO * t_ + (b - X1AA)]), waits=waits))
        if w0 + w > SOWN:
            aa = max(w0, SOWN)
            toks.append(so(lambda e, a=aa, n=w0 + w - aa, o=aa - w0, t_=t, ss=bufs["sown"]: e.activation(
                dstap_fn(o, n),
                ss[:, NSOWN * t_ + (a - SOWN):NSOWN * t_ + (a - SOWN) + n], AF.Copy),
                waits=waits))
        return toks

    def stream_slice(t, w0, w):
        if w0 < SOWN:
            assert w0 + w <= SOWN
            return bufs["halo"][:, NHALO * t + (w0 - X1AA):NHALO * t + (w0 - X1AA) + w]
        return bufs["sown"][:, NSOWN * t + (w0 - SOWN):NSOWN * t + (w0 - SOWN) + w]

    def ffn_layer(l, tok_prev):
        kv0, q0 = KV[l], Q0[l]
        nkv, nq, nqc, nkvc = NKV[l], NQ[l], NQC[l], NKVC[l]
        penlim = (PADTOP - kv0) // 128

        if l == 0:
            xnwin = bufs["xnwin"]
        else:
            xnwin = None
        wqkvb = salloc("wqkvb", [128, 8 * 1536], BF16)
        for kt in range(8):
            dma("gpsimd", "dW", wqkvb[:, 1536 * kt:1536 * kt + 1536],
                bass.AP(wqkv_d.tensor, l * D * 1536 + 128 * kt * 1536, [[1536, 128], [1, 1536]]),
                waits=tok_prev)
        twq = sch.tok("dW")
        qf = salloc("qf", [128, 4 * nq], BF16)
        kf = salloc("kf", [128, 4 * nkv], BF16)
        vbf = salloc("vbf", [128, nkvc * 520], BF16)
        if l == 1:
            xq2s = [salloc("xq2a", [128, 8 * 513], BF16),
                    salloc("xq2b", [128, 8 * 513], BF16)]

        pfree = [None, None, None, None]
        pbi = 0
        tmemset = vo(lambda e, n=nkvc: e.memset(
            bass.AP(vbf, 64, [[vbf.shape[1], 128], [520, n], [65, 8]]), 1.0),
            waits=tok_prev)
        xqtoks = []
        xq_mm_slot = [None, None]
        for cb, (c0, w) in enumerate(_blocks(nkv)):
            xq_mm_prev = xq_mm_slot[cb % 2]
            if l == 0:
                def rhs(kt, cc=c0, ww=w):
                    off = kv0 + cc - (1 if kt >= 4 else 0)
                    return xnwin[:, WIN * kt + off:WIN * kt + off + ww]
                rtoks = list(tok_prev)
            else:
                xq2 = xq2s[cb % 2]
                xqtoks = []
                for t in range(8):
                    xqtoks += stream_rd_bf(
                        t, kv0 + c0 - 1, w + 1,
                        (lambda o, n, t_=t, x2=xq2: x2[:, 513 * t_ + o:513 * t_ + o + n]),
                        waits=tok_prev + [xq_mm_prev])

                def rhs(kt, ww=w, x2=xq2):
                    off = 1 if kt < 4 else 0
                    return x2[:, 513 * kt + off:513 * kt + off + ww]
                rtoks = list(xqtoks)
            for t in range(4):
                pb = pbi % 4
                pbi += 1
                tmm = None
                for kt in range(8):
                    tmm = mm(bank(pb, 0, 128, 0, w),
                             wqkvb[:, 1536 * kt + 512 + 128 * t:1536 * kt + 512 + 128 * t + 128],
                             rhs(kt), start=(kt == 0), stop=(kt == 7),
                             waits=([twq, pfree[pb]] + rtoks) if kt == 0 else [])
                ev = so(lambda e, b=pb, t_=t, c=c0, ww=w: e.activation(
                    kf[:, nkv * t_ + c:nkv * t_ + c + ww], bank(b, 0, 128, 0, ww), AF.Copy),
                    waits=[tmm])
                pfree[pb] = ev
            if c0 >= 512:
                for t in range(4):
                    pb = pbi % 4
                    pbi += 1
                    tmm = None
                    for kt in range(8):
                        tmm = mm(bank(pb, 0, 128, 0, w),
                                 wqkvb[:, 1536 * kt + 128 * t:1536 * kt + 128 * t + 128],
                                 rhs(kt), start=(kt == 0), stop=(kt == 7),
                                 waits=([pfree[pb]] + rtoks) if kt == 0 else [])
                    ev = so(lambda e, b=pb, t_=t, c=c0 - 512, ww=w: e.activation(
                        qf[:, nq * t_ + c:nq * t_ + c + ww], bank(b, 0, 128, 0, ww), AF.Copy),
                        waits=[tmm])
                    pfree[pb] = ev
            for tci in range(w // 128):
                tc = c0 // 128 + tci
                pb = pbi % 4
                pbi += 1
                tmm = None
                for kt in range(8):
                    tmm = mm(bank(pb, 0, 128, 0, 512),
                             rhs(kt)[:, 128 * tci:128 * tci + 128],
                             wqkvb[:, 1536 * kt + 1024:1536 * kt + 1536],
                             start=(kt == 0), stop=(kt == 7),
                             waits=([pfree[pb], tmemset] + rtoks) if kt == 0 else [])
                ev = so(lambda e, b=pb, t=tc: e.activation(
                    bass.AP(vbf, 520 * t, [[vbf.shape[1], 128], [65, 8], [1, 64]]),
                    bass.AP(ps_t, 512 * b, [[4096, 128], [64, 8], [1, 64]]), AF.Copy),
                    waits=[tmm])
                pfree[pb] = ev
            xq_mm_slot[cb % 2] = sch.tok("sT")
        if l == 0:
            halo = bufs["halo"] = salloc("halo", [128, 8 * NHALO], BF16, high=True)
            for t in range(8):
                vo(lambda e, t_=t: e.tensor_copy(
                    halo[:, NHALO * t_:NHALO * t_ + NHALO],
                    xnwin[:, WIN * t_ + X1AA:WIN * t_ + SOWN]), waits=[sch.tok("sS")])
            sfree("xnwin")
        tok_qkv = [sch.tok("sS"), sch.tok("sV")]
        if l == 1:
            sfree("xq2a", "xq2b")
        sfree("wqkvb")

        rsnq = salloc("rsnq", [8, nq], BF16)
        rsnk = salloc("rsnk", [8, nkv], BF16)
        sq2 = salloc("sq2", [128, 2 * 512], BF16)
        m2s = salloc("m2s", [8, 512], F32)
        vmax_prev = [None, None]
        m2s_prev = None
        gcb = 0
        for (src, dst, ntot) in ((qf, rsnq, nq), (kf, rsnk, nkv)):
            for cb, (c0, w) in enumerate(_blocks(ntot)):
                lb = 2 + gcb % 2
                gcb += 1
                tmm = None
                sq_toks = [None, None]
                for t in range(4):
                    sb = t % 2
                    tq = so(lambda e, s=sb, t_=t, c=c0, ww=w, sr=src, nt=ntot: e.activation(
                        sq2[:, 512 * s:512 * s + ww], sr[:, nt * t_ + c:nt * t_ + c + ww],
                        AF.Square), waits=tok_qkv + [sq_toks[sb]])
                    tmm = mm(bank(lb, 0, 8, 0, w), c_l2o[:, 8 * t:8 * t + 8],
                             sq2[:, 512 * sb:512 * sb + w], start=(t == 0), stop=(t == 3),
                             waits=[tq] + ([vmax_prev[lb - 2]] if t == 0 else []))
                    sq_toks[sb] = tmm
                tv = vo(lambda e, b=lb, ww=w: e.tensor_scalar_max(
                    m2s[:, 0:ww], bank(b, 0, 8, 0, ww), 1e-24),
                    waits=[tmm, m2s_prev])
                vmax_prev[lb - 2] = tv
                ts1 = so(lambda e, ww=w: e.activation(m2s[:, 0:ww], m2s[:, 0:ww], AF.Ln), waits=[tv])
                m2s_prev = so(lambda e, d=dst, c=c0, ww=w: e.activation(
                    d[:, c:c + ww], m2s[:, 0:ww], AF.Exp, scale=-0.5), waits=[ts1])
        tok_rsn = [sch.tok("sS"), sch.tok("sV")]
        sfree("sq2", "m2s")

        swapb = salloc("swapb", [128, 4 * nkv], BF16)
        tb = salloc("tb", [128, 4 * 512], F32)
        sc1 = salloc("sc1", [128, 512], BF16)
        sc2 = salloc("sc2", [128, 512], BF16)
        rep_prev = [None, None]
        grt = 0
        for (buf, rsn, ntot, ti, qcol, woff) in (
                (qf, rsnq, nq, 0, 4 * l, q0), (kf, rsnk, nkv, 2, 4 * l + 2, kv0)):
            tok_sw = []
            swgate = [sch.tok("sV")]
            for t in range(4):
                for a in range(2):
                    for bb2 in range(2):
                        tok_sw.append(dma(
                            "sync", "dU",
                            sb_ap(swapb, 64 * a + 32 * (1 - bb2), 32, ntot * t, ntot),
                            sb_ap(buf, 64 * a + 32 * bb2, 32, ntot * t, ntot),
                            waits=tok_qkv + tok_rsn + swgate))
            tok_sw = [tok_sw[-1]]
            BART = [sch.tok("sT")] + tok_rsn
            tbfree = [None, None]
            for cb, (c0, w) in enumerate(_blocks(ntot)):
                rot = cb % 2
                dma("gpsimd", f"dT{rot}", tb[:, 1024 * rot:1024 * rot + w],
                    tab_d[ti][:, woff + c0:woff + c0 + w], waits=[tbfree[rot]] + BART)
                tt2 = dma("gpsimd", f"dT{rot}", tb[:, 1024 * rot + 512:1024 * rot + 512 + w],
                          tab_d[ti + 1][:, woff + c0:woff + c0 + w], waits=[])
                tt1 = tt2
                for t in range(4):
                    ta = vo(lambda e, t_=t, c=c0, ww=w, r=rot, b=buf, nt=ntot, qc=qcol:
                            e.scalar_tensor_tensor(
                                sc1[:, 0:ww], b[:, nt * t_ + c:nt * t_ + c + ww],
                                qspp[:, qc:qc + 1], tb[:, 1024 * r:1024 * r + ww],
                                ALU.mult, ALU.mult),
                            waits=[tt1] + tok_sw)
                    tbm = vo(lambda e, t_=t, c=c0, ww=w, r=rot, nt=ntot, qc=qcol:
                             e.scalar_tensor_tensor(
                                 sc2[:, 0:ww], swapb[:, nt * t_ + c:nt * t_ + c + ww],
                                 qspp[:, qc + 1:qc + 2], tb[:, 1024 * r + 512:1024 * r + 512 + ww],
                                 ALU.mult, ALU.mult),
                             waits=[tt2] + tok_sw)
                    tadd = vo(lambda e, t_=t, c=c0, ww=w, b=buf, nt=ntot: e.tensor_add(
                        b[:, nt * t_ + c:nt * t_ + c + ww], sc1[:, 0:ww], sc2[:, 0:ww]),
                        waits=[ta, tbm])
                    rb3 = 2 + grt % 2
                    grt += 1
                    trep = mm(bank(rb3, 0, 128, 0, w), c_rep8[0:8, 128 * t:128 * t + 128],
                              rsn[0:8, c0:c0 + w], start=True, stop=True,
                              waits=[rep_prev[rb3 - 2], tadd] + tok_rsn)
                    tmul = vo(lambda e, t_=t, c=c0, ww=w, b=buf, r=rb3, nt=ntot: e.tensor_mul(
                        b[:, nt * t_ + c:nt * t_ + c + ww],
                        b[:, nt * t_ + c:nt * t_ + c + ww], bank(r, 0, 128, 0, ww)),
                        waits=[trep])
                    rep_prev[rb3 - 2] = tmul
                tbfree[rot] = sch.tok("sV")
        tok_rot = sch.tok("sV")
        sfree("swapb", "tb", "sc1", "sc2", "rsnq", "rsnk")

        p1f_init = [None, None]

        o_fm = salloc("o_fm", [128, 4 * nq], BF16)
        odd2 = salloc("odd2", [64, nq], BF16)
        unno = salloc("unno", [65, 512], F32)
        recf2 = salloc("recf2", [1, 512], F32)
        qsg2 = [salloc(f"qsg{i}", [65, nq], BF16) for i in range(2)]
        ksg2 = [salloc(f"ksg{i}", [65, nkv], BF16) for i in range(2)]
        negM2s = salloc("negM2s", [1, nq], BF16)
        nMc2f = [salloc(f"nMcGf{i}", [128, 1], F32) for i in range(2)]
        negMc2f = [salloc(f"nMcLf{i}", [128, 1], F32) for i in range(2)]
        PW = (penlim - 1) * 128 + 640
        c_pen = salloc("c_pen", [1, PW], BF16)
        tok_pen = dma("gpsimd", "dW", c_pen[:], padpen_d[0:1, kv0:kv0 + PW],
                      waits=[sch.tok("sT")])
        pt2 = salloc("pt2", [128, 5 * 640], BF16)

        fst = {"p1": [p1f_init[0]], "st": [None, None], "pt": [None, None, None, None, None],
               "pv": [None, None], "rep": None, "mu": None}
        fargs = {}

        def fgA(h):
            par = h % 2
            pairt, odd = h // 2, h % 2
            qsg, ksg, negM2 = qsg2[par], ksg2[par], negM2s
            negMc_, negMcL = nMc2f[par], negMc2f[par]
            dma("sync", "dU", qsg[0:64, :], sb_ap(qf, 64 * odd, 64, nq * pairt, nq),
                waits=[tok_rot, fargs.get(("fstg", par))])
            tu2 = dma("sync", "dU", ksg[0:64, :], sb_ap(kf, 64 * odd, 64, nkv * pairt, nkv),
                      waits=[])
            tko = vo(lambda e, ks=ksg: e.memset(ks[64:65, :], 1.0), waits=[tu2])
            pend = None
            tneg = None
            for qc in range(nqc):
                tmm = mm(bank(0, 0, 128, 0, 512),
                         qsg[0:64, 128 * qc:128 * qc + 128],
                         ksg[0:64, 128 * qc:128 * qc + 512],
                         start=True, stop=(qc >= penlim),
                         waits=[tu2, tko, fst["p1"][0]])
                tmm = mm(bank(1, 0, 128, 0, 128),
                         qsg[0:64, 128 * qc:128 * qc + 128],
                         ksg[0:64, 128 * qc + 512:128 * qc + 640],
                         start=True, stop=(qc >= penlim), waits=[])
                if qc < penlim:
                    mm(bank(0, 0, 128, 0, 512), c_onesr[0:1, 0:128],
                       c_pen[0:1, 128 * qc:128 * qc + 512],
                       start=False, stop=True, waits=[tok_pen])
                    tmm = mm(bank(1, 0, 128, 0, 128), c_onesr[0:1, 0:128],
                             c_pen[0:1, 128 * qc + 512:128 * qc + 640],
                             start=False, stop=True, waits=[])
                tred = vo(lambda e, n=negMc_: e.tensor_reduce(
                    n[:], pscols(0, 640), AX.X, ALU.max, negate=True),
                    waits=[tmm, nmst.get("ftrp"), pend[1] if pend else None])
                fst["p1"][0] = tred
                if pend is not None:
                    q0, tn0 = pend
                    ttr = mm(bank(0, 0, 1, 0, 128), negMcL[:], c_iden[:],
                             start=True, stop=True, waits=[tn0, fst["p1"][0]])
                    nmst["ftrp"] = ttr
                    trev = so(lambda e, q=q0, nf=negM2: e.activation(
                        nf[0:1, 128 * q:128 * q + 128], bank(0, 0, 1, 0, 128), AF.Copy),
                        waits=[ttr, nmst.get("ftnmp")])
                    fst["p1"][0] = trev
                tcp = vo(lambda e, a=negMcL, b=negMc_: e.tensor_copy(a[:], b[:]),
                         waits=[tred, nmst.get("ftrp")])
                pend = (qc, tcp)
                yield
            q0, tn0 = pend
            ttr = mm(bank(0, 0, 1, 0, 128), negMc2f[par][:], c_iden[:],
                     start=True, stop=True, waits=[tn0, fst["p1"][0]])
            nmst["ftrp"] = ttr
            trev = so(lambda e, q=q0, nf=negM2: e.activation(
                nf[0:1, 128 * q:128 * q + 128], bank(0, 0, 1, 0, 128), AF.Copy),
                waits=[ttr, nmst.get("ftnmp")])
            fst["p1"][0] = trev
            tnm = dma("sync", "dSa" if par == 0 else "dSb", qsg[64:65, :], negM2[:], waits=[trev])
            nmst["ftnmp"] = tnm
            fargs[("tnm", h)] = tnm

        def fgB(h):
            par = h % 2
            pairt, odd = h // 2, h % 2
            qsg, ksg = qsg2[par], ksg2[par]
            tnm = fargs.pop(("tnm", h))
            pend = []
            for qc in range(nqc + 4):
                if qc < nqc:
                    stb = qc % 2
                    rb = 2 + 2 * stb
                    tmm = None
                    for jj in range(5):
                        edge = jj in (0, 4)
                        outap = (bank(rb, 0, 128, 128 * jj, 128) if jj < 4
                                 else bank(rb + 1, 0, 128, 0, 128))
                        tmm = mm(outap, ksg[0:65, 128 * (qc + jj):128 * (qc + jj) + 128],
                                 qsg[0:65, 128 * qc:128 * qc + 128],
                                 start=True, stop=(qc + jj >= penlim) and not edge,
                                 waits=([tnm, fst["st"][stb]] if jj == 0 else []))
                        if qc + jj < penlim:
                            tmm = mm(outap,
                                     c_pen[0:1, 128 * (qc + jj):128 * (qc + jj) + 128],
                                     c_onesr[0:1, 0:128], start=False, stop=not edge, waits=[])
                        if edge:
                            tmm = mm(outap, c_idenb[:],
                                     (c_maskf if jj == 0 else c_maskl)[:],
                                     start=False, stop=True, waits=[])
                    te2 = so(lambda e, a=rb, p=qc % 5: e.activation(
                        pt2[:, 640 * p:640 * p + 640], pscols(512 * a, 640), AF.Exp),
                        waits=[tmm, fst["pt"][qc % 5]])
                    fst["st"][stb] = te2
                    pend.append((qc, te2))
                if len(pend) > 4 or (qc >= nqc and pend):
                    q0, te0 = pend.pop(0)
                    pvb = q0 % 2
                    tpv = None
                    for jj in range(5):
                        tc = q0 + jj
                        tpv = mm(bank(6 + pvb, 0, 65, 0, 128),
                                 vbf[:, 520 * tc + 65 * h:520 * tc + 65 * h + 65],
                                 pt2[:, 640 * (q0 % 5) + 128 * jj:640 * (q0 % 5) + 128 * jj + 128],
                                 start=(jj == 0), stop=(jj == 4),
                                 waits=([te0] + ([fst["pv"][pvb]] if jj == 0 else [])))
                    fst["pt"][q0 % 5] = tpv
                    tevu = vo(lambda e, q=q0, b=pvb: e.tensor_copy(
                        unno[:, 128 * (q % 4):128 * (q % 4) + 128], bank(6 + b, 0, 65, 0, 128)),
                        waits=[tpv, fst["mu"]])
                    fst["pv"][pvb] = tevu
                    if q0 % 4 == 3 or q0 == nqc - 1:
                        w0 = 128 * (q0 % 4) + 128
                        c0 = 128 * (q0 - q0 % 4)
                        tall = sch.tok("sV")
                        tl1 = so(lambda e, ww=w0: e.activation(
                            recf2[0:1, 0:ww], sb_ap(unno, 64, 1, 0, ww), AF.Ln),
                            waits=[tall, fst["rep"]])
                        tl2 = so(lambda e, ww=w0: e.activation(
                            recf2[0:1, 0:ww], recf2[0:1, 0:ww],
                            AF.Exp, scale=-1.0), waits=[tl1])
                        rb2 = 6 + (q0 % 2)
                        trep = mm(bank(rb2, 0, 64, 0, w0), c_1f[0:1, 0:64],
                                  recf2[0:1, 0:w0],
                                  start=True, stop=True, waits=[tl2, fst["rep"]])
                        fst["rep"] = trep
                        dst = o_fm if odd == 0 else odd2
                        dc = (nq * pairt + c0) if odd == 0 else c0
                        tmu = vo(lambda e, d=dst, c=dc, r=rb2, ww=w0: e.tensor_mul(
                            sb_ap(d, 0, 64, c, ww), unno[0:64, 0:ww], bank(r, 0, 64, 0, ww)),
                            waits=[trep, fargs.get(("hd", h - 2)) if (odd and c0 == 0) else None])
                        fst["mu"] = tmu
                        fst["pv"][q0 % 2] = tmu
                yield
            fst["st"][0] = sch.tok("sV")
            if odd:
                fargs[("hd", h)] = dma("sync", "dX", sb_ap(o_fm, 64, 64, nq * pairt, nq),
                                       odd2[:], waits=[sch.tok("sV")])
            fargs[("fstg", par)] = sch.tok("sV")

        prevB = None
        for h in range(8):
            run_interleaved(prevB, fgA(h))
            prevB = fgB(h)
        run_interleaved(prevB, None)
        tok_attn = [sch.tok("sV"), sch.tok("sS"), sch.tok("sT"), sch.tok("dX")]
        sfree("qsg0", "qsg1", "ksg0", "ksg1", "negM2s", "pt2", "unno", "recf2",
              "odd2", "qf", "kf", "vbf", "c_pen",
              "nMcGf0", "nMcGf1", "nMcLf0", "nMcLf1")
        if l == 0:
            sown = bufs["sown"] = salloc("sown", [128, 8 * NSOWN], F32, high=True)
            gate = [sch.tok("sT"), sch.tok("sV"), sch.tok("sS"), sch.tok("dX")]
            lc_state["regfree"] = [None, None]
            lc_state["xfree"] = list(gate)
            for i in range(11, 28):
                ln_chunk("win", i, 32 + i, dst_f32=sown, f32col=128 * (i - 11), gate=gate)
            tok_stream = sch.all_toks()
            sfree("xtm2a", "xntm2a", "st_rs", "st_nb")
        else:
            tok_stream = tok_prev

        wof = salloc("wof", [128, 4 * 1024], BF16)
        for kt in range(4):
            dma("gpsimd", "dW", wof[:, 1024 * kt:1024 * kt + 1024],
                bass.AP(wof_d.tensor, l * 512 * D + 128 * kt * D, [[D, 128], [1, D]]),
                waits=tok_attn)
        twof = sch.tok("dW")
        wfree = [None, None, None, None]
        for (b0, w) in WO_BLKS[l]:
            for dmc in range(8):
                pb = dmc % 4
                tmm = None
                for kt in range(4):
                    tmm = mm(bank(pb, 0, 128, 0, w),
                             wof[:, 1024 * kt + 128 * dmc:1024 * kt + 128 * dmc + 128],
                             o_fm[:, nq * kt + (b0 - q0):nq * kt + (b0 - q0) + w],
                             start=(kt == 0), stop=(kt == 3),
                             waits=([twof, wfree[pb]] + tok_attn + list(tok_stream))
                             if kt == 0 else [])
                sl = stream_slice(dmc, b0, w)
                tadd = vo(lambda e, b=pb, s=sl, ww=w: e.tensor_add(
                    s, bank(b, 0, 128, 0, ww), s), waits=[tmm])
                wfree[pb] = tadd
        if l == 1:
            EDGE2 = 1536 - 1 - SOWN
            for t in range(8):
                vo(lambda e, t_=t: e.tensor_scalar_mul(
                    bufs["sown"][:, NSOWN * t_ + EDGE2:NSOWN * t_ + EDGE2 + 1],
                    bufs["sown"][:, NSOWN * t_ + EDGE2:NSOWN * t_ + EDGE2 + 1],
                    padedge[:, 0:1]), waits=[sch.tok("sV"), wfree[0], wfree[1]])
        tok_wo = [sch.tok("sV")]
        sfree("wof", "o_fm")

        w1b = salloc("w1b", [128, 8 * 5460], BF16)
        for kt in range(8):
            dma("gpsimd", "dW", w1b[:, 5460 * kt:5460 * kt + 5460],
                bass.AP(w1_d.tensor, l * D * 2 * HID + 128 * kt * 2 * HID, [[2 * HID, 128], [1, 2 * HID]]),
                waits=tok_attn + tok_wo)
        tw1 = sch.tok("dW")
        if l == 0:
            for lc in range(2):
                for dmc in range(8):
                    base = (lc * 8 + dmc) * 128 * 2816
                    dma("gpsimd", "dWc",
                        bass.AP(w2bf_d.tensor, base, [[2816, 128], [128, 21], [1, 128]]),
                        bass.AP(w2_d.tensor, lc * HID * D + 128 * dmc,
                                [[D, 128], [128 * D, 21], [1, 128]]), waits=[])
                    dma("gpsimd", "dWc",
                        bass.AP(w2bf_d.tensor, base + 2688, [[2816, 42], [1, 128]]),
                        bass.AP(w2_d.tensor, lc * HID * D + 2688 * D + 128 * dmc,
                                [[D, 42], [1, 128]]), waits=[])
            bufs["w2conv"] = sch.tok("dWc")
        w2s = salloc("w2s", [128, 3 * 2816], BF16)
        hbD = [salloc("hb0", [128, 8 * MLPW], BF16),
               salloc("hb1", [128, 8 * MLPW], BF16)]
        lnoD = hbD
        hidA = salloc("hidbA", [128, 11 * MLPW], BF16)
        hidB = salloc("hidbB", [128, 11 * MLPW], BF16)
        def hid_sl(hc, p1, c0, w):
            t = hidA if hc < 11 else hidB
            c = MLPW * (hc if hc < 11 else hc - 11)
            return t[0:p1, c + c0:c + c0 + w]
        sqh = salloc("sqh", [128, MLPW], BF16)
        statw = salloc("statw", [33, 2 * MLPW], F32)
        mu_r = statw[0:1, 0:MLPW]
        srow = statw[0:1, MLPW:2 * MLPW]
        ms_r = statw[32:33, 0:MLPW]
        vscr = statw[32:33, MLPW:2 * MLPW]
        sv = salloc("sv", [128, 8], F32)

        w2free = [tok_wo[0], tok_wo[0], tok_wo[0]]
        w2bank = [None, None]
        st = {"agfree": [None, None], "hbfree": [[], []], "svfree": [],
              "r1free": None, "r2free": None, "sqfree": None, "gglfree": None,
              "lnofree": [[], []], "prev_w": None, "lno_tok": {}, "sv_tok": {}}

        def gen_prep(bi, b0, w):
            par = bi % 2
            hb = hbD[par]
            lno = hb
            th = []
            for t in range(8):
                hbw = st["hbfree"][par] + st["lnofree"][par]
                if t < 4:
                    th += stream_rd_bf(t, b0, w,
                                       (lambda o, n, t_=t: hb[:, MLPW * t_ + o:MLPW * t_ + o + n]),
                                       waits=tok_wo + hbw)
                else:
                    if bi == 0:
                        th += stream_rd_bf(t, b0 - 1, w,
                                           (lambda o, n, t_=t: hb[:, MLPW * t_ + o:MLPW * t_ + o + n]),
                                           waits=tok_wo + hbw)
                    else:
                        th.append(vo(lambda e, t_=t: e.tensor_copy(
                            hb[:, MLPW * t_:MLPW * t_ + 1], sv[:, t_ - 4:t_ - 3]),
                            waits=st["svfree"] + hbw))
                        th += stream_rd_bf(t, b0, w - 1,
                                           (lambda o, n, t_=t: hb[:, MLPW * t_ + 1 + o:MLPW * t_ + 1 + o + n]),
                                           waits=tok_wo + hbw)
                if t % 2 == 1:
                    yield
            svfree = []
            svg = [sch.tok("sV")]
            for t in range(4, 8):
                svfree.append(vo(lambda e, t_=t, c=b0 + w - 1: e.tensor_copy(
                    sv[:, t_ - 4:t_ - 3], stream_slice(t_, c, 1)),
                    waits=tok_wo + [st["prev_w"]] + svg))
            st["svfree"] = svfree
            st["sv_tok"][bi] = svfree
            yield
            tmm_mu = tmm_ms = None
            for t in range(8):
                tsq = so(lambda e, t_=t, ww=w: e.activation(
                    sqh[:, 0:ww], hb[:, MLPW * t_:MLPW * t_ + ww], AF.Square),
                    waits=th + [st["sqfree"]])
                tmm_mu = mm(bank(4, 0, 1, 0, w), c_onesc[:], hb[:, MLPW * t:MLPW * t + w],
                            start=(t == 0), stop=(t == 7),
                            waits=(th + [st["r1free"]]) if t == 0 else [])
                tmm_ms = mm(bank(5, 0, 1, 0, w), c_onesc[:], sqh[:, 0:w],
                            start=(t == 0), stop=(t == 7),
                            waits=[tsq] + ([st["r2free"]] if t == 0 else []))
                st["sqfree"] = tmm_ms
                if t % 2 == 1:
                    yield
            tev = so(lambda e, ww=w: e.activation(mu_r[0:1, 0:ww], bank(4, 0, 1, 0, ww),
                                                  AF.Copy, scale=1.0 / 1024), waits=[tmm_mu])
            tev2 = so(lambda e, ww=w: e.activation(ms_r[0:1, 0:ww], bank(5, 0, 1, 0, ww),
                                                   AF.Copy, scale=1.0 / 1024), waits=[tmm_ms])
            tmu2 = so(lambda e, ww=w: e.activation(vscr[0:1, 0:ww], mu_r[0:1, 0:ww], AF.Square),
                      waits=[tev])
            tvar = vo(lambda e, ww=w: e.tensor_sub(vscr[0:1, 0:ww], ms_r[0:1, 0:ww],
                                                   vscr[0:1, 0:ww]), waits=[tmu2, tev2])
            tln = so(lambda e, ww=w: e.activation(vscr[0:1, 0:ww], vscr[0:1, 0:ww], AF.Ln,
                                                  bias=1.0e-5), waits=[tvar])
            texp = so(lambda e, ww=w: e.activation(srow[0:1, 0:ww], vscr[0:1, 0:ww], AF.Exp,
                                                   scale=-0.5), waits=[tln])
            yield
            tr1 = mm(bank(4, 0, 128, 0, w), c_1f[0:1, :], srow[0:1, 0:w],
                     start=True, stop=True, waits=[texp, st["r1free"]])
            tr2 = mm(bank(5, 0, 128, 0, w), c_1f[0:1, :], mu_r[0:1, 0:w],
                     start=True, stop=True, waits=[tev, st["r2free"]])
            for t in range(8):
                tv1 = vo(lambda e, t_=t, ww=w: e.scalar_tensor_tensor(
                    lno[:, MLPW * t_:MLPW * t_ + ww], hb[:, MLPW * t_:MLPW * t_ + ww], 1.0,
                    bank(5, 0, 128, 0, ww), ALU.mult, ALU.subtract),
                    waits=[tr2])
                st["r2free"] = tv1
                lastadd = vo(lambda e, t_=t, ww=w: e.scalar_tensor_tensor(
                    lno[:, MLPW * t_:MLPW * t_ + ww], lno[:, MLPW * t_:MLPW * t_ + ww], 1.0,
                    bank(4, 0, 128, 0, ww), ALU.mult, ALU.mult), waits=[tr1, tv1])
                st["r1free"] = lastadd
                so(lambda e, t_=t, ww=w: e.activation(
                    lno[:, MLPW * t_:MLPW * t_ + ww], lno[:, MLPW * t_:MLPW * t_ + ww],
                    AF.Identity, scale=lgpp[:, 8 * l + t_:8 * l + t_ + 1],
                    bias=lbpp[:, 8 * l + t_:8 * l + t_ + 1]), waits=[lastadd])
                if t % 2 == 1:
                    yield
            tok_lno = [sch.tok("sS"), sch.tok("sV")]
            st["lno_tok"][bi] = tok_lno
            st["hbfree"][par] = list(tok_lno)

        def gen_mlp(bi, b0, w):
            par = bi % 2
            lno = lnoD[par]
            tok_lno = st["lno_tok"].pop(bi)
            svfree_bi = st["sv_tok"].pop(bi)
            pendh = None
            for hc in range(NHC):
                whc = _hc_w(hc)
                ab, gb = (6, 7) if hc % 2 == 0 else (2, 3)
                ta = tg = None
                for kt in range(8):
                    ta = mm(bank(ab, 0, whc, 0, w),
                            w1b[:, 5460 * kt + 128 * hc:5460 * kt + 128 * hc + whc],
                            lno[:, MLPW * kt:MLPW * kt + w], start=(kt == 0), stop=(kt == 7),
                            waits=([tw1, st["agfree"][hc % 2]] + tok_lno if kt == 0 else []))
                    tg = mm(bank(gb, 0, whc, 0, w),
                            w1b[:, 5460 * kt + HID + 128 * hc:5460 * kt + HID + 128 * hc + whc],
                            lno[:, MLPW * kt:MLPW * kt + w], start=(kt == 0), stop=(kt == 7),
                            waits=[])
                if pendh is not None:
                    c0h, wh0, ab0, gb0, tg0, ta0 = pendh
                    tgg = so(lambda e, ww=w, wh=wh0, c=c0h, g=gb0: e.activation(
                        hid_sl(c, wh, 0, ww), bank(g, 0, wh, 0, ww), AF.Gelu,
                        bias=b1g[0:wh, 22 * l + c:22 * l + c + 1]),
                        waits=[tg0, st["gglfree"]])
                    gf = vo(lambda e, ww=w, wh=wh0, c=c0h, a=ab0: e.scalar_tensor_tensor(
                        hid_sl(c, wh, 0, ww), bank(a, 0, wh, 0, ww),
                        b1a[0:wh, 22 * l + c:22 * l + c + 1], hid_sl(c, wh, 0, ww),
                        ALU.add, ALU.mult), waits=[ta0, tgg])
                    st["gglfree"] = gf
                    st["agfree"][c0h % 2] = gf
                pendh = (hc, whc, ab, gb, tg, ta)
                yield
            c0h, wh0, ab0, gb0, tg0, ta0 = pendh
            tgg = so(lambda e, ww=w, wh=wh0, c=c0h, g=gb0: e.activation(
                hid_sl(c, wh, 0, ww), bank(g, 0, wh, 0, ww), AF.Gelu,
                bias=b1g[0:wh, 22 * l + c:22 * l + c + 1]), waits=[tg0, st["gglfree"]])
            gf = vo(lambda e, ww=w, wh=wh0, c=c0h, a=ab0: e.scalar_tensor_tensor(
                hid_sl(c, wh, 0, ww), bank(a, 0, wh, 0, ww),
                b1a[0:wh, 22 * l + c:22 * l + c + 1], hid_sl(c, wh, 0, ww),
                ALU.add, ALU.mult), waits=[ta0, tgg])
            st["gglfree"] = gf
            st["agfree"][c0h % 2] = gf
            st["lnofree"][par] = [sch.tok("sT")]
            tok_hid = sch.tok("sV")
            for dmc in range(8):
                slo = (8 * bi + dmc) % 3
                pbk = dmc % 2
                sname = ("dWa", "dWb", "dWd")[slo]
                dma("sync", sname,
                    bass.AP(w2s, 2816 * slo, [[3 * 2816, 128], [1, 2688]]),
                    bass.AP(w2bf_d.tensor, (l * 8 + dmc) * 128 * 2816,
                            [[2816, 128], [1, 2688]]),
                    waits=[w2free[slo], bufs["w2conv"]] +
                          (tok_wo if bi == 0 and dmc < 3 else []))
                tld2 = dma("sync", sname,
                           bass.AP(w2s, 2816 * slo + 2688, [[3 * 2816, 42], [1, 128]]),
                           bass.AP(w2bf_d.tensor, (l * 8 + dmc) * 128 * 2816 + 2688,
                                   [[2816, 42], [1, 128]]), waits=[])
                tld1 = tld2
                tmm = None
                for hc in range(NHC):
                    whc = _hc_w(hc)
                    tmm = mm(bank(pbk, 0, 128, 0, w),
                             w2s[0:whc, 2816 * slo + 128 * hc:2816 * slo + 128 * hc + 128],
                             hid_sl(hc, whc, 0, w),
                             start=(hc == 0), stop=(hc == NHC - 1),
                             waits=([tld1, tld2, tok_hid, w2bank[pbk]] if hc == 0 else []))
                sl2 = stream_slice(dmc, b0, w)
                tadd = vo(lambda e, b=pbk, s=sl2, ww=w, d=dmc: e.scalar_tensor_tensor(
                    s, bank(b, 0, 128, 0, ww), b2c[:, 8 * l + d:8 * l + d + 1], s,
                    ALU.add, ALU.add), waits=[tmm] + svfree_bi)
                w2free[slo] = tadd
                w2bank[pbk] = tadd
                yield
            st["prev_w"] = w2free[1] if w2free[1] else w2free[0]

        blks = MLP_BLKS[l]
        run_interleaved(gen_prep(0, *blks[0]), None)
        for bi, (b0, w) in enumerate(blks):
            nxt = gen_prep(bi + 1, *blks[bi + 1]) if bi + 1 < len(blks) else None
            run_interleaved(gen_mlp(bi, b0, w), nxt, 2, 1)
        tok_mlp = sch.all_toks()
        sfree("w1b", "w2s", "hb0", "hb1", "hidbA", "hidbB",
              "sqh", "statw", "sv")
        return tok_mlp

    toks = ffn_layer(0, [tok_spill] + sch.all_toks())
    EDGE = 1536 - 1 - SOWN
    for t in range(8):
        vo(lambda e, t_=t: e.tensor_scalar_mul(
            bufs["sown"][:, NSOWN * t_ + EDGE:NSOWN * t_ + EDGE + 1],
            bufs["sown"][:, NSOWN * t_ + EDGE:NSOWN * t_ + EDGE + 1],
            padedge[:, 0:1]), waits=toks[:])
    toks = sch.all_toks()
    toks = ffn_layer(1, toks)

    woab = salloc("woab", [128, 8 * 1024], BF16)
    bo_row = salloc("bo_row", [1, 1024], BF16)
    dma("gpsimd", "dW", bo_row[:], bo_d[None, :], waits=toks)
    for kt in range(8):
        dma("gpsimd", "dW", woab[:, 1024 * kt:1024 * kt + 1024],
            wo_a_d[128 * kt:128 * kt + 128, :], waits=toks)
    twoa = sch.tok("dW")
    attn2 = salloc("attn2", [128, 8 * NOWN], BF16)
    trl = None
    for t in range(8):
        trl = dma("sync", "dX", attn2[:, 2048 * t:2048 * t + 2048],
                  spill_d[:, 2048 * t:2048 * t + 2048], waits=[tok_spill] + toks)
    xtm2 = salloc("xtm2", [128, 2 * 1024], F32)
    stg = salloc("stg", [128, 2 * 1024], F32)
    stg2 = salloc("stg2", [128, 2 * 1024], F32)
    tregf = [None, None]
    wregf = [None, None]
    stgf = [None, None]
    stg2f = [None, None]
    for c in range(16):
        rr = c % 2
        tl = dma("sync", f"dL{rr}", xtm2[:, 1024 * rr:1024 * rr + 1024],
                 xseq[128 * c:128 * c + 128, :],
                 waits=[stgf[rr]] + (toks if c < 2 else []))
        tt = None
        for t in range(8):
            tt = tsp(pscols(1024 * rr + 128 * t, 128),
                     bufs["sown"][:, NSOWN * t + 128 + 128 * c:NSOWN * t + 128 + 128 * c + 128],
                     waits=toks + [tregf[rr]])
        tw = None
        for kt in range(8):
            for hh in range(2):
                tw = mm(pscols(2048 + 1024 * rr + 512 * hh, 512),
                        attn2[:, 2048 * kt + 128 * c:2048 * kt + 128 * c + 128],
                        woab[:, 1024 * kt + 512 * hh:1024 * kt + 512 * hh + 512],
                        start=(kt == 0), stop=False,
                        waits=([twoa, trl, wregf[rr]] if (kt == 0 and hh == 0) else []))
        for hh in range(2):
            tw = mm(pscols(2048 + 1024 * rr + 512 * hh, 512), c_onesr[0:1, 0:128],
                    bo_row[0:1, 512 * hh:512 * hh + 512], start=False, stop=True, waits=[])
        tv1 = vo(lambda e, r=rr: e.tensor_add(
            stg[:, 1024 * r:1024 * r + 1024], pscols(1024 * r, 1024),
            xtm2[:, 1024 * r:1024 * r + 1024]), waits=[tt, tl, stg2f[rr]])
        tregf[rr] = tv1
        tv2 = vo(lambda e, r=rr: e.tensor_add(
            stg2[:, 1024 * r:1024 * r + 1024], pscols(2048 + 1024 * r, 1024),
            stg[:, 1024 * r:1024 * r + 1024]), waits=[tw, tv1])
        wregf[rr] = tv2
        stg2f[rr] = tv2
        stgf[rr] = dma("sync", f"dO{rr}", out_d[128 * c:128 * c + 128, :],
                       stg2[:, 1024 * rr:1024 * rr + 1024], waits=[tv2])
    tok_out = (sch.tok("dO0"), sch.tok("dO1"))

    with nc.Block() as blk:
        def mkbody(name):
            def body(e):
                lastw = {}
                for waits, inc, incn, fn in sch.ops[name]:
                    for sem, val in waits:
                        if val > 0 and lastw.get(sem, -1) < val:
                            e.wait_ge(semh[sem], val)
                            lastw[sem] = val
                    ins = fn(e)
                    if inc is not None:
                        ins.then_inc(semh[inc], incn)
                if name == "sync":
                    e.wait_ge(semh["dO0"], tok_out[0][1])
                    e.wait_ge(semh["dO1"], tok_out[1][1])
            return body
        blk.sync(mkbody("sync"))
        blk.gpsimd(mkbody("gpsimd"))
        blk.tensor(mkbody("tensor"))
        blk.vector(mkbody("vector"))
        blk.scalar(mkbody("scalar"))
    if __name__ == "__main__":
        print({e: len(sch.ops[e]) for e in ENGS}, "sem:", sch.cnt,
              "sbuf free:", sorted(free), file=sys.stderr)
    return nc


def kernel(**inputs):
    x = np.ascontiguousarray(np.asarray(inputs["L0_attn_x"], dtype=np.float32))
    if _PROG[0] is None:
        _PROG[0] = build_program()
    nc = _PROG[0]
    shared = dict(
        wq=np.ascontiguousarray(inputs["L0_attn_wq"], dtype=np.float32),
        wk=np.ascontiguousarray(inputs["L0_attn_wk"], dtype=np.float32),
        wv=np.ascontiguousarray(inputs["L0_attn_wv"], dtype=np.float32),
        wo_a=np.ascontiguousarray(inputs["L0_attn_wo"], dtype=np.float32),
        bq=np.ascontiguousarray(inputs["L0_attn_bq"], dtype=np.float32),
        bk=np.ascontiguousarray(inputs["L0_attn_bk"], dtype=np.float32),
        bv=np.ascontiguousarray(inputs["L0_attn_bv"], dtype=np.float32),
        bo=np.ascontiguousarray(inputs["L0_attn_bo"], dtype=np.float32),
        wqkv=np.ascontiguousarray(inputs["L0_ffn_wqkv"], dtype=np.float32),
        qscale=np.ascontiguousarray(inputs["L0_ffn_q_scale"], dtype=np.float32),
        kscale=np.ascontiguousarray(inputs["L0_ffn_k_scale"], dtype=np.float32),
        wo_f=np.ascontiguousarray(inputs["L0_ffn_wo"], dtype=np.float32),
        ln_g=np.ascontiguousarray(inputs["L0_ffn_ln_g"], dtype=np.float32),
        ln_b=np.ascontiguousarray(inputs["L0_ffn_ln_b"], dtype=np.float32),
        w1=np.ascontiguousarray(inputs["L0_ffn_w1"], dtype=np.float32),
        b1=np.ascontiguousarray(inputs["L0_ffn_b1"], dtype=np.float32),
        w2=np.ascontiguousarray(inputs["L0_ffn_w2"], dtype=np.float32),
        b2=np.ascontiguousarray(inputs["L0_ffn_b2"], dtype=np.float32),
        iden=_CONST["iden"], ones_row=_CONST["ones_row"], ones_col=_CONST["ones_col"],
        mask_first=_CONST["mask_first"], mask_last=_CONST["mask_last"],
        l2ones=_CONST["l2ones"], rep8=_CONST["rep8"], ones1f=_CONST["ones1f"],
    )
    in_maps = []
    for c in range(8):
        b, h = c // 2, c % 2
        T0 = NOWN * h
        xs = np.ascontiguousarray(np.concatenate([x[b][T0:], x[b][:T0]], axis=0))
        if h == 1:
            xw = np.ascontiguousarray(x[b][T0 - 1536:T0 + NOWN])
            pen = np.zeros((1, WIN), dtype=ml_dtypes.bfloat16)
        else:
            xw = np.ascontiguousarray(
                np.concatenate([np.zeros((1536, D), np.float32), x[b][0:NOWN]], axis=0))
            penf = np.zeros((1, WIN), dtype=np.float32)
            penf[0, :PADTOP] = NEG
            pen = penf.astype(ml_dtypes.bfloat16)
        m = dict(shared)
        m["xseq"] = xs
        m["xwin"] = xw
        m["padpen"] = pen
        m["padedge"] = np.full((128, 1), float(h), dtype=np.float32)
        for i in range(4):
            m[f"tab{i}"] = _TABS[h][i]
        in_maps.append(m)
    res = run_bass_kernel_spmd(nc, in_maps, core_ids=list(range(8)))
    out = np.empty((B, S, D), dtype=np.float32)
    for c in range(8):
        b, h = c // 2, c % 2
        out[b, NOWN * h:NOWN * h + NOWN] = res.results[c]["out"]
    return out


if __name__ == "__main__":
    build_program()
    print("build ok")


# version 39
# speedup vs baseline: 1.0142x; 1.0142x over previous
import numpy as np
import ml_dtypes
import concourse.bass as bass
import concourse.mybir as mybir
from concourse.bass_utils import run_bass_kernel_spmd
import sys

DEBUG_TAGS = False
_TAGS = []

F32 = mybir.dt.float32
BF16 = mybir.dt.bfloat16
AF = mybir.ActivationFunctionType
ALU = mybir.AluOpType
AX = mybir.AxisListType

B, S, D = 4, 4096, 1024
NOWN = 2048
NEG = -float(2.0 ** 99)
WIN = 3584
KV = {0: 128, 1: 768}
Q0 = {0: 640, 1: 1280}
X1AA = 766
SOWN = 1408
NHALO = SOWN - X1AA
NSOWN = WIN - SOWN
HID = 2730
NHC = 22
PADTOP = 1536
DEBUG_SPILL = False
NKV = {l: WIN - KV[l] for l in (0, 1)}
NQ = {l: WIN - Q0[l] for l in (0, 1)}
NQC = {l: NQ[l] // 128 for l in (0, 1)}
NKVC = {l: NKV[l] // 128 for l in (0, 1)}

WO_BLKS = {0: [(766, 322), (1088, 320), (1408, 512), (1920, 512), (2432, 512), (2944, 512), (3456, 128)],
           1: [(1280, 128), (1408, 512), (1920, 512), (2432, 512), (2944, 512), (3456, 128)]}
MLP_BLKS = {0: [(767, 257), (1024, 256), (1280, 128)]
               + [(1408 + 256 * i, 256) for i in range(8)] + [(3456, 128)],
            1: [(1408 + 256 * i, 256) for i in range(8)] + [(3456, 128)]}
MLPW = 257


def _hc_w(hc):
    return 128 if hc < NHC - 1 else HID - 128 * (NHC - 1)


def _blocks(total, step=512):
    res, o = [], 0
    while o < total:
        res.append((o, min(step, total - o)))
        o += step
    return res


def _rotary_tables(abs_t0):
    DH, WINDOW, QK = 64, 512, 8.0
    half = np.arange(0, DH, 2, dtype=np.float64)
    inv = 1.0 / 10000.0 ** (half / DH)
    t = abs_t0 + np.arange(WIN, dtype=np.float64)
    fr = t[:, None] * inv[None, :]
    fr = np.concatenate([fr, fr], -1)
    base = (half + 0.4 * DH) / (1.4 * DH)
    power = np.clip((t - S // 2) / float(WINDOW // 2), -12.0, 12.0)
    sc = base[None, :] ** power[:, None]
    sc = np.concatenate([sc, sc], -1)
    cos, sin = np.cos(fr), np.sin(fr)
    sgn = np.concatenate([-np.ones(32), np.ones(32)])[None, :]
    outs = [(cos * sc) * QK, (sin * sc) * sgn * QK, cos / sc, (sin / sc) * sgn]

    def dup(a):
        a = a.T.astype(np.float32)
        return np.ascontiguousarray(np.concatenate([a, a], 0))
    return [dup(o) for o in outs]


def _host_consts():
    c = {}
    c["iden"] = np.eye(128, dtype=np.float32)
    c["ones_row"] = np.ones((1, 640), dtype=ml_dtypes.bfloat16)
    c["ones_col"] = np.ones((128, 1), dtype=ml_dtypes.bfloat16)
    s = np.arange(128)[:, None]
    r = np.arange(128)[None, :]
    c["mask_first"] = np.where(s >= r, 0.0, NEG).astype(np.float32)
    c["mask_last"] = np.where(s <= r, 0.0, NEG).astype(np.float32)
    l2 = np.zeros((128, 32), dtype=np.float32)
    for i in range(4):
        l2[0:64, 8 * i + 2 * i] = 1
        l2[64:128, 8 * i + 2 * i + 1] = 1
    c["l2ones"] = l2.astype(ml_dtypes.bfloat16)
    rp = np.zeros((8, 512), dtype=np.float32)
    for i in range(4):
        rp[2 * i, 128 * i:128 * i + 64] = 1
        rp[2 * i + 1, 128 * i + 64:128 * i + 128] = 1
    c["rep8"] = rp.astype(ml_dtypes.bfloat16)
    c["ones1f"] = np.ones((1, 128), dtype=np.float32)
    return c


_CONST = _host_consts()
_TABS = {0: _rotary_tables(-1536), 1: _rotary_tables(512)}
_PROG = [None]

ENGS = ("sync", "gpsimd", "tensor", "vector", "scalar")
SEMS = ("dX", "dW", "dS", "sT", "sV", "sS", "dL0", "dL1", "dL2", "dL3", "dLC", "dU", "dT0", "dT1", "dWa", "dWb", "dO0", "dO1", "dSa", "dSb", "dWc", "dWd")


class Sched:
    def __init__(self):
        self.ops = {e: [] for e in ENGS}
        self.cnt = {s: 0 for s in SEMS}

    def op(self, eng, fn, waits=(), inc=None, incn=1):
        w = [t for t in waits if t is not None and t[1] > 0]
        self.ops[eng].append((w, inc, incn, fn))
        if inc is not None:
            self.cnt[inc] += incn

    def tok(self, sem):
        return (sem, self.cnt[sem])

    def all_toks(self):
        return [self.tok(s) for s in SEMS]


def build_program():
    nc = bass.Bass("TRN2", target_bir_lowering=False)
    sch = Sched()
    semh = {s: nc.alloc_semaphore(s) for s in SEMS}

    def din(name, shape, dt=F32):
        return nc.dram_tensor(name, list(shape), dt, kind="ExternalInput").ap()

    xseq = din("xseq", (S, D))
    xwin = din("xwin", (WIN, D))
    padpen_d = din("padpen", (1, WIN), BF16)
    padedge_d = din("padedge", (128, 1))
    wq_d = din("wq", (D, D)); wk_d = din("wk", (D, D)); wv_d = din("wv", (D, D))
    wo_a_d = din("wo_a", (D, D))
    bq_d = din("bq", (D,)); bk_d = din("bk", (D,)); bv_d = din("bv", (D,)); bo_d = din("bo", (D,))
    wqkv_d = din("wqkv", (2, D, 1536))
    qs_d = din("qscale", (2, 64)); ks_d = din("kscale", (2, 64))
    wof_d = din("wo_f", (2, 512, D))
    lng_d = din("ln_g", (2, D)); lnb_d = din("ln_b", (2, D))
    w1_d = din("w1", (2, D, 2 * HID)); b1_d = din("b1", (2, 2 * HID))
    w2_d = din("w2", (2, HID, D)); b2_d = din("b2", (2, D))
    iden_d = din("iden", (128, 128))
    onesr_d = din("ones_row", (1, 640), BF16)
    onesc_d = din("ones_col", (128, 1), BF16)
    maskf_d = din("mask_first", (128, 128))
    maskl_d = din("mask_last", (128, 128))
    l2o_d = din("l2ones", (128, 32), BF16)
    rep8_d = din("rep8", (8, 512), BF16)
    ones1f_d = din("ones1f", (1, 128))
    tab_d = [din(f"tab{i}", (128, WIN)) for i in range(4)]
    out_d = nc.dram_tensor("out", [NOWN, D], F32, kind="ExternalOutput").ap()
    spill_d = nc.dram_tensor("attn_spill", [128, 8 * NOWN], BF16, kind="ExternalOutput" if DEBUG_SPILL else "Internal").ap()
    w2bf_d = nc.dram_tensor("w2bf", [2 * 8 * 128 * 2816], BF16, kind="Internal").ap()

    free = [[17024, 229376]]
    live = {}

    def salloc(name, shape, dt, high=False):
        al = 64
        bpp = (int(np.prod(shape[1:])) * mybir.dt.size(dt) + al - 1) // al * al
        segs = sorted(free, reverse=high)
        for seg in segs:
            if seg[1] - seg[0] < bpp:
                continue
            off = ((seg[1] - bpp) // al * al) if high else seg[0]
            if off < seg[0]:
                continue
            t = nc.alloc_sbuf_tensor_at(name, list(shape), dt, offset=off)
            live[name] = [off, off + bpp]
            free.remove(seg)
            if seg[0] < off:
                free.append([seg[0], off])
            if off + bpp < seg[1]:
                free.append([off + bpp, seg[1]])
            return t
        raise RuntimeError(f"SBUF OOM {name}: need {bpp}, free={sorted(free)}, live={ {k: (v[1]-v[0]) for k, v in sorted(live.items(), key=lambda x: -x[1][1]+x[1][0])} }")

    def sfree(*names):
        for n in names:
            free.append(live.pop(n))
        free.sort()
        m = [free[0]]
        for seg in free[1:]:
            if seg[0] <= m[-1][1]:
                m[-1][1] = max(m[-1][1], seg[1])
            else:
                m.append(seg)
        free[:] = m

    ps_t = nc.alloc_psum_tensor("ps", [128, 4096], F32)

    def bank(i, p0=0, p1=128, c0=0, cw=512):
        return ps_t[p0:p1, 512 * i + c0:512 * i + c0 + cw]

    def pscols(c0, cw, p0=0, p1=128):
        return ps_t[p0:p1, c0:c0 + cw]

    def dma(queue, sem, outap, inap, waits=(), slow=False):
        sch.op(queue, lambda e, o=outap, i=inap, s=slow: e.dma_start(
            o, i, allow_slow_non_contiguous=s), waits, sem, 16)
        return sch.tok(sem)

    def mm(outap, lhsT, rhs, start, stop, waits=(), inc="sT", tp=None):
        if DEBUG_TAGS:
            _TAGS.append(sys._getframe(1).f_lineno)
        sch.op("tensor",
               lambda e, o=outap, l=lhsT, r=rhs, st=start, sp=stop, t=tp:
               e.matmul(o, l, r, start=st, stop=sp, tile_position=t), waits, inc)
        return sch.tok("sT")

    def tsp(outap, in_ap, waits=()):
        if DEBUG_TAGS:
            _TAGS.append(-sys._getframe(1).f_lineno)
        sch.op("tensor", lambda e, o=outap, i=in_ap: e.transpose(o, i, c_iden[:]), waits, "sT")
        return sch.tok("sT")

    def vo(fn, waits=(), inc="sV"):
        sch.op("vector", fn, waits, inc)
        return sch.tok("sV")

    def so(fn, waits=(), inc="sS"):
        sch.op("scalar", fn, waits, inc)
        return sch.tok("sS")

    def sb_ap(t, part0, nparts, col0, ncols):
        return bass.AP(t, part0 * t.shape[1] + col0, [[t.shape[1], nparts], [1, ncols]])

    c_iden = salloc("c_iden", [128, 128], F32)
    c_onesr = salloc("c_onesr", [1, 128], BF16)
    c_onesc = salloc("c_onesc", [128, 1], BF16)
    c_maskf = salloc("c_maskf", [128, 128], BF16)
    c_maskl = salloc("c_maskl", [128, 128], BF16)
    c_idenb = salloc("c_idenb", [128, 128], BF16)
    c_l2o = salloc("c_l2o", [128, 32], BF16)
    c_rep8 = salloc("c_rep8", [8, 512], BF16)
    c_1f = salloc("c_1f", [1, 128], F32)
    qkvb = salloc("qkvb", [64, 48], F32)
    b2c = salloc("b2c", [128, 16], F32)
    lgpp = salloc("lgpp", [128, 16], F32)
    lbpp = salloc("lbpp", [128, 16], F32)
    b1a = salloc("b1a", [128, 44], F32)
    b1g = salloc("b1g", [128, 44], F32)
    qspp = salloc("qspp", [128, 8], F32)
    st_rs = salloc("st_rs", [128, 60], F32)
    st_nb = salloc("st_nb", [128, 60], F32)
    padedge = salloc("padedge", [128, 1], F32)
    Mb = salloc("Mb", [128, 4], F32)
    cflt = salloc("cflt", [128, 1], F32)
    tok_cf = vo(lambda e: e.memset(cflt[:], 1.0e-5))
    nc.const_aps.register(F32, 1.0e-5, cflt[:, 0:1])

    dma("gpsimd", "dW", c_iden[:], iden_d)
    dma("gpsimd", "dW", c_onesr[:], onesr_d[0:1, 0:128])
    dma("gpsimd", "dW", c_onesc[:], onesc_d)
    dma("gpsimd", "dW", c_maskf[:], maskf_d)
    dma("gpsimd", "dW", c_maskl[:], maskl_d)
    dma("gpsimd", "dW", c_idenb[:], iden_d)
    dma("gpsimd", "dW", c_l2o[:], l2o_d)
    dma("gpsimd", "dW", c_rep8[:], rep8_d)
    dma("gpsimd", "dW", c_1f[:], ones1f_d)
    dma("gpsimd", "dW", padedge[:], padedge_d)
    for j, bsrc in enumerate((bq_d, bk_d, bv_d)):
        dma("gpsimd", "dW", qkvb[:, 16 * j:16 * j + 16],
            bass.AP(bsrc.tensor, 0, [[1, 64], [64, 16]]), slow=True)
    for l in range(2):
        dma("gpsimd", "dW", b2c[:, 8 * l:8 * l + 8],
            bass.AP(b2_d.tensor, l * D, [[1, 128], [128, 8]]), slow=True)
        dma("gpsimd", "dW", lgpp[:, 8 * l:8 * l + 8],
            bass.AP(lng_d.tensor, 1024 * l, [[1, 128], [128, 8]]), slow=True)
        dma("gpsimd", "dW", lbpp[:, 8 * l:8 * l + 8],
            bass.AP(lnb_d.tensor, 1024 * l, [[1, 128], [128, 8]]), slow=True)
        for (dst, off) in ((b1a, 0), (b1g, HID)):
            dma("gpsimd", "dW", dst[:, 22 * l:22 * l + 21],
                bass.AP(b1_d.tensor, l * 2 * HID + off, [[1, 128], [128, 21]]), slow=True)
            dma("gpsimd", "dW", dst[0:42, 22 * l + 21:22 * l + 22],
                bass.AP(b1_d.tensor, l * 2 * HID + off + 2688, [[1, 42], [42, 1]]), slow=True)
        for half in range(2):
            p0 = 64 * half
            for (cc, src, o1, o2) in ((0, qs_d, 0, 32), (2, ks_d, 0, 32)):
                dma("gpsimd", "dW", qspp[p0:p0 + 64, 4 * l + cc:4 * l + cc + 1],
                    bass.AP(src.tensor, 64 * l, [[1, 64], [64, 1]]), slow=True)
                dma("gpsimd", "dW", qspp[p0:p0 + 32, 4 * l + cc + 1:4 * l + cc + 2],
                    bass.AP(src.tensor, 64 * l + 32, [[1, 32], [32, 1]]), slow=True)
                dma("gpsimd", "dW", qspp[p0 + 32:p0 + 64, 4 * l + cc + 1:4 * l + cc + 2],
                    bass.AP(src.tensor, 64 * l, [[1, 32], [32, 1]]), slow=True)
    tok_consts = sch.tok("dW")
    tok_bq = vo(lambda e: e.tensor_scalar_mul(qkvb[:, 0:16], qkvb[:, 0:16], 0.125),
                waits=[tok_consts])

    xnseq = salloc("xnseq", [128, 8 * S], BF16)
    xtm = salloc("xtm", [128, 4 * 1024], F32)
    st_sum = salloc("st_sum", [128, 60], F32)
    st_sq = salloc("st_sq", [128, 60], F32)
    xn_tm = salloc("xn_tm", [128, 1024], F32)

    CH = [("seq", i) for i in range(32)] + [("win", i) for i in range(28)]

    def xsrc(which, i):
        src = xseq if which == "seq" else xwin
        return src[128 * i:128 * i + 128, :]

    lc_state = dict(xfree=[], reg=0, regfree=[None, None], xtm=xtm, xntm=None)

    xtm_free = [[], [], [], []]
    for ci, (which, i) in enumerate(CH):
        sl = ci % 4
        tl = dma("sync", f"dL{sl}", xtm[:, 1024 * sl:1024 * sl + 1024], xsrc(which, i),
                 waits=xtm_free[sl])
        tv = vo(lambda e, s=sl, c=ci: e.reduce_sum(
            st_sum[:, c:c + 1], xtm[:, 1024 * s:1024 * s + 1024], AX.X), waits=[tl])
        tq = so(lambda e, s=sl, c=ci: e.activation(
            xtm[:, 1024 * s:1024 * s + 1024], xtm[:, 1024 * s:1024 * s + 1024], AF.Square,
            accum_out=st_sq[:, c:c + 1]), waits=[tl, tok_cf, tv])
        xtm_free[sl] = [tv, tq]
    t1 = so(lambda e: e.activation(st_sum[:], st_sum[:], AF.Copy, scale=1.0 / 1024),
            waits=[sch.tok("sV")])
    t2 = so(lambda e: e.activation(st_nb[:], st_sum[:], AF.Square), waits=[t1])
    t3 = vo(lambda e: e.scalar_tensor_tensor(st_sq[:], st_sq[:], 1.0 / 1024, st_nb[:],
                                             ALU.mult, ALU.subtract), waits=[t2])
    t4 = so(lambda e: e.activation(st_sq[:], st_sq[:], AF.Ln, bias=1.0e-5), waits=[t3])
    t5 = so(lambda e: e.activation(st_rs[:], st_sq[:], AF.Exp, scale=-0.5), waits=[t4])
    tok_stats = vo(lambda e: e.scalar_tensor_tensor(st_nb[:], st_sum[:], -1.0, st_rs[:],
                                                    ALU.mult, ALU.mult), waits=[t5])

    def ln_chunk(which, i, statcol, dst_bf=None, bfcol=0, dst_f32=None, f32col=0, gate=()):
        xt, xnt = lc_state["xtm"], lc_state["xntm"]
        tl = dma("sync", "dLC", xt[:, 0:1024], xsrc(which, i), waits=list(lc_state["xfree"]) + list(gate))
        ta = so(lambda e, c=statcol, x1=xt, x2=xnt: e.activation(
            x2[:], x1[:, 0:1024], AF.Identity,
            scale=st_rs[:, c:c + 1], bias=st_nb[:, c:c + 1]), waits=[tl, tok_stats])
        lc_state["xfree"] = [ta]
        if lc_state.get("regpin"):
            reg = 0
        else:
            reg = lc_state["reg"]
            lc_state["reg"] ^= 1
        tt = None
        for t in range(8):
            tt = tsp(pscols(1024 * reg + 128 * t, 128),
                     lc_state["xntm"][:, 128 * t:128 * t + 128],
                     waits=[ta, lc_state["regfree"][reg]] + list(gate))
        ev = None
        for (dd, dc) in ((dst_bf, bfcol), (dst_f32, f32col)):
            if dd is None:
                continue
            sd = dd.shape[1] // 8
            ev = so(lambda e, r=reg, d=dd, c=dc, s=sd: e.activation(
                bass.AP(d, c, [[d.shape[1], 128], [s, 8], [1, 128]]),
                bass.AP(ps_t, 1024 * r, [[4096, 128], [128, 8], [1, 128]]),
                AF.Copy), waits=[tt])
        lc_state["regfree"][reg] = ev

    lc_state["xntm"] = xn_tm
    lc_state["xfree"] = [sch.tok("sS"), sch.tok("sV")]
    for i in range(32):
        ln_chunk("seq", i, i, dst_bf=xnseq, bfcol=128 * i)
    tok_xn = sch.all_toks()
    sfree("st_sum", "st_sq", "xtm", "xn_tm")

    wqb = salloc("wqb", [128, 8 * 1024], BF16)
    wkb = salloc("wkb", [128, 8 * 1024], BF16)
    wvb = salloc("wvb", [128, 8 * 1024], BF16)
    BARB = [sch.tok("sT"), sch.tok("sV"), sch.tok("sS")]
    for kt in range(8):
        dma("gpsimd", "dW", wqb[:, 1024 * kt:1024 * kt + 1024], wq_d[128 * kt:128 * kt + 128, :], waits=BARB)
        dma("gpsimd", "dW", wkb[:, 1024 * kt:1024 * kt + 1024], wk_d[128 * kt:128 * kt + 128, :], waits=BARB)
        dma("gpsimd", "dW", wvb[:, 1024 * kt:1024 * kt + 1024], wv_d[128 * kt:128 * kt + 128, :], waits=BARB)
    tok_wqkv = sch.tok("dW")

    attn_out = salloc("attn_out", [128, 2 * NOWN], BF16)
    repsG = salloc("repsG", [64, 512], F32)
    oddst = salloc("oddst", [64, NOWN], BF16)
    ptb = salloc("ptb", [128, 5 * 512], BF16)
    lnrecB = salloc("lnrecB", [33, 512], F32)
    recG = lnrecB[0:1, :]
    lnG = lnrecB[32:33, :]
    qstg4 = [[salloc(f"qstg{i}{j}", [65, NOWN], BF16) for j in range(2)] for i in range(2)]
    kstg4 = [[salloc(f"kstg{i}{j}", [65, S], BF16) for j in range(2)] for i in range(2)]
    vb2 = [salloc(f"vb{i}", [128, 32 * 132], BF16) for i in range(2)]
    sqscr = salloc("sqscr", [64, 2048], BF16)
    nrmx = salloc("nrmx", [1, 32], F32)
    cch = salloc("cch", [1, 16], F32)

    tko4 = [None, None]
    for i in range(2):
        t0k = vo(lambda e, a=i: e.memset(kstg4[a][0][64:65, :], 1.0),
                 waits=BARB if i == 0 else [])
        tko4[i] = vo(lambda e, a=i: e.memset(kstg4[a][1][64:65, :], 1.0), waits=[t0k])

    gstate = {"p1": [None, None, None, None],
              "st": [None, None, None, None],
              "pt": [None, None, None, None, None],
              "pv": [None, None],
              "rep": None, "vb": [None, None],
              "tmu_prev": None}
    nmst = {"sq": [None, None, None, None], "nrm": [None, None], "cch": [None, None]}
    spill_tok = [None, None]
    hargs = {}

    def gA2(pairt):
        H0, H1 = 2 * pairt, 2 * pairt + 1
        qstg2 = qstg4[pairt % 2]
        kstg2 = kstg4[pairt % 2]
        vb = vb2[pairt % 2]
        gT = sch.tok("sT")
        tms = vo(lambda e, v=vb: e.memset(
            bass.AP(v, 64, [[v.shape[1], 128], [132, 32], [66, 2]]), 1.0),
            waits=[gT, tok_xn[4], tok_xn[5], gstate["vb"][pairt % 2]])
        for tc in range(32):
            pb = tc % 2
            tmm = None
            for kt in range(8):
                tmm = mm(bank(pb, 0, 128, 0, 128),
                         xnseq[:, 4096 * kt + 128 * tc:4096 * kt + 128 * tc + 128],
                         wvb[:, 1024 * kt + 128 * pairt:1024 * kt + 128 * pairt + 128],
                         start=(kt == 0), stop=(kt == 7),
                         waits=([tok_wqkv, tok_xn[4], tok_xn[5], gstate["p1"][pb], gT]
                                if kt == 0 else []))
            ev = vo(lambda e, b=pb, t=tc, v=vb: e.tensor_copy(
                bass.AP(v, 132 * t, [[v.shape[1], 128], [66, 2], [1, 64]]),
                bass.AP(ps_t, 512 * b, [[4096, 128], [64, 2], [1, 64]])),
                waits=[tmm, tms])
            gstate["p1"][pb] = ev
            if tc % 4 == 3:
                yield
        hargs[("v", pairt)] = sch.tok("sV")
        evq = [None, None]
        evk = [None, None]
        for src_w, dstl, bias0, scl, nn, tag in (
                (wqb, qstg2, 0, 0.125, NOWN, "q"), (wkb, kstg2, 16, 1.0, S, "k")):
            for cb, (c0, w) in enumerate(_blocks(nn)):
                pb = cb % 2
                tmm = None
                for kt in range(8):
                    for par in range(2):
                        tmm = mm(bank(pb, 64 * par, 64 * par + 64, 0, w),
                                 src_w[:, 1024 * kt + 64 * (H0 + par):1024 * kt + 64 * (H0 + par) + 64],
                                 xnseq[:, 4096 * kt + c0:4096 * kt + c0 + w],
                                 start=(kt == 0), stop=(kt == 7), tp=(0, 64 * par),
                                 waits=([gstate["p1"][pb], tok_bq,
                                         hargs.get(("stgfree", pairt % 2))] if kt == 0 else []))
                ev = None
                for par in range(2):
                    ev = vo(lambda e, b=pb, c=c0, ww=w, p=par, d=dstl, bc=bias0, s=scl:
                            e.tensor_scalar(d[p][0:64, c:c + ww],
                                            bank(b, 64 * p, 64 * p + 64, 0, ww),
                                            s, qkvb[0:64, bc + H0 + p:bc + H0 + p + 1],
                                            ALU.mult, ALU.add),
                            waits=[tmm])
                gstate["p1"][pb] = ev
                if cb % 2 == 1:
                    yield
        tko1 = tko4[pairt % 2]
        tok_stg = sch.tok("sV")
        tkon = [tko1, tko1]
        for par in range(2):
            blks = [(qstg2[par], c) for c in range(0, NOWN, 512)] + \
                   [(kstg2[par], c) for c in range(0, S, 512)]
            nb0 = 16 * par
            sqfree = nmst["sq"]
            tred = None
            for bi, (stg, c0) in enumerate(blks):
                sl = bi % 4
                nb = bi % 2
                tsq = vo(lambda e, t=stg, c=c0, s=sl: e.tensor_tensor(
                    sqscr[:, 512 * s:512 * s + 512], t[0:64, c:c + 512],
                    t[0:64, c:c + 512], ALU.mult),
                    waits=[tok_stg, sqfree[sl]])
                tnmm = mm(bank(nb, 0, 1, 0, 512), c_onesc[0:64, 0:1],
                          sqscr[0:64, 512 * sl:512 * sl + 512],
                          start=True, stop=True, waits=[tsq, gstate["p1"][nb]])
                sqfree[sl] = tnmm
                tred = vo(lambda e, b=bi, n=nb, p=par: e.tensor_reduce(
                    nrmx[0:1, nb0 + b:nb0 + b + 1], bank(n, 0, 1, 0, 512), AX.X, ALU.max),
                    waits=[tnmm, nmst["nrm"][par] if bi < 2 else None])
                gstate["p1"][nb] = tred
                yield
            cb = 8 * par
            tq = vo(lambda e: e.tensor_reduce(
                cch[0:1, cb + 0:cb + 1], nrmx[0:1, nb0:nb0 + 4], AX.X, ALU.max),
                waits=[tred, nmst["cch"][par]])
            tk = vo(lambda e: e.tensor_reduce(
                cch[0:1, cb + 1:cb + 2], nrmx[0:1, nb0 + 4:nb0 + 12], AX.X, ALU.max),
                waits=[tq])
            nmst["nrm"][par] = tk
            tl1 = so(lambda e: e.activation(cch[0:1, cb + 2:cb + 3],
                                            cch[0:1, cb + 0:cb + 1], AF.Ln), waits=[tq])
            tl2 = so(lambda e: e.activation(cch[0:1, cb + 3:cb + 4],
                                            cch[0:1, cb + 1:cb + 2], AF.Ln), waits=[tk])
            tadd = vo(lambda e: e.tensor_tensor(cch[0:1, cb + 4:cb + 5],
                                                cch[0:1, cb + 2:cb + 3],
                                                cch[0:1, cb + 3:cb + 4], ALU.add),
                      waits=[tl1, tl2])
            tC = so(lambda e: e.activation(cch[0:1, cb + 5:cb + 6],
                                           cch[0:1, cb + 4:cb + 5], AF.Exp, scale=0.5),
                    waits=[tadd])
            tneg = vo(lambda e: e.tensor_scalar(cch[0:1, cb + 6:cb + 7],
                                                cch[0:1, cb + 5:cb + 6], -1.0, None,
                                                ALU.mult),
                      waits=[tC])
            trow = vo(lambda e, p=par: e.tensor_scalar(
                qstg2[p][64:65, 0:NOWN], kstg2[p][64:65, 0:NOWN],
                cch[0:1, cb + 6:cb + 7], None, ALU.mult),
                waits=[tneg, tkon[par], hargs.get(("stgfree", pairt % 2))])
            nmst["cch"][par] = trow
            hargs[("tnm", H0 + par)] = trow
            yield

    def gB(H):
        par = H % 2
        pairt, odd = H // 2, H % 2
        qstg, kstg = qstg4[pairt % 2][par], kstg4[pairt % 2][par]
        vb = vb2[pairt % 2]
        tnm = hargs.pop(("tnm", H))
        tokv = hargs.get(("v", pairt))
        tmu = gstate["tmu_prev"]
        for qg in range(4):
            pvb = qg % 2
            pend = []
            tpv = None
            for kc in range(32):
                stb = kc % 4
                sbk = (4, 5, 2, 3)[stb]
                tmm = mm(bank(sbk), kstg[0:65, 128 * kc:128 * kc + 128],
                         qstg[0:65, 512 * qg:512 * qg + 512],
                         start=True, stop=True,
                         waits=[tnm, tokv, gstate["st"][stb]] +
                               ([gstate["pv"][pvb]] if kc == 0 else []))
                texp = so(lambda e, b=sbk, p=kc % 5: e.activation(
                    ptb[:, 512 * p:512 * p + 512], bank(b), AF.Exp),
                    waits=[tmm, gstate["pt"][kc % 5]])
                gstate["st"][stb] = texp
                pend.append((kc, texp))
                while len(pend) > 4:
                    k0, te0 = pend.pop(0)
                    tpv = mm(bank(6 + pvb, 0, 65, 0, 512),
                             vb[:, 132 * k0 + 66 * odd:132 * k0 + 66 * odd + 65],
                             ptb[:, 512 * (k0 % 5):512 * (k0 % 5) + 512],
                             start=(k0 == 0), stop=False, waits=[te0])
                    gstate["pt"][k0 % 5] = tpv
                if kc % 2 == 1:
                    yield
            for (k0, te0) in pend:
                tpv = mm(bank(6 + pvb, 0, 65, 0, 512),
                         vb[:, 132 * k0 + 66 * odd:132 * k0 + 66 * odd + 65],
                         ptb[:, 512 * (k0 % 5):512 * (k0 % 5) + 512],
                         start=(k0 == 0), stop=(k0 == 31), waits=[te0])
                gstate["pt"][k0 % 5] = tpv
            tl1 = so(lambda e, b=pvb: e.activation(lnG[:], bank(6 + b, 64, 65, 0, 512), AF.Ln),
                     waits=[tpv])
            tl2 = so(lambda e: e.activation(recG[:], lnG[:], AF.Exp, scale=-1.0),
                     waits=[tl1, gstate["rep"]])
            rb = 6 + (1 - pvb)
            trep = mm(bank(rb, 0, 64, 0, 512), c_1f[0:1, 0:64], recG[:],
                      start=True, stop=True, waits=[tl2, gstate["pv"][1 - pvb]])
            gstate["rep"] = trep
            tev = vo(lambda e, r=rb: e.tensor_copy(repsG[:], bank(r, 0, 64, 0, 512)),
                     waits=[trep, tmu])
            dst = attn_out if odd == 0 else oddst
            dc = (2048 * (pairt % 2) + 512 * qg) if odd == 0 else (512 * qg)
            tmu = vo(lambda e, b=pvb, d=dst, c=dc: e.tensor_mul(
                sb_ap(d, 0, 64, c, 512), bank(6 + b, 0, 64, 0, 512), repsG[:]),
                waits=[tev, sch.tok("dX") if qg == 0 else None])
            gstate["pv"][1 - pvb] = tev
            tbv = so(lambda e, d=dst, c=dc, hh=H: e.activation(
                sb_ap(d, 0, 64, c, 512), sb_ap(d, 0, 64, c, 512),
                AF.Identity, bias=qkvb[0:64, 32 + hh:33 + hh]), waits=[tmu])
            gstate["pv"][pvb] = tbv
            yield
        gstate["tmu_prev"] = tmu
        hargs[("stgfree", pairt % 2)] = sch.tok("sS")
        if odd:
            todd = dma("sync", "dX", sb_ap(attn_out, 64, 64, 2048 * (pairt % 2), 2048),
                       oddst[:], waits=[sch.tok("sS")])
            spill_tok[pairt % 2] = dma(
                "sync", "dX", spill_d[:, 2048 * pairt:2048 * pairt + 2048],
                attn_out[:, 2048 * (pairt % 2):2048 * (pairt % 2) + 2048],
                waits=[todd, sch.tok("sS")])
            gstate["vb"][pairt % 2] = sch.tok("sT")

    def run_interleaved(g1, g2, r1=1, r2=1):
        gens = [[g, r] for g, r in ((g1, r1), (g2, r2)) if g is not None]
        while gens:
            for item in list(gens):
                try:
                    for _ in range(item[1]):
                        next(item[0])
                except StopIteration:
                    gens.remove(item)

    def chainB(p):
        yield from gB(2 * p)
        yield from gB(2 * p + 1)

    bufs = {}
    prevB = None
    for p in range(8):
        run_interleaved(prevB, gA2(p), 3, 1)
        prevB = chainB(p)
    sfree("xnseq", "wqb", "wkb", "wvb", "sqscr", "nrmx", "cch")
    tok_g7 = [sch.tok("sT"), sch.tok("sV"), sch.tok("sS")]
    xnwin_pre = salloc("xnwin", [128, 8 * WIN], BF16)
    lc_state["xtm"] = salloc("xtm2a", [128, 1024], F32)
    lc_state["xntm"] = salloc("xntm2a", [128, 1024], F32)
    lc_state["xfree"] = list(tok_g7)
    lc_state["regfree"] = [None, None]
    lc_state["regpin"] = True

    def gen_lnwin():
        for i in range(28):
            ln_chunk("win", i, 32 + i, dst_bf=xnwin_pre, bfcol=128 * i, gate=tok_g7)
            yield
    run_interleaved(prevB, gen_lnwin(), 5, 1)
    lc_state["regpin"] = False
    bufs["xnwin"] = xnwin_pre
    tok_spill = spill_tok[1]
    sfree("attn_out", "oddst", "ptb", "repsG", "lnrecB",
          "qstg00", "qstg01", "qstg10", "qstg11", "kstg00", "kstg01", "kstg10", "kstg11",
          "vb0", "vb1", "qkvb", "Mb")


    def stream_rd_bf(t, w0, w, dstap_fn, waits):
        toks = []
        if w0 < SOWN:
            bb = min(w0 + w, SOWN)
            toks.append(vo(lambda e, a=w0, b=bb, t_=t, hh=bufs["halo"]: e.tensor_copy(
                dstap_fn(0, b - a),
                hh[:, NHALO * t_ + (a - X1AA):NHALO * t_ + (b - X1AA)]), waits=waits))
        if w0 + w > SOWN:
            aa = max(w0, SOWN)
            toks.append(so(lambda e, a=aa, n=w0 + w - aa, o=aa - w0, t_=t, ss=bufs["sown"]: e.activation(
                dstap_fn(o, n),
                ss[:, NSOWN * t_ + (a - SOWN):NSOWN * t_ + (a - SOWN) + n], AF.Copy),
                waits=waits))
        return toks

    def stream_slice(t, w0, w):
        if w0 < SOWN:
            assert w0 + w <= SOWN
            return bufs["halo"][:, NHALO * t + (w0 - X1AA):NHALO * t + (w0 - X1AA) + w]
        return bufs["sown"][:, NSOWN * t + (w0 - SOWN):NSOWN * t + (w0 - SOWN) + w]

    def ffn_layer(l, tok_prev):
        kv0, q0 = KV[l], Q0[l]
        nkv, nq, nqc, nkvc = NKV[l], NQ[l], NQC[l], NKVC[l]
        penlim = (PADTOP - kv0) // 128

        if l == 0:
            xnwin = bufs["xnwin"]
        else:
            xnwin = None
        wqkvb = salloc("wqkvb", [128, 8 * 1536], BF16)
        for kt in range(8):
            dma("gpsimd", "dW", wqkvb[:, 1536 * kt:1536 * kt + 1536],
                bass.AP(wqkv_d.tensor, l * D * 1536 + 128 * kt * 1536, [[1536, 128], [1, 1536]]),
                waits=tok_prev)
        twq = sch.tok("dW")
        qf = salloc("qf", [128, 4 * nq], BF16)
        kf = salloc("kf", [128, 4 * nkv], BF16)
        vbf = salloc("vbf", [128, nkvc * 520], BF16)
        if l == 1:
            xq2s = [salloc("xq2a", [128, 8 * 513], BF16),
                    salloc("xq2b", [128, 8 * 513], BF16)]

        pfree = [None, None, None, None]
        pbi = 0
        tmemset = vo(lambda e, n=nkvc: e.memset(
            bass.AP(vbf, 64, [[vbf.shape[1], 128], [520, n], [65, 8]]), 1.0),
            waits=tok_prev)
        xqtoks = []
        xq_mm_slot = [None, None]
        for cb, (c0, w) in enumerate(_blocks(nkv)):
            xq_mm_prev = xq_mm_slot[cb % 2]
            if l == 0:
                def rhs(kt, cc=c0, ww=w):
                    off = kv0 + cc - (1 if kt >= 4 else 0)
                    return xnwin[:, WIN * kt + off:WIN * kt + off + ww]
                rtoks = list(tok_prev)
            else:
                xq2 = xq2s[cb % 2]
                xqtoks = []
                for t in range(8):
                    xqtoks += stream_rd_bf(
                        t, kv0 + c0 - 1, w + 1,
                        (lambda o, n, t_=t, x2=xq2: x2[:, 513 * t_ + o:513 * t_ + o + n]),
                        waits=tok_prev + [xq_mm_prev])

                def rhs(kt, ww=w, x2=xq2):
                    off = 1 if kt < 4 else 0
                    return x2[:, 513 * kt + off:513 * kt + off + ww]
                rtoks = list(xqtoks)
            for t in range(4):
                pb = pbi % 4
                pbi += 1
                tmm = None
                for kt in range(8):
                    tmm = mm(bank(pb, 0, 128, 0, w),
                             wqkvb[:, 1536 * kt + 512 + 128 * t:1536 * kt + 512 + 128 * t + 128],
                             rhs(kt), start=(kt == 0), stop=(kt == 7),
                             waits=([twq, pfree[pb]] + rtoks) if kt == 0 else [])
                ev = so(lambda e, b=pb, t_=t, c=c0, ww=w: e.activation(
                    kf[:, nkv * t_ + c:nkv * t_ + c + ww], bank(b, 0, 128, 0, ww), AF.Copy),
                    waits=[tmm])
                pfree[pb] = ev
            if c0 >= 512:
                for t in range(4):
                    pb = pbi % 4
                    pbi += 1
                    tmm = None
                    for kt in range(8):
                        tmm = mm(bank(pb, 0, 128, 0, w),
                                 wqkvb[:, 1536 * kt + 128 * t:1536 * kt + 128 * t + 128],
                                 rhs(kt), start=(kt == 0), stop=(kt == 7),
                                 waits=([pfree[pb]] + rtoks) if kt == 0 else [])
                    ev = so(lambda e, b=pb, t_=t, c=c0 - 512, ww=w: e.activation(
                        qf[:, nq * t_ + c:nq * t_ + c + ww], bank(b, 0, 128, 0, ww), AF.Copy),
                        waits=[tmm])
                    pfree[pb] = ev
            for tci in range(w // 128):
                tc = c0 // 128 + tci
                pb = pbi % 4
                pbi += 1
                tmm = None
                for kt in range(8):
                    tmm = mm(bank(pb, 0, 128, 0, 512),
                             rhs(kt)[:, 128 * tci:128 * tci + 128],
                             wqkvb[:, 1536 * kt + 1024:1536 * kt + 1536],
                             start=(kt == 0), stop=(kt == 7),
                             waits=([pfree[pb], tmemset] + rtoks) if kt == 0 else [])
                ev = so(lambda e, b=pb, t=tc: e.activation(
                    bass.AP(vbf, 520 * t, [[vbf.shape[1], 128], [65, 8], [1, 64]]),
                    bass.AP(ps_t, 512 * b, [[4096, 128], [64, 8], [1, 64]]), AF.Copy),
                    waits=[tmm])
                pfree[pb] = ev
            xq_mm_slot[cb % 2] = sch.tok("sT")
        if l == 0:
            halo = bufs["halo"] = salloc("halo", [128, 8 * NHALO], BF16, high=True)
            for t in range(8):
                vo(lambda e, t_=t: e.tensor_copy(
                    halo[:, NHALO * t_:NHALO * t_ + NHALO],
                    xnwin[:, WIN * t_ + X1AA:WIN * t_ + SOWN]), waits=[sch.tok("sS")])
            sfree("xnwin")
        tok_qkv = [sch.tok("sS"), sch.tok("sV")]
        if l == 1:
            sfree("xq2a", "xq2b")
        sfree("wqkvb")

        rsnq = salloc("rsnq", [8, nq], BF16)
        rsnk = salloc("rsnk", [8, nkv], BF16)
        sq2 = salloc("sq2", [128, 2 * 512], BF16)
        m2s = salloc("m2s", [8, 512], F32)
        vmax_prev = [None, None]
        m2s_prev = None
        gcb = 0
        for (src, dst, ntot) in ((qf, rsnq, nq), (kf, rsnk, nkv)):
            for cb, (c0, w) in enumerate(_blocks(ntot)):
                lb = 2 + gcb % 2
                gcb += 1
                tmm = None
                sq_toks = [None, None]
                for t in range(4):
                    sb = t % 2
                    tq = so(lambda e, s=sb, t_=t, c=c0, ww=w, sr=src, nt=ntot: e.activation(
                        sq2[:, 512 * s:512 * s + ww], sr[:, nt * t_ + c:nt * t_ + c + ww],
                        AF.Square), waits=tok_qkv + [sq_toks[sb]])
                    tmm = mm(bank(lb, 0, 8, 0, w), c_l2o[:, 8 * t:8 * t + 8],
                             sq2[:, 512 * sb:512 * sb + w], start=(t == 0), stop=(t == 3),
                             waits=[tq] + ([vmax_prev[lb - 2]] if t == 0 else []))
                    sq_toks[sb] = tmm
                tv = vo(lambda e, b=lb, ww=w: e.tensor_scalar_max(
                    m2s[:, 0:ww], bank(b, 0, 8, 0, ww), 1e-24),
                    waits=[tmm, m2s_prev])
                vmax_prev[lb - 2] = tv
                ts1 = so(lambda e, ww=w: e.activation(m2s[:, 0:ww], m2s[:, 0:ww], AF.Ln), waits=[tv])
                m2s_prev = so(lambda e, d=dst, c=c0, ww=w: e.activation(
                    d[:, c:c + ww], m2s[:, 0:ww], AF.Exp, scale=-0.5), waits=[ts1])
        tok_rsn = [sch.tok("sS"), sch.tok("sV")]
        sfree("sq2", "m2s")

        swapb = salloc("swapb", [128, 4 * nkv], BF16)
        tb = salloc("tb", [128, 4 * 512], F32)
        sc1 = salloc("sc1", [128, 512], BF16)
        sc2 = salloc("sc2", [128, 512], BF16)
        rep_prev = [None, None]
        grt = 0
        for (buf, rsn, ntot, ti, qcol, woff) in (
                (qf, rsnq, nq, 0, 4 * l, q0), (kf, rsnk, nkv, 2, 4 * l + 2, kv0)):
            tok_sw = []
            swgate = [sch.tok("sV")]
            for t in range(4):
                for a in range(2):
                    for bb2 in range(2):
                        tok_sw.append(dma(
                            "sync", "dU",
                            sb_ap(swapb, 64 * a + 32 * (1 - bb2), 32, ntot * t, ntot),
                            sb_ap(buf, 64 * a + 32 * bb2, 32, ntot * t, ntot),
                            waits=tok_qkv + tok_rsn + swgate))
            tok_sw = [tok_sw[-1]]
            BART = [sch.tok("sT")] + tok_rsn
            tbfree = [None, None]
            for cb, (c0, w) in enumerate(_blocks(ntot)):
                rot = cb % 2
                dma("gpsimd", f"dT{rot}", tb[:, 1024 * rot:1024 * rot + w],
                    tab_d[ti][:, woff + c0:woff + c0 + w], waits=[tbfree[rot]] + BART)
                tt2 = dma("gpsimd", f"dT{rot}", tb[:, 1024 * rot + 512:1024 * rot + 512 + w],
                          tab_d[ti + 1][:, woff + c0:woff + c0 + w], waits=[])
                tt1 = tt2
                for t in range(4):
                    ta = vo(lambda e, t_=t, c=c0, ww=w, r=rot, b=buf, nt=ntot, qc=qcol:
                            e.scalar_tensor_tensor(
                                sc1[:, 0:ww], b[:, nt * t_ + c:nt * t_ + c + ww],
                                qspp[:, qc:qc + 1], tb[:, 1024 * r:1024 * r + ww],
                                ALU.mult, ALU.mult),
                            waits=[tt1] + tok_sw)
                    tbm = vo(lambda e, t_=t, c=c0, ww=w, r=rot, nt=ntot, qc=qcol:
                             e.scalar_tensor_tensor(
                                 sc2[:, 0:ww], swapb[:, nt * t_ + c:nt * t_ + c + ww],
                                 qspp[:, qc + 1:qc + 2], tb[:, 1024 * r + 512:1024 * r + 512 + ww],
                                 ALU.mult, ALU.mult),
                             waits=[tt2] + tok_sw)
                    tadd = vo(lambda e, t_=t, c=c0, ww=w, b=buf, nt=ntot: e.tensor_add(
                        b[:, nt * t_ + c:nt * t_ + c + ww], sc1[:, 0:ww], sc2[:, 0:ww]),
                        waits=[ta, tbm])
                    rb3 = 2 + grt % 2
                    grt += 1
                    trep = mm(bank(rb3, 0, 128, 0, w), c_rep8[0:8, 128 * t:128 * t + 128],
                              rsn[0:8, c0:c0 + w], start=True, stop=True,
                              waits=[rep_prev[rb3 - 2], tadd] + tok_rsn)
                    tmul = vo(lambda e, t_=t, c=c0, ww=w, b=buf, r=rb3, nt=ntot: e.tensor_mul(
                        b[:, nt * t_ + c:nt * t_ + c + ww],
                        b[:, nt * t_ + c:nt * t_ + c + ww], bank(r, 0, 128, 0, ww)),
                        waits=[trep])
                    rep_prev[rb3 - 2] = tmul
                tbfree[rot] = sch.tok("sV")
        tok_rot = sch.tok("sV")
        sfree("swapb", "tb", "sc1", "sc2", "rsnq", "rsnk")

        p1f_init = [None, None]

        o_fm = salloc("o_fm", [128, 4 * nq], BF16)
        odd2 = salloc("odd2", [64, nq], BF16)
        unno = salloc("unno", [65, 512], F32)
        recf2 = salloc("recf2", [1, 512], F32)
        qsg2 = [salloc(f"qsg{i}", [65, nq], BF16) for i in range(2)]
        ksg2 = [salloc(f"ksg{i}", [65, nkv], BF16) for i in range(2)]
        negM2s = salloc("negM2s", [1, nq], BF16)
        nMc2f = [salloc(f"nMcGf{i}", [128, 1], F32) for i in range(2)]
        negMc2f = [salloc(f"nMcLf{i}", [128, 1], F32) for i in range(2)]
        PW = (penlim - 1) * 128 + 640
        c_pen = salloc("c_pen", [1, PW], BF16)
        tok_pen = dma("gpsimd", "dW", c_pen[:], padpen_d[0:1, kv0:kv0 + PW],
                      waits=[sch.tok("sT")])
        pt2 = salloc("pt2", [128, 5 * 640], BF16)

        fst = {"p1": [p1f_init[0]], "st": [None, None], "pt": [None, None, None, None, None],
               "pv": [None, None], "rep": None, "mu": None}
        fargs = {}

        def fgA(h):
            par = h % 2
            pairt, odd = h // 2, h % 2
            qsg, ksg, negM2 = qsg2[par], ksg2[par], negM2s
            negMc_, negMcL = nMc2f[par], negMc2f[par]
            dma("sync", "dU", qsg[0:64, :], sb_ap(qf, 64 * odd, 64, nq * pairt, nq),
                waits=[tok_rot, fargs.get(("fstg", par))])
            tu2 = dma("sync", "dU", ksg[0:64, :], sb_ap(kf, 64 * odd, 64, nkv * pairt, nkv),
                      waits=[])
            tko = vo(lambda e, ks=ksg: e.memset(ks[64:65, :], 1.0), waits=[tu2])
            pend = None
            tneg = None
            for qc in range(nqc):
                tmm = mm(bank(0, 0, 128, 0, 512),
                         qsg[0:64, 128 * qc:128 * qc + 128],
                         ksg[0:64, 128 * qc:128 * qc + 512],
                         start=True, stop=(qc >= penlim),
                         waits=[tu2, tko, fst["p1"][0]])
                tmm = mm(bank(1, 0, 128, 0, 128),
                         qsg[0:64, 128 * qc:128 * qc + 128],
                         ksg[0:64, 128 * qc + 512:128 * qc + 640],
                         start=True, stop=(qc >= penlim), waits=[])
                if qc < penlim:
                    mm(bank(0, 0, 128, 0, 512), c_onesr[0:1, 0:128],
                       c_pen[0:1, 128 * qc:128 * qc + 512],
                       start=False, stop=True, waits=[tok_pen])
                    tmm = mm(bank(1, 0, 128, 0, 128), c_onesr[0:1, 0:128],
                             c_pen[0:1, 128 * qc + 512:128 * qc + 640],
                             start=False, stop=True, waits=[])
                tred = vo(lambda e, n=negMc_: e.tensor_reduce(
                    n[:], pscols(0, 640), AX.X, ALU.max, negate=True),
                    waits=[tmm, nmst.get("ftrp"), pend[1] if pend else None])
                fst["p1"][0] = tred
                if pend is not None:
                    q0, tn0 = pend
                    ttr = mm(bank(0, 0, 1, 0, 128), negMcL[:], c_iden[:],
                             start=True, stop=True, waits=[tn0, fst["p1"][0]])
                    nmst["ftrp"] = ttr
                    trev = so(lambda e, q=q0, nf=negM2: e.activation(
                        nf[0:1, 128 * q:128 * q + 128], bank(0, 0, 1, 0, 128), AF.Copy),
                        waits=[ttr, nmst.get("ftnmp")])
                    fst["p1"][0] = trev
                tcp = vo(lambda e, a=negMcL, b=negMc_: e.tensor_copy(a[:], b[:]),
                         waits=[tred, nmst.get("ftrp")])
                pend = (qc, tcp)
                yield
            q0, tn0 = pend
            ttr = mm(bank(0, 0, 1, 0, 128), negMc2f[par][:], c_iden[:],
                     start=True, stop=True, waits=[tn0, fst["p1"][0]])
            nmst["ftrp"] = ttr
            trev = so(lambda e, q=q0, nf=negM2: e.activation(
                nf[0:1, 128 * q:128 * q + 128], bank(0, 0, 1, 0, 128), AF.Copy),
                waits=[ttr, nmst.get("ftnmp")])
            fst["p1"][0] = trev
            tnm = dma("sync", "dSa" if par == 0 else "dSb", qsg[64:65, :], negM2[:], waits=[trev])
            nmst["ftnmp"] = tnm
            fargs[("tnm", h)] = tnm

        def fgB(h):
            par = h % 2
            pairt, odd = h // 2, h % 2
            qsg, ksg = qsg2[par], ksg2[par]
            tnm = fargs.pop(("tnm", h))
            pend = []
            for qc in range(nqc + 4):
                if qc < nqc:
                    stb = qc % 2
                    rb = 2 + 2 * stb
                    tmm = None
                    for jj in range(5):
                        edge = jj in (0, 4)
                        outap = (bank(rb, 0, 128, 128 * jj, 128) if jj < 4
                                 else bank(rb + 1, 0, 128, 0, 128))
                        tmm = mm(outap, ksg[0:65, 128 * (qc + jj):128 * (qc + jj) + 128],
                                 qsg[0:65, 128 * qc:128 * qc + 128],
                                 start=True, stop=(qc + jj >= penlim) and not edge,
                                 waits=([tnm, fst["st"][stb]] if jj == 0 else []))
                        if qc + jj < penlim:
                            tmm = mm(outap,
                                     c_pen[0:1, 128 * (qc + jj):128 * (qc + jj) + 128],
                                     c_onesr[0:1, 0:128], start=False, stop=not edge, waits=[])
                        if edge:
                            tmm = mm(outap, c_idenb[:],
                                     (c_maskf if jj == 0 else c_maskl)[:],
                                     start=False, stop=True, waits=[])
                    te2 = so(lambda e, a=rb, p=qc % 5: e.activation(
                        pt2[:, 640 * p:640 * p + 640], pscols(512 * a, 640), AF.Exp),
                        waits=[tmm, fst["pt"][qc % 5]])
                    fst["st"][stb] = te2
                    pend.append((qc, te2))
                if len(pend) > 4 or (qc >= nqc and pend):
                    q0, te0 = pend.pop(0)
                    pvb = q0 % 2
                    tpv = None
                    for jj in range(5):
                        tc = q0 + jj
                        tpv = mm(bank(6 + pvb, 0, 65, 0, 128),
                                 vbf[:, 520 * tc + 65 * h:520 * tc + 65 * h + 65],
                                 pt2[:, 640 * (q0 % 5) + 128 * jj:640 * (q0 % 5) + 128 * jj + 128],
                                 start=(jj == 0), stop=(jj == 4),
                                 waits=([te0] + ([fst["pv"][pvb]] if jj == 0 else [])))
                    fst["pt"][q0 % 5] = tpv
                    tevu = vo(lambda e, q=q0, b=pvb: e.tensor_copy(
                        unno[:, 128 * (q % 4):128 * (q % 4) + 128], bank(6 + b, 0, 65, 0, 128)),
                        waits=[tpv, fst["mu"]])
                    fst["pv"][pvb] = tevu
                    if q0 % 4 == 3 or q0 == nqc - 1:
                        w0 = 128 * (q0 % 4) + 128
                        c0 = 128 * (q0 - q0 % 4)
                        tall = sch.tok("sV")
                        tl1 = so(lambda e, ww=w0: e.activation(
                            recf2[0:1, 0:ww], sb_ap(unno, 64, 1, 0, ww), AF.Ln),
                            waits=[tall, fst["rep"]])
                        tl2 = so(lambda e, ww=w0: e.activation(
                            recf2[0:1, 0:ww], recf2[0:1, 0:ww],
                            AF.Exp, scale=-1.0), waits=[tl1])
                        rb2 = 6 + (q0 % 2)
                        trep = mm(bank(rb2, 0, 64, 0, w0), c_1f[0:1, 0:64],
                                  recf2[0:1, 0:w0],
                                  start=True, stop=True, waits=[tl2, fst["rep"]])
                        fst["rep"] = trep
                        dst = o_fm if odd == 0 else odd2
                        dc = (nq * pairt + c0) if odd == 0 else c0
                        tmu = vo(lambda e, d=dst, c=dc, r=rb2, ww=w0: e.tensor_mul(
                            sb_ap(d, 0, 64, c, ww), unno[0:64, 0:ww], bank(r, 0, 64, 0, ww)),
                            waits=[trep, fargs.get(("hd", h - 2)) if (odd and c0 == 0) else None])
                        fst["mu"] = tmu
                        fst["pv"][q0 % 2] = tmu
                yield
            fst["st"][0] = sch.tok("sV")
            if odd:
                fargs[("hd", h)] = dma("sync", "dX", sb_ap(o_fm, 64, 64, nq * pairt, nq),
                                       odd2[:], waits=[sch.tok("sV")])
            fargs[("fstg", par)] = sch.tok("sV")

        prevB = None
        for h in range(8):
            run_interleaved(prevB, fgA(h))
            prevB = fgB(h)
        run_interleaved(prevB, None)
        tok_attn = [sch.tok("sV"), sch.tok("sS"), sch.tok("sT"), sch.tok("dX")]
        sfree("qsg0", "qsg1", "ksg0", "ksg1", "negM2s", "pt2", "unno", "recf2",
              "odd2", "qf", "kf", "vbf", "c_pen",
              "nMcGf0", "nMcGf1", "nMcLf0", "nMcLf1")
        if l == 0:
            sown = bufs["sown"] = salloc("sown", [128, 8 * NSOWN], F32, high=True)
            gate = [sch.tok("sT"), sch.tok("sV"), sch.tok("sS"), sch.tok("dX")]
            lc_state["regfree"] = [None, None]
            lc_state["xfree"] = list(gate)
            for i in range(11, 28):
                ln_chunk("win", i, 32 + i, dst_f32=sown, f32col=128 * (i - 11), gate=gate)
            tok_stream = sch.all_toks()
            sfree("xtm2a", "xntm2a", "st_rs", "st_nb")
        else:
            tok_stream = tok_prev

        wof = salloc("wof", [128, 4 * 1024], BF16)
        for kt in range(4):
            dma("gpsimd", "dW", wof[:, 1024 * kt:1024 * kt + 1024],
                bass.AP(wof_d.tensor, l * 512 * D + 128 * kt * D, [[D, 128], [1, D]]),
                waits=tok_attn)
        twof = sch.tok("dW")
        wfree = [None, None, None, None]
        for (b0, w) in WO_BLKS[l]:
            for dmc in range(8):
                pb = dmc % 4
                tmm = None
                for kt in range(4):
                    tmm = mm(bank(pb, 0, 128, 0, w),
                             wof[:, 1024 * kt + 128 * dmc:1024 * kt + 128 * dmc + 128],
                             o_fm[:, nq * kt + (b0 - q0):nq * kt + (b0 - q0) + w],
                             start=(kt == 0), stop=(kt == 3),
                             waits=([twof, wfree[pb]] + tok_attn + list(tok_stream))
                             if kt == 0 else [])
                sl = stream_slice(dmc, b0, w)
                tadd = vo(lambda e, b=pb, s=sl, ww=w: e.tensor_add(
                    s, bank(b, 0, 128, 0, ww), s), waits=[tmm])
                wfree[pb] = tadd
        if l == 1:
            EDGE2 = 1536 - 1 - SOWN
            for t in range(8):
                vo(lambda e, t_=t: e.tensor_scalar_mul(
                    bufs["sown"][:, NSOWN * t_ + EDGE2:NSOWN * t_ + EDGE2 + 1],
                    bufs["sown"][:, NSOWN * t_ + EDGE2:NSOWN * t_ + EDGE2 + 1],
                    padedge[:, 0:1]), waits=[sch.tok("sV"), wfree[0], wfree[1]])
        tok_wo = [sch.tok("sV")]
        sfree("wof", "o_fm")

        w1b = salloc("w1b", [128, 8 * 5460], BF16)
        for kt in range(8):
            dma("gpsimd", "dW", w1b[:, 5460 * kt:5460 * kt + 5460],
                bass.AP(w1_d.tensor, l * D * 2 * HID + 128 * kt * 2 * HID, [[2 * HID, 128], [1, 2 * HID]]),
                waits=tok_attn + tok_wo)
        tw1 = sch.tok("dW")
        if l == 0:
            for lc in range(2):
                for dmc in range(8):
                    base = (lc * 8 + dmc) * 128 * 2816
                    dma("gpsimd", "dWc",
                        bass.AP(w2bf_d.tensor, base, [[2816, 128], [128, 21], [1, 128]]),
                        bass.AP(w2_d.tensor, lc * HID * D + 128 * dmc,
                                [[D, 128], [128 * D, 21], [1, 128]]), waits=[])
                    dma("gpsimd", "dWc",
                        bass.AP(w2bf_d.tensor, base + 2688, [[2816, 42], [1, 128]]),
                        bass.AP(w2_d.tensor, lc * HID * D + 2688 * D + 128 * dmc,
                                [[D, 42], [1, 128]]), waits=[])
            bufs["w2conv"] = sch.tok("dWc")
        w2s = salloc("w2s", [128, 3 * 2816], BF16)
        hbD = [salloc("hb0", [128, 8 * MLPW], BF16),
               salloc("hb1", [128, 8 * MLPW], BF16)]
        lnoD = hbD
        hidA = salloc("hidbA", [128, 11 * MLPW], BF16)
        hidB = salloc("hidbB", [128, 11 * MLPW], BF16)
        def hid_sl(hc, p1, c0, w):
            t = hidA if hc < 11 else hidB
            c = MLPW * (hc if hc < 11 else hc - 11)
            return t[0:p1, c + c0:c + c0 + w]
        sqh = salloc("sqh", [128, MLPW], BF16)
        statw = salloc("statw", [33, 2 * MLPW], F32)
        mu_r = statw[0:1, 0:MLPW]
        srow = statw[0:1, MLPW:2 * MLPW]
        ms_r = statw[32:33, 0:MLPW]
        vscr = statw[32:33, MLPW:2 * MLPW]
        sv = salloc("sv", [128, 8], F32)

        w2free = [tok_wo[0], tok_wo[0], tok_wo[0]]
        w2bank = [None, None]
        st = {"agfree": [None, None], "hbfree": [[], []], "svfree": [],
              "r1free": None, "r2free": None, "sqfree": None, "gglfree": None,
              "lnofree": [[], []], "prev_w": None, "lno_tok": {}, "sv_tok": {}}

        def gen_prep(bi, b0, w):
            par = bi % 2
            hb = hbD[par]
            lno = hb
            th = []
            for t in range(8):
                hbw = st["hbfree"][par] + st["lnofree"][par]
                if t < 4:
                    th += stream_rd_bf(t, b0, w,
                                       (lambda o, n, t_=t: hb[:, MLPW * t_ + o:MLPW * t_ + o + n]),
                                       waits=tok_wo + hbw)
                else:
                    if bi == 0:
                        th += stream_rd_bf(t, b0 - 1, w,
                                           (lambda o, n, t_=t: hb[:, MLPW * t_ + o:MLPW * t_ + o + n]),
                                           waits=tok_wo + hbw)
                    else:
                        th.append(vo(lambda e, t_=t: e.tensor_copy(
                            hb[:, MLPW * t_:MLPW * t_ + 1], sv[:, t_ - 4:t_ - 3]),
                            waits=st["svfree"] + hbw))
                        th += stream_rd_bf(t, b0, w - 1,
                                           (lambda o, n, t_=t: hb[:, MLPW * t_ + 1 + o:MLPW * t_ + 1 + o + n]),
                                           waits=tok_wo + hbw)
                if t % 2 == 1:
                    yield
            svfree = []
            svg = [sch.tok("sV")]
            for t in range(4, 8):
                svfree.append(vo(lambda e, t_=t, c=b0 + w - 1: e.tensor_copy(
                    sv[:, t_ - 4:t_ - 3], stream_slice(t_, c, 1)),
                    waits=tok_wo + [st["prev_w"]] + svg))
            st["svfree"] = svfree
            st["sv_tok"][bi] = svfree
            yield
            tmm_mu = tmm_ms = None
            for t in range(8):
                tsq = so(lambda e, t_=t, ww=w: e.activation(
                    sqh[:, 0:ww], hb[:, MLPW * t_:MLPW * t_ + ww], AF.Square),
                    waits=th + [st["sqfree"]])
                tmm_mu = mm(bank(4, 0, 1, 0, w), c_onesc[:], hb[:, MLPW * t:MLPW * t + w],
                            start=(t == 0), stop=(t == 7),
                            waits=(th + [st["r1free"]]) if t == 0 else [])
                tmm_ms = mm(bank(5, 0, 1, 0, w), c_onesc[:], sqh[:, 0:w],
                            start=(t == 0), stop=(t == 7),
                            waits=[tsq] + ([st["r2free"]] if t == 0 else []))
                st["sqfree"] = tmm_ms
                if t % 2 == 1:
                    yield
            tev = so(lambda e, ww=w: e.activation(mu_r[0:1, 0:ww], bank(4, 0, 1, 0, ww),
                                                  AF.Copy, scale=1.0 / 1024), waits=[tmm_mu])
            tev2 = so(lambda e, ww=w: e.activation(ms_r[0:1, 0:ww], bank(5, 0, 1, 0, ww),
                                                   AF.Copy, scale=1.0 / 1024), waits=[tmm_ms])
            tmu2 = so(lambda e, ww=w: e.activation(vscr[0:1, 0:ww], mu_r[0:1, 0:ww], AF.Square),
                      waits=[tev])
            tvar = vo(lambda e, ww=w: e.tensor_sub(vscr[0:1, 0:ww], ms_r[0:1, 0:ww],
                                                   vscr[0:1, 0:ww]), waits=[tmu2, tev2])
            tln = so(lambda e, ww=w: e.activation(vscr[0:1, 0:ww], vscr[0:1, 0:ww], AF.Ln,
                                                  bias=1.0e-5), waits=[tvar])
            texp = so(lambda e, ww=w: e.activation(srow[0:1, 0:ww], vscr[0:1, 0:ww], AF.Exp,
                                                   scale=-0.5), waits=[tln])
            yield
            tr1 = mm(bank(4, 0, 128, 0, w), c_1f[0:1, :], srow[0:1, 0:w],
                     start=True, stop=True, waits=[texp, st["r1free"]])
            tr2 = mm(bank(5, 0, 128, 0, w), c_1f[0:1, :], mu_r[0:1, 0:w],
                     start=True, stop=True, waits=[tev, st["r2free"]])
            for t in range(8):
                tv1 = vo(lambda e, t_=t, ww=w: e.scalar_tensor_tensor(
                    lno[:, MLPW * t_:MLPW * t_ + ww], hb[:, MLPW * t_:MLPW * t_ + ww], 1.0,
                    bank(5, 0, 128, 0, ww), ALU.mult, ALU.subtract),
                    waits=[tr2])
                st["r2free"] = tv1
                lastadd = vo(lambda e, t_=t, ww=w: e.scalar_tensor_tensor(
                    lno[:, MLPW * t_:MLPW * t_ + ww], lno[:, MLPW * t_:MLPW * t_ + ww], 1.0,
                    bank(4, 0, 128, 0, ww), ALU.mult, ALU.mult), waits=[tr1, tv1])
                st["r1free"] = lastadd
                so(lambda e, t_=t, ww=w: e.activation(
                    lno[:, MLPW * t_:MLPW * t_ + ww], lno[:, MLPW * t_:MLPW * t_ + ww],
                    AF.Identity, scale=lgpp[:, 8 * l + t_:8 * l + t_ + 1],
                    bias=lbpp[:, 8 * l + t_:8 * l + t_ + 1]), waits=[lastadd])
                if t % 2 == 1:
                    yield
            tok_lno = [sch.tok("sS"), sch.tok("sV")]
            st["lno_tok"][bi] = tok_lno
            st["hbfree"][par] = list(tok_lno)

        def gen_mlp(bi, b0, w):
            par = bi % 2
            lno = lnoD[par]
            tok_lno = st["lno_tok"].pop(bi)
            svfree_bi = st["sv_tok"].pop(bi)
            pendh = None
            for hc in range(NHC):
                whc = _hc_w(hc)
                ab, gb = (6, 7) if hc % 2 == 0 else (2, 3)
                ta = tg = None
                for kt in range(8):
                    ta = mm(bank(ab, 0, whc, 0, w),
                            w1b[:, 5460 * kt + 128 * hc:5460 * kt + 128 * hc + whc],
                            lno[:, MLPW * kt:MLPW * kt + w], start=(kt == 0), stop=(kt == 7),
                            waits=([tw1, st["agfree"][hc % 2]] + tok_lno if kt == 0 else []))
                    tg = mm(bank(gb, 0, whc, 0, w),
                            w1b[:, 5460 * kt + HID + 128 * hc:5460 * kt + HID + 128 * hc + whc],
                            lno[:, MLPW * kt:MLPW * kt + w], start=(kt == 0), stop=(kt == 7),
                            waits=[])
                if pendh is not None:
                    c0h, wh0, ab0, gb0, tg0, ta0 = pendh
                    tgg = so(lambda e, ww=w, wh=wh0, c=c0h, g=gb0: e.activation(
                        hid_sl(c, wh, 0, ww), bank(g, 0, wh, 0, ww), AF.Gelu,
                        bias=b1g[0:wh, 22 * l + c:22 * l + c + 1]),
                        waits=[tg0, st["gglfree"]])
                    gf = vo(lambda e, ww=w, wh=wh0, c=c0h, a=ab0: e.scalar_tensor_tensor(
                        hid_sl(c, wh, 0, ww), bank(a, 0, wh, 0, ww),
                        b1a[0:wh, 22 * l + c:22 * l + c + 1], hid_sl(c, wh, 0, ww),
                        ALU.add, ALU.mult), waits=[ta0, tgg])
                    st["gglfree"] = gf
                    st["agfree"][c0h % 2] = gf
                pendh = (hc, whc, ab, gb, tg, ta)
                yield
            c0h, wh0, ab0, gb0, tg0, ta0 = pendh
            tgg = so(lambda e, ww=w, wh=wh0, c=c0h, g=gb0: e.activation(
                hid_sl(c, wh, 0, ww), bank(g, 0, wh, 0, ww), AF.Gelu,
                bias=b1g[0:wh, 22 * l + c:22 * l + c + 1]), waits=[tg0, st["gglfree"]])
            gf = vo(lambda e, ww=w, wh=wh0, c=c0h, a=ab0: e.scalar_tensor_tensor(
                hid_sl(c, wh, 0, ww), bank(a, 0, wh, 0, ww),
                b1a[0:wh, 22 * l + c:22 * l + c + 1], hid_sl(c, wh, 0, ww),
                ALU.add, ALU.mult), waits=[ta0, tgg])
            st["gglfree"] = gf
            st["agfree"][c0h % 2] = gf
            st["lnofree"][par] = [sch.tok("sT")]
            tok_hid = sch.tok("sV")
            for dmc in range(8):
                slo = (8 * bi + dmc) % 3
                pbk = dmc % 2
                sname = ("dWa", "dWb", "dWd")[slo]
                dma("sync", sname,
                    bass.AP(w2s, 2816 * slo, [[3 * 2816, 128], [1, 2688]]),
                    bass.AP(w2bf_d.tensor, (l * 8 + dmc) * 128 * 2816,
                            [[2816, 128], [1, 2688]]),
                    waits=[w2free[slo], bufs["w2conv"]] +
                          (tok_wo if bi == 0 and dmc < 3 else []))
                tld2 = dma("sync", sname,
                           bass.AP(w2s, 2816 * slo + 2688, [[3 * 2816, 42], [1, 128]]),
                           bass.AP(w2bf_d.tensor, (l * 8 + dmc) * 128 * 2816 + 2688,
                                   [[2816, 42], [1, 128]]), waits=[])
                tld1 = tld2
                tmm = None
                for hc in range(NHC):
                    whc = _hc_w(hc)
                    tmm = mm(bank(pbk, 0, 128, 0, w),
                             w2s[0:whc, 2816 * slo + 128 * hc:2816 * slo + 128 * hc + 128],
                             hid_sl(hc, whc, 0, w),
                             start=(hc == 0), stop=(hc == NHC - 1),
                             waits=([tld1, tld2, tok_hid, w2bank[pbk]] if hc == 0 else []))
                sl2 = stream_slice(dmc, b0, w)
                tadd = vo(lambda e, b=pbk, s=sl2, ww=w, d=dmc: e.scalar_tensor_tensor(
                    s, bank(b, 0, 128, 0, ww), b2c[:, 8 * l + d:8 * l + d + 1], s,
                    ALU.add, ALU.add), waits=[tmm] + svfree_bi)
                w2free[slo] = tadd
                w2bank[pbk] = tadd
                yield
            st["prev_w"] = w2free[1] if w2free[1] else w2free[0]

        blks = MLP_BLKS[l]
        run_interleaved(gen_prep(0, *blks[0]), None)
        for bi, (b0, w) in enumerate(blks):
            nxt = gen_prep(bi + 1, *blks[bi + 1]) if bi + 1 < len(blks) else None
            run_interleaved(gen_mlp(bi, b0, w), nxt, 2, 1)
        tok_mlp = sch.all_toks()
        sfree("w1b", "w2s", "hb0", "hb1", "hidbA", "hidbB",
              "sqh", "statw", "sv")
        return tok_mlp

    toks = ffn_layer(0, [tok_spill] + sch.all_toks())
    EDGE = 1536 - 1 - SOWN
    for t in range(8):
        vo(lambda e, t_=t: e.tensor_scalar_mul(
            bufs["sown"][:, NSOWN * t_ + EDGE:NSOWN * t_ + EDGE + 1],
            bufs["sown"][:, NSOWN * t_ + EDGE:NSOWN * t_ + EDGE + 1],
            padedge[:, 0:1]), waits=toks[:])
    toks = sch.all_toks()
    toks = ffn_layer(1, toks)

    woab = salloc("woab", [128, 8 * 1024], BF16)
    bo_row = salloc("bo_row", [1, 1024], BF16)
    dma("gpsimd", "dW", bo_row[:], bo_d[None, :], waits=toks)
    for kt in range(8):
        dma("gpsimd", "dW", woab[:, 1024 * kt:1024 * kt + 1024],
            wo_a_d[128 * kt:128 * kt + 128, :], waits=toks)
    twoa = sch.tok("dW")
    attn2 = salloc("attn2", [128, 8 * NOWN], BF16)
    trl = None
    for t in range(8):
        trl = dma("gpsimd", "dS", attn2[:, 2048 * t:2048 * t + 2048],
                  spill_d[:, 2048 * t:2048 * t + 2048], waits=[tok_spill] + toks)
    xtm2 = salloc("xtm2", [128, 2 * 1024], F32)
    stg = salloc("stg", [128, 2 * 1024], F32)
    stg2 = salloc("stg2", [128, 2 * 1024], F32)
    tregf = [None, None]
    wregf = [None, None]
    stgf = [None, None]
    stg2f = [None, None]
    for c in range(16):
        rr = c % 2
        tl = dma("gpsimd", f"dT{rr}", xtm2[:, 1024 * rr:1024 * rr + 1024],
                 xseq[128 * c:128 * c + 128, :],
                 waits=[stgf[rr]] + (toks if c < 2 else []))
        tt = None
        for t in range(8):
            tt = tsp(pscols(1024 * rr + 128 * t, 128),
                     bufs["sown"][:, NSOWN * t + 128 + 128 * c:NSOWN * t + 128 + 128 * c + 128],
                     waits=toks + [tregf[rr]])
        tw = None
        for kt in range(8):
            for hh in range(2):
                tw = mm(pscols(2048 + 1024 * rr + 512 * hh, 512),
                        attn2[:, 2048 * kt + 128 * c:2048 * kt + 128 * c + 128],
                        woab[:, 1024 * kt + 512 * hh:1024 * kt + 512 * hh + 512],
                        start=(kt == 0), stop=False,
                        waits=([twoa, trl, wregf[rr]] if (kt == 0 and hh == 0) else []))
        for hh in range(2):
            tw = mm(pscols(2048 + 1024 * rr + 512 * hh, 512), c_onesr[0:1, 0:128],
                    bo_row[0:1, 512 * hh:512 * hh + 512], start=False, stop=True, waits=[])
        tv1 = vo(lambda e, r=rr: e.tensor_add(
            stg[:, 1024 * r:1024 * r + 1024], pscols(1024 * r, 1024),
            xtm2[:, 1024 * r:1024 * r + 1024]), waits=[tt, tl, stg2f[rr]])
        tregf[rr] = tv1
        tv2 = vo(lambda e, r=rr: e.tensor_add(
            stg2[:, 1024 * r:1024 * r + 1024], pscols(2048 + 1024 * r, 1024),
            stg[:, 1024 * r:1024 * r + 1024]), waits=[tw, tv1])
        wregf[rr] = tv2
        stg2f[rr] = tv2
        stgf[rr] = dma("sync", f"dO{rr}", out_d[128 * c:128 * c + 128, :],
                       stg2[:, 1024 * rr:1024 * rr + 1024], waits=[tv2])
    tok_out = (sch.tok("dO0"), sch.tok("dO1"))

    with nc.Block() as blk:
        def mkbody(name):
            def body(e):
                lastw = {}
                for waits, inc, incn, fn in sch.ops[name]:
                    for sem, val in waits:
                        if val > 0 and lastw.get(sem, -1) < val:
                            e.wait_ge(semh[sem], val)
                            lastw[sem] = val
                    ins = fn(e)
                    if inc is not None:
                        ins.then_inc(semh[inc], incn)
                if name == "sync":
                    e.wait_ge(semh["dO0"], tok_out[0][1])
                    e.wait_ge(semh["dO1"], tok_out[1][1])
            return body
        blk.sync(mkbody("sync"))
        blk.gpsimd(mkbody("gpsimd"))
        blk.tensor(mkbody("tensor"))
        blk.vector(mkbody("vector"))
        blk.scalar(mkbody("scalar"))
    if __name__ == "__main__":
        print({e: len(sch.ops[e]) for e in ENGS}, "sem:", sch.cnt,
              "sbuf free:", sorted(free), file=sys.stderr)
    return nc


def kernel(**inputs):
    x = np.ascontiguousarray(np.asarray(inputs["L0_attn_x"], dtype=np.float32))
    if _PROG[0] is None:
        _PROG[0] = build_program()
    nc = _PROG[0]
    shared = dict(
        wq=np.ascontiguousarray(inputs["L0_attn_wq"], dtype=np.float32),
        wk=np.ascontiguousarray(inputs["L0_attn_wk"], dtype=np.float32),
        wv=np.ascontiguousarray(inputs["L0_attn_wv"], dtype=np.float32),
        wo_a=np.ascontiguousarray(inputs["L0_attn_wo"], dtype=np.float32),
        bq=np.ascontiguousarray(inputs["L0_attn_bq"], dtype=np.float32),
        bk=np.ascontiguousarray(inputs["L0_attn_bk"], dtype=np.float32),
        bv=np.ascontiguousarray(inputs["L0_attn_bv"], dtype=np.float32),
        bo=np.ascontiguousarray(inputs["L0_attn_bo"], dtype=np.float32),
        wqkv=np.ascontiguousarray(inputs["L0_ffn_wqkv"], dtype=np.float32),
        qscale=np.ascontiguousarray(inputs["L0_ffn_q_scale"], dtype=np.float32),
        kscale=np.ascontiguousarray(inputs["L0_ffn_k_scale"], dtype=np.float32),
        wo_f=np.ascontiguousarray(inputs["L0_ffn_wo"], dtype=np.float32),
        ln_g=np.ascontiguousarray(inputs["L0_ffn_ln_g"], dtype=np.float32),
        ln_b=np.ascontiguousarray(inputs["L0_ffn_ln_b"], dtype=np.float32),
        w1=np.ascontiguousarray(inputs["L0_ffn_w1"], dtype=np.float32),
        b1=np.ascontiguousarray(inputs["L0_ffn_b1"], dtype=np.float32),
        w2=np.ascontiguousarray(inputs["L0_ffn_w2"], dtype=np.float32),
        b2=np.ascontiguousarray(inputs["L0_ffn_b2"], dtype=np.float32),
        iden=_CONST["iden"], ones_row=_CONST["ones_row"], ones_col=_CONST["ones_col"],
        mask_first=_CONST["mask_first"], mask_last=_CONST["mask_last"],
        l2ones=_CONST["l2ones"], rep8=_CONST["rep8"], ones1f=_CONST["ones1f"],
    )
    in_maps = []
    for c in range(8):
        b, h = c // 2, c % 2
        T0 = NOWN * h
        xs = np.ascontiguousarray(np.concatenate([x[b][T0:], x[b][:T0]], axis=0))
        if h == 1:
            xw = np.ascontiguousarray(x[b][T0 - 1536:T0 + NOWN])
            pen = np.zeros((1, WIN), dtype=ml_dtypes.bfloat16)
        else:
            xw = np.ascontiguousarray(
                np.concatenate([np.zeros((1536, D), np.float32), x[b][0:NOWN]], axis=0))
            penf = np.zeros((1, WIN), dtype=np.float32)
            penf[0, :PADTOP] = NEG
            pen = penf.astype(ml_dtypes.bfloat16)
        m = dict(shared)
        m["xseq"] = xs
        m["xwin"] = xw
        m["padpen"] = pen
        m["padedge"] = np.full((128, 1), float(h), dtype=np.float32)
        for i in range(4):
            m[f"tab{i}"] = _TABS[h][i]
        in_maps.append(m)
    res = run_bass_kernel_spmd(nc, in_maps, core_ids=list(range(8)))
    out = np.empty((B, S, D), dtype=np.float32)
    for c in range(8):
        b, h = c // 2, c % 2
        out[b, NOWN * h:NOWN * h + NOWN] = res.results[c]["out"]
    return out


if __name__ == "__main__":
    build_program()
    print("build ok")


# version 40
# speedup vs baseline: 1.0175x; 1.0033x over previous
import numpy as np
import ml_dtypes
import concourse.bass as bass
import concourse.mybir as mybir
from concourse.bass_utils import run_bass_kernel_spmd
import sys

DEBUG_TAGS = False
_TAGS = []

F32 = mybir.dt.float32
BF16 = mybir.dt.bfloat16
AF = mybir.ActivationFunctionType
ALU = mybir.AluOpType
AX = mybir.AxisListType

B, S, D = 4, 4096, 1024
NOWN = 2048
NEG = -float(2.0 ** 99)
WIN = 3584
KV = {0: 128, 1: 768}
Q0 = {0: 640, 1: 1280}
X1AA = 766
SOWN = 1408
NHALO = SOWN - X1AA
NSOWN = WIN - SOWN
HID = 2730
NHC = 22
PADTOP = 1536
DEBUG_SPILL = False
NKV = {l: WIN - KV[l] for l in (0, 1)}
NQ = {l: WIN - Q0[l] for l in (0, 1)}
NQC = {l: NQ[l] // 128 for l in (0, 1)}
NKVC = {l: NKV[l] // 128 for l in (0, 1)}

WO_BLKS = {0: [(766, 322), (1088, 320), (1408, 512), (1920, 512), (2432, 512), (2944, 512), (3456, 128)],
           1: [(1280, 128), (1408, 512), (1920, 512), (2432, 512), (2944, 512), (3456, 128)]}
MLP_BLKS = {0: [(767, 257), (1024, 256), (1280, 128)]
               + [(1408 + 256 * i, 256) for i in range(8)] + [(3456, 128)],
            1: [(1408 + 256 * i, 256) for i in range(8)] + [(3456, 128)]}
MLPW = 257


def _hc_w(hc):
    return 128 if hc < NHC - 1 else HID - 128 * (NHC - 1)


def _blocks(total, step=512):
    res, o = [], 0
    while o < total:
        res.append((o, min(step, total - o)))
        o += step
    return res


def _rotary_tables(abs_t0):
    DH, WINDOW, QK = 64, 512, 8.0
    half = np.arange(0, DH, 2, dtype=np.float64)
    inv = 1.0 / 10000.0 ** (half / DH)
    t = abs_t0 + np.arange(WIN, dtype=np.float64)
    fr = t[:, None] * inv[None, :]
    fr = np.concatenate([fr, fr], -1)
    base = (half + 0.4 * DH) / (1.4 * DH)
    power = np.clip((t - S // 2) / float(WINDOW // 2), -12.0, 12.0)
    sc = base[None, :] ** power[:, None]
    sc = np.concatenate([sc, sc], -1)
    cos, sin = np.cos(fr), np.sin(fr)
    sgn = np.concatenate([-np.ones(32), np.ones(32)])[None, :]
    outs = [(cos * sc) * QK, (sin * sc) * sgn * QK, cos / sc, (sin / sc) * sgn]

    def dup(a):
        a = a.T.astype(np.float32)
        return np.ascontiguousarray(np.concatenate([a, a], 0))
    return [dup(o) for o in outs]


def _host_consts():
    c = {}
    c["iden"] = np.eye(128, dtype=np.float32)
    c["ones_row"] = np.ones((1, 640), dtype=ml_dtypes.bfloat16)
    c["ones_col"] = np.ones((128, 1), dtype=ml_dtypes.bfloat16)
    s = np.arange(128)[:, None]
    r = np.arange(128)[None, :]
    c["mask_first"] = np.where(s >= r, 0.0, NEG).astype(np.float32)
    c["mask_last"] = np.where(s <= r, 0.0, NEG).astype(np.float32)
    l2 = np.zeros((128, 32), dtype=np.float32)
    for i in range(4):
        l2[0:64, 8 * i + 2 * i] = 1
        l2[64:128, 8 * i + 2 * i + 1] = 1
    c["l2ones"] = l2.astype(ml_dtypes.bfloat16)
    rp = np.zeros((8, 512), dtype=np.float32)
    for i in range(4):
        rp[2 * i, 128 * i:128 * i + 64] = 1
        rp[2 * i + 1, 128 * i + 64:128 * i + 128] = 1
    c["rep8"] = rp.astype(ml_dtypes.bfloat16)
    c["ones1f"] = np.ones((1, 128), dtype=np.float32)
    return c


_CONST = _host_consts()
_TABS = {0: _rotary_tables(-1536), 1: _rotary_tables(512)}
_PROG = [None]

ENGS = ("sync", "gpsimd", "tensor", "vector", "scalar")
SEMS = ("dX", "dW", "dS", "sT", "sV", "sS", "dL0", "dL1", "dL2", "dL3", "dLC", "dU", "dT0", "dT1", "dWa", "dWb", "dO0", "dO1", "dSa", "dSb", "dWc", "dWd", "dUg", "dUh")


class Sched:
    def __init__(self):
        self.ops = {e: [] for e in ENGS}
        self.cnt = {s: 0 for s in SEMS}

    def op(self, eng, fn, waits=(), inc=None, incn=1):
        w = [t for t in waits if t is not None and t[1] > 0]
        self.ops[eng].append((w, inc, incn, fn))
        if inc is not None:
            self.cnt[inc] += incn

    def tok(self, sem):
        return (sem, self.cnt[sem])

    def all_toks(self):
        return [self.tok(s) for s in SEMS]


def build_program():
    nc = bass.Bass("TRN2", target_bir_lowering=False)
    sch = Sched()
    semh = {s: nc.alloc_semaphore(s) for s in SEMS}

    def din(name, shape, dt=F32):
        return nc.dram_tensor(name, list(shape), dt, kind="ExternalInput").ap()

    xseq = din("xseq", (S, D))
    xwin = din("xwin", (WIN, D))
    padpen_d = din("padpen", (1, WIN), BF16)
    padedge_d = din("padedge", (128, 1))
    wq_d = din("wq", (D, D)); wk_d = din("wk", (D, D)); wv_d = din("wv", (D, D))
    wo_a_d = din("wo_a", (D, D))
    bq_d = din("bq", (D,)); bk_d = din("bk", (D,)); bv_d = din("bv", (D,)); bo_d = din("bo", (D,))
    wqkv_d = din("wqkv", (2, D, 1536))
    qs_d = din("qscale", (2, 64)); ks_d = din("kscale", (2, 64))
    wof_d = din("wo_f", (2, 512, D))
    lng_d = din("ln_g", (2, D)); lnb_d = din("ln_b", (2, D))
    w1_d = din("w1", (2, D, 2 * HID)); b1_d = din("b1", (2, 2 * HID))
    w2_d = din("w2", (2, HID, D)); b2_d = din("b2", (2, D))
    iden_d = din("iden", (128, 128))
    onesr_d = din("ones_row", (1, 640), BF16)
    onesc_d = din("ones_col", (128, 1), BF16)
    maskf_d = din("mask_first", (128, 128))
    maskl_d = din("mask_last", (128, 128))
    l2o_d = din("l2ones", (128, 32), BF16)
    rep8_d = din("rep8", (8, 512), BF16)
    ones1f_d = din("ones1f", (1, 128))
    tab_d = [din(f"tab{i}", (128, WIN)) for i in range(4)]
    out_d = nc.dram_tensor("out", [NOWN, D], F32, kind="ExternalOutput").ap()
    spill_d = nc.dram_tensor("attn_spill", [128, 8 * NOWN], BF16, kind="ExternalOutput" if DEBUG_SPILL else "Internal").ap()
    w2bf_d = nc.dram_tensor("w2bf", [2 * 8 * 128 * 2816], BF16, kind="Internal").ap()

    free = [[17024, 229376]]
    live = {}

    def salloc(name, shape, dt, high=False):
        al = 64
        bpp = (int(np.prod(shape[1:])) * mybir.dt.size(dt) + al - 1) // al * al
        segs = sorted(free, reverse=high)
        for seg in segs:
            if seg[1] - seg[0] < bpp:
                continue
            off = ((seg[1] - bpp) // al * al) if high else seg[0]
            if off < seg[0]:
                continue
            t = nc.alloc_sbuf_tensor_at(name, list(shape), dt, offset=off)
            live[name] = [off, off + bpp]
            free.remove(seg)
            if seg[0] < off:
                free.append([seg[0], off])
            if off + bpp < seg[1]:
                free.append([off + bpp, seg[1]])
            return t
        raise RuntimeError(f"SBUF OOM {name}: need {bpp}, free={sorted(free)}, live={ {k: (v[1]-v[0]) for k, v in sorted(live.items(), key=lambda x: -x[1][1]+x[1][0])} }")

    def sfree(*names):
        for n in names:
            free.append(live.pop(n))
        free.sort()
        m = [free[0]]
        for seg in free[1:]:
            if seg[0] <= m[-1][1]:
                m[-1][1] = max(m[-1][1], seg[1])
            else:
                m.append(seg)
        free[:] = m

    ps_t = nc.alloc_psum_tensor("ps", [128, 4096], F32)

    def bank(i, p0=0, p1=128, c0=0, cw=512):
        return ps_t[p0:p1, 512 * i + c0:512 * i + c0 + cw]

    def pscols(c0, cw, p0=0, p1=128):
        return ps_t[p0:p1, c0:c0 + cw]

    def dma(queue, sem, outap, inap, waits=(), slow=False):
        sch.op(queue, lambda e, o=outap, i=inap, s=slow: e.dma_start(
            o, i, allow_slow_non_contiguous=s), waits, sem, 16)
        return sch.tok(sem)

    def mm(outap, lhsT, rhs, start, stop, waits=(), inc="sT", tp=None):
        if DEBUG_TAGS:
            _TAGS.append(sys._getframe(1).f_lineno)
        sch.op("tensor",
               lambda e, o=outap, l=lhsT, r=rhs, st=start, sp=stop, t=tp:
               e.matmul(o, l, r, start=st, stop=sp, tile_position=t), waits, inc)
        return sch.tok("sT")

    def tsp(outap, in_ap, waits=()):
        if DEBUG_TAGS:
            _TAGS.append(-sys._getframe(1).f_lineno)
        sch.op("tensor", lambda e, o=outap, i=in_ap: e.transpose(o, i, c_iden[:]), waits, "sT")
        return sch.tok("sT")

    def vo(fn, waits=(), inc="sV"):
        sch.op("vector", fn, waits, inc)
        return sch.tok("sV")

    def so(fn, waits=(), inc="sS"):
        sch.op("scalar", fn, waits, inc)
        return sch.tok("sS")

    def sb_ap(t, part0, nparts, col0, ncols):
        return bass.AP(t, part0 * t.shape[1] + col0, [[t.shape[1], nparts], [1, ncols]])

    c_iden = salloc("c_iden", [128, 128], F32)
    c_onesr = salloc("c_onesr", [1, 128], BF16)
    c_onesc = salloc("c_onesc", [128, 1], BF16)
    c_maskf = salloc("c_maskf", [128, 128], BF16)
    c_maskl = salloc("c_maskl", [128, 128], BF16)
    c_idenb = salloc("c_idenb", [128, 128], BF16)
    c_l2o = salloc("c_l2o", [128, 32], BF16)
    c_rep8 = salloc("c_rep8", [8, 512], BF16)
    c_1f = salloc("c_1f", [1, 128], F32)
    qkvb = salloc("qkvb", [64, 48], F32)
    b2c = salloc("b2c", [128, 16], F32)
    lgpp = salloc("lgpp", [128, 16], F32)
    lbpp = salloc("lbpp", [128, 16], F32)
    b1a = salloc("b1a", [128, 44], F32)
    b1g = salloc("b1g", [128, 44], F32)
    qspp = salloc("qspp", [128, 8], F32)
    st_rs = salloc("st_rs", [128, 60], F32)
    st_nb = salloc("st_nb", [128, 60], F32)
    padedge = salloc("padedge", [128, 1], F32)
    Mb = salloc("Mb", [128, 4], F32)
    cflt = salloc("cflt", [128, 1], F32)
    tok_cf = vo(lambda e: e.memset(cflt[:], 1.0e-5))
    nc.const_aps.register(F32, 1.0e-5, cflt[:, 0:1])

    dma("gpsimd", "dW", c_iden[:], iden_d)
    dma("gpsimd", "dW", c_onesr[:], onesr_d[0:1, 0:128])
    dma("gpsimd", "dW", c_onesc[:], onesc_d)
    dma("gpsimd", "dW", c_maskf[:], maskf_d)
    dma("gpsimd", "dW", c_maskl[:], maskl_d)
    dma("gpsimd", "dW", c_idenb[:], iden_d)
    dma("gpsimd", "dW", c_l2o[:], l2o_d)
    dma("gpsimd", "dW", c_rep8[:], rep8_d)
    dma("gpsimd", "dW", c_1f[:], ones1f_d)
    dma("gpsimd", "dW", padedge[:], padedge_d)
    for j, bsrc in enumerate((bq_d, bk_d, bv_d)):
        dma("gpsimd", "dW", qkvb[:, 16 * j:16 * j + 16],
            bass.AP(bsrc.tensor, 0, [[1, 64], [64, 16]]), slow=True)
    for l in range(2):
        dma("gpsimd", "dW", b2c[:, 8 * l:8 * l + 8],
            bass.AP(b2_d.tensor, l * D, [[1, 128], [128, 8]]), slow=True)
        dma("gpsimd", "dW", lgpp[:, 8 * l:8 * l + 8],
            bass.AP(lng_d.tensor, 1024 * l, [[1, 128], [128, 8]]), slow=True)
        dma("gpsimd", "dW", lbpp[:, 8 * l:8 * l + 8],
            bass.AP(lnb_d.tensor, 1024 * l, [[1, 128], [128, 8]]), slow=True)
        for (dst, off) in ((b1a, 0), (b1g, HID)):
            dma("gpsimd", "dW", dst[:, 22 * l:22 * l + 21],
                bass.AP(b1_d.tensor, l * 2 * HID + off, [[1, 128], [128, 21]]), slow=True)
            dma("gpsimd", "dW", dst[0:42, 22 * l + 21:22 * l + 22],
                bass.AP(b1_d.tensor, l * 2 * HID + off + 2688, [[1, 42], [42, 1]]), slow=True)
        for half in range(2):
            p0 = 64 * half
            for (cc, src, o1, o2) in ((0, qs_d, 0, 32), (2, ks_d, 0, 32)):
                dma("gpsimd", "dW", qspp[p0:p0 + 64, 4 * l + cc:4 * l + cc + 1],
                    bass.AP(src.tensor, 64 * l, [[1, 64], [64, 1]]), slow=True)
                dma("gpsimd", "dW", qspp[p0:p0 + 32, 4 * l + cc + 1:4 * l + cc + 2],
                    bass.AP(src.tensor, 64 * l + 32, [[1, 32], [32, 1]]), slow=True)
                dma("gpsimd", "dW", qspp[p0 + 32:p0 + 64, 4 * l + cc + 1:4 * l + cc + 2],
                    bass.AP(src.tensor, 64 * l, [[1, 32], [32, 1]]), slow=True)
    tok_consts = sch.tok("dW")
    tok_bq = vo(lambda e: e.tensor_scalar_mul(qkvb[:, 0:16], qkvb[:, 0:16], 0.125),
                waits=[tok_consts])

    xnseq = salloc("xnseq", [128, 8 * S], BF16)
    xtm = salloc("xtm", [128, 4 * 1024], F32)
    st_sum = salloc("st_sum", [128, 60], F32)
    st_sq = salloc("st_sq", [128, 60], F32)
    xn_tm = salloc("xn_tm", [128, 1024], F32)

    CH = [("seq", i) for i in range(32)] + [("win", i) for i in range(28)]

    def xsrc(which, i):
        src = xseq if which == "seq" else xwin
        return src[128 * i:128 * i + 128, :]

    lc_state = dict(xfree=[], reg=0, regfree=[None, None], xtm=xtm, xntm=None)

    xtm_free = [[], [], [], []]
    for ci, (which, i) in enumerate(CH):
        sl = ci % 4
        tl = dma("sync", f"dL{sl}", xtm[:, 1024 * sl:1024 * sl + 1024], xsrc(which, i),
                 waits=xtm_free[sl])
        tv = vo(lambda e, s=sl, c=ci: e.reduce_sum(
            st_sum[:, c:c + 1], xtm[:, 1024 * s:1024 * s + 1024], AX.X), waits=[tl])
        tq = so(lambda e, s=sl, c=ci: e.activation(
            xtm[:, 1024 * s:1024 * s + 1024], xtm[:, 1024 * s:1024 * s + 1024], AF.Square,
            accum_out=st_sq[:, c:c + 1]), waits=[tl, tok_cf, tv])
        xtm_free[sl] = [tv, tq]
    t1 = so(lambda e: e.activation(st_sum[:], st_sum[:], AF.Copy, scale=1.0 / 1024),
            waits=[sch.tok("sV")])
    t2 = so(lambda e: e.activation(st_nb[:], st_sum[:], AF.Square), waits=[t1])
    t3 = vo(lambda e: e.scalar_tensor_tensor(st_sq[:], st_sq[:], 1.0 / 1024, st_nb[:],
                                             ALU.mult, ALU.subtract), waits=[t2])
    t4 = so(lambda e: e.activation(st_sq[:], st_sq[:], AF.Ln, bias=1.0e-5), waits=[t3])
    t5 = so(lambda e: e.activation(st_rs[:], st_sq[:], AF.Exp, scale=-0.5), waits=[t4])
    tok_stats = vo(lambda e: e.scalar_tensor_tensor(st_nb[:], st_sum[:], -1.0, st_rs[:],
                                                    ALU.mult, ALU.mult), waits=[t5])

    def ln_chunk(which, i, statcol, dst_bf=None, bfcol=0, dst_f32=None, f32col=0, gate=()):
        xt, xnt = lc_state["xtm"], lc_state["xntm"]
        tl = dma("sync", "dLC", xt[:, 0:1024], xsrc(which, i), waits=list(lc_state["xfree"]) + list(gate))
        ta = so(lambda e, c=statcol, x1=xt, x2=xnt: e.activation(
            x2[:], x1[:, 0:1024], AF.Identity,
            scale=st_rs[:, c:c + 1], bias=st_nb[:, c:c + 1]), waits=[tl, tok_stats])
        lc_state["xfree"] = [ta]
        if lc_state.get("regpin"):
            reg = 0
        else:
            reg = lc_state["reg"]
            lc_state["reg"] ^= 1
        tt = None
        for t in range(8):
            tt = tsp(pscols(1024 * reg + 128 * t, 128),
                     lc_state["xntm"][:, 128 * t:128 * t + 128],
                     waits=[ta, lc_state["regfree"][reg]] + list(gate))
        ev = None
        for (dd, dc) in ((dst_bf, bfcol), (dst_f32, f32col)):
            if dd is None:
                continue
            sd = dd.shape[1] // 8
            ev = so(lambda e, r=reg, d=dd, c=dc, s=sd: e.activation(
                bass.AP(d, c, [[d.shape[1], 128], [s, 8], [1, 128]]),
                bass.AP(ps_t, 1024 * r, [[4096, 128], [128, 8], [1, 128]]),
                AF.Copy), waits=[tt])
        lc_state["regfree"][reg] = ev

    lc_state["xntm"] = xn_tm
    lc_state["xfree"] = [sch.tok("sS"), sch.tok("sV")]
    for i in range(32):
        ln_chunk("seq", i, i, dst_bf=xnseq, bfcol=128 * i)
    tok_xn = sch.all_toks()
    sfree("st_sum", "st_sq", "xtm", "xn_tm")

    wqb = salloc("wqb", [128, 8 * 1024], BF16)
    wkb = salloc("wkb", [128, 8 * 1024], BF16)
    wvb = salloc("wvb", [128, 8 * 1024], BF16)
    BARB = [sch.tok("sT"), sch.tok("sV"), sch.tok("sS")]
    for kt in range(8):
        dma("gpsimd", "dW", wqb[:, 1024 * kt:1024 * kt + 1024], wq_d[128 * kt:128 * kt + 128, :], waits=BARB)
        dma("gpsimd", "dW", wkb[:, 1024 * kt:1024 * kt + 1024], wk_d[128 * kt:128 * kt + 128, :], waits=BARB)
        dma("gpsimd", "dW", wvb[:, 1024 * kt:1024 * kt + 1024], wv_d[128 * kt:128 * kt + 128, :], waits=BARB)
    tok_wqkv = sch.tok("dW")

    attn_out = salloc("attn_out", [128, 2 * NOWN], BF16)
    repsG = salloc("repsG", [64, 512], F32)
    oddst = salloc("oddst", [64, NOWN], BF16)
    ptb = salloc("ptb", [128, 5 * 512], BF16)
    lnrecB = salloc("lnrecB", [33, 512], F32)
    recG = lnrecB[0:1, :]
    lnG = lnrecB[32:33, :]
    qstg4 = [[salloc(f"qstg{i}{j}", [65, NOWN], BF16) for j in range(2)] for i in range(2)]
    kstg4 = [[salloc(f"kstg{i}{j}", [65, S], BF16) for j in range(2)] for i in range(2)]
    vb2 = [salloc(f"vb{i}", [128, 32 * 132], BF16) for i in range(2)]
    sqscr = salloc("sqscr", [64, 2048], BF16)
    nrmx = salloc("nrmx", [1, 32], F32)
    cch = salloc("cch", [1, 16], F32)

    tko4 = [None, None]
    for i in range(2):
        t0k = vo(lambda e, a=i: e.memset(kstg4[a][0][64:65, :], 1.0),
                 waits=BARB if i == 0 else [])
        tko4[i] = vo(lambda e, a=i: e.memset(kstg4[a][1][64:65, :], 1.0), waits=[t0k])

    gstate = {"p1": [None, None, None, None],
              "st": [None, None, None, None],
              "pt": [None, None, None, None, None],
              "pv": [None, None],
              "rep": None, "vb": [None, None],
              "tmu_prev": None}
    nmst = {"sq": [None, None, None, None], "nrm": [None, None], "cch": [None, None]}
    spill_tok = [None, None]
    hargs = {}

    def gA2(pairt):
        H0, H1 = 2 * pairt, 2 * pairt + 1
        qstg2 = qstg4[pairt % 2]
        kstg2 = kstg4[pairt % 2]
        vb = vb2[pairt % 2]
        gT = sch.tok("sT")
        tms = vo(lambda e, v=vb: e.memset(
            bass.AP(v, 64, [[v.shape[1], 128], [132, 32], [66, 2]]), 1.0),
            waits=[gT, tok_xn[4], tok_xn[5], gstate["vb"][pairt % 2]])
        for tc in range(32):
            pb = tc % 2
            tmm = None
            for kt in range(8):
                tmm = mm(bank(pb, 0, 128, 0, 128),
                         xnseq[:, 4096 * kt + 128 * tc:4096 * kt + 128 * tc + 128],
                         wvb[:, 1024 * kt + 128 * pairt:1024 * kt + 128 * pairt + 128],
                         start=(kt == 0), stop=(kt == 7),
                         waits=([tok_wqkv, tok_xn[4], tok_xn[5], gstate["p1"][pb], gT]
                                if kt == 0 else []))
            ev = vo(lambda e, b=pb, t=tc, v=vb: e.tensor_copy(
                bass.AP(v, 132 * t, [[v.shape[1], 128], [66, 2], [1, 64]]),
                bass.AP(ps_t, 512 * b, [[4096, 128], [64, 2], [1, 64]])),
                waits=[tmm, tms])
            gstate["p1"][pb] = ev
            if tc % 4 == 3:
                yield
        hargs[("v", pairt)] = sch.tok("sV")
        evq = [None, None]
        evk = [None, None]
        for src_w, dstl, bias0, scl, nn, tag in (
                (wqb, qstg2, 0, 0.125, NOWN, "q"), (wkb, kstg2, 16, 1.0, S, "k")):
            for cb, (c0, w) in enumerate(_blocks(nn)):
                pb = cb % 2
                tmm = None
                for kt in range(8):
                    for par in range(2):
                        tmm = mm(bank(pb, 64 * par, 64 * par + 64, 0, w),
                                 src_w[:, 1024 * kt + 64 * (H0 + par):1024 * kt + 64 * (H0 + par) + 64],
                                 xnseq[:, 4096 * kt + c0:4096 * kt + c0 + w],
                                 start=(kt == 0), stop=(kt == 7), tp=(0, 64 * par),
                                 waits=([gstate["p1"][pb], tok_bq,
                                         hargs.get(("stgfree", pairt % 2))] if kt == 0 else []))
                ev = None
                for par in range(2):
                    ev = vo(lambda e, b=pb, c=c0, ww=w, p=par, d=dstl, bc=bias0, s=scl:
                            e.tensor_scalar(d[p][0:64, c:c + ww],
                                            bank(b, 64 * p, 64 * p + 64, 0, ww),
                                            s, qkvb[0:64, bc + H0 + p:bc + H0 + p + 1],
                                            ALU.mult, ALU.add),
                            waits=[tmm])
                gstate["p1"][pb] = ev
                if cb % 2 == 1:
                    yield
        tko1 = tko4[pairt % 2]
        tok_stg = sch.tok("sV")
        tkon = [tko1, tko1]
        for par in range(2):
            blks = [(qstg2[par], c) for c in range(0, NOWN, 512)] + \
                   [(kstg2[par], c) for c in range(0, S, 512)]
            nb0 = 16 * par
            sqfree = nmst["sq"]
            tred = None
            for bi, (stg, c0) in enumerate(blks):
                sl = bi % 4
                nb = bi % 2
                tsq = vo(lambda e, t=stg, c=c0, s=sl: e.tensor_tensor(
                    sqscr[:, 512 * s:512 * s + 512], t[0:64, c:c + 512],
                    t[0:64, c:c + 512], ALU.mult),
                    waits=[tok_stg, sqfree[sl]])
                tnmm = mm(bank(nb, 0, 1, 0, 512), c_onesc[0:64, 0:1],
                          sqscr[0:64, 512 * sl:512 * sl + 512],
                          start=True, stop=True, waits=[tsq, gstate["p1"][nb]])
                sqfree[sl] = tnmm
                tred = vo(lambda e, b=bi, n=nb, p=par: e.tensor_reduce(
                    nrmx[0:1, nb0 + b:nb0 + b + 1], bank(n, 0, 1, 0, 512), AX.X, ALU.max),
                    waits=[tnmm, nmst["nrm"][par] if bi < 2 else None])
                gstate["p1"][nb] = tred
                yield
            cb = 8 * par
            tq = vo(lambda e: e.tensor_reduce(
                cch[0:1, cb + 0:cb + 1], nrmx[0:1, nb0:nb0 + 4], AX.X, ALU.max),
                waits=[tred, nmst["cch"][par]])
            tk = vo(lambda e: e.tensor_reduce(
                cch[0:1, cb + 1:cb + 2], nrmx[0:1, nb0 + 4:nb0 + 12], AX.X, ALU.max),
                waits=[tq])
            nmst["nrm"][par] = tk
            tl1 = so(lambda e: e.activation(cch[0:1, cb + 2:cb + 3],
                                            cch[0:1, cb + 0:cb + 1], AF.Ln), waits=[tq])
            tl2 = so(lambda e: e.activation(cch[0:1, cb + 3:cb + 4],
                                            cch[0:1, cb + 1:cb + 2], AF.Ln), waits=[tk])
            tadd = vo(lambda e: e.tensor_tensor(cch[0:1, cb + 4:cb + 5],
                                                cch[0:1, cb + 2:cb + 3],
                                                cch[0:1, cb + 3:cb + 4], ALU.add),
                      waits=[tl1, tl2])
            tC = so(lambda e: e.activation(cch[0:1, cb + 5:cb + 6],
                                           cch[0:1, cb + 4:cb + 5], AF.Exp, scale=0.5),
                    waits=[tadd])
            tneg = vo(lambda e: e.tensor_scalar(cch[0:1, cb + 6:cb + 7],
                                                cch[0:1, cb + 5:cb + 6], -1.0, None,
                                                ALU.mult),
                      waits=[tC])
            trow = vo(lambda e, p=par: e.tensor_scalar(
                qstg2[p][64:65, 0:NOWN], kstg2[p][64:65, 0:NOWN],
                cch[0:1, cb + 6:cb + 7], None, ALU.mult),
                waits=[tneg, tkon[par], hargs.get(("stgfree", pairt % 2))])
            nmst["cch"][par] = trow
            hargs[("tnm", H0 + par)] = trow
            yield

    def gB(H):
        par = H % 2
        pairt, odd = H // 2, H % 2
        qstg, kstg = qstg4[pairt % 2][par], kstg4[pairt % 2][par]
        vb = vb2[pairt % 2]
        tnm = hargs.pop(("tnm", H))
        tokv = hargs.get(("v", pairt))
        tmu = gstate["tmu_prev"]
        for qg in range(4):
            pvb = qg % 2
            pend = []
            tpv = None
            for kc in range(32):
                stb = kc % 4
                sbk = (4, 5, 2, 3)[stb]
                tmm = mm(bank(sbk), kstg[0:65, 128 * kc:128 * kc + 128],
                         qstg[0:65, 512 * qg:512 * qg + 512],
                         start=True, stop=True,
                         waits=[tnm, tokv, gstate["st"][stb]] +
                               ([gstate["pv"][pvb]] if kc == 0 else []))
                texp = so(lambda e, b=sbk, p=kc % 5: e.activation(
                    ptb[:, 512 * p:512 * p + 512], bank(b), AF.Exp),
                    waits=[tmm, gstate["pt"][kc % 5]])
                gstate["st"][stb] = texp
                pend.append((kc, texp))
                while len(pend) > 4:
                    k0, te0 = pend.pop(0)
                    tpv = mm(bank(6 + pvb, 0, 65, 0, 512),
                             vb[:, 132 * k0 + 66 * odd:132 * k0 + 66 * odd + 65],
                             ptb[:, 512 * (k0 % 5):512 * (k0 % 5) + 512],
                             start=(k0 == 0), stop=False, waits=[te0])
                    gstate["pt"][k0 % 5] = tpv
                if kc % 2 == 1:
                    yield
            for (k0, te0) in pend:
                tpv = mm(bank(6 + pvb, 0, 65, 0, 512),
                         vb[:, 132 * k0 + 66 * odd:132 * k0 + 66 * odd + 65],
                         ptb[:, 512 * (k0 % 5):512 * (k0 % 5) + 512],
                         start=(k0 == 0), stop=(k0 == 31), waits=[te0])
                gstate["pt"][k0 % 5] = tpv
            tl1 = so(lambda e, b=pvb: e.activation(lnG[:], bank(6 + b, 64, 65, 0, 512), AF.Ln),
                     waits=[tpv])
            tl2 = so(lambda e: e.activation(recG[:], lnG[:], AF.Exp, scale=-1.0),
                     waits=[tl1, gstate["rep"]])
            rb = 6 + (1 - pvb)
            trep = mm(bank(rb, 0, 64, 0, 512), c_1f[0:1, 0:64], recG[:],
                      start=True, stop=True, waits=[tl2, gstate["pv"][1 - pvb]])
            gstate["rep"] = trep
            tev = vo(lambda e, r=rb: e.tensor_copy(repsG[:], bank(r, 0, 64, 0, 512)),
                     waits=[trep, tmu])
            dst = attn_out if odd == 0 else oddst
            dc = (2048 * (pairt % 2) + 512 * qg) if odd == 0 else (512 * qg)
            tmu = vo(lambda e, b=pvb, d=dst, c=dc: e.tensor_mul(
                sb_ap(d, 0, 64, c, 512), bank(6 + b, 0, 64, 0, 512), repsG[:]),
                waits=[tev, sch.tok("dX") if qg == 0 else None])
            gstate["pv"][1 - pvb] = tev
            tbv = so(lambda e, d=dst, c=dc, hh=H: e.activation(
                sb_ap(d, 0, 64, c, 512), sb_ap(d, 0, 64, c, 512),
                AF.Identity, bias=qkvb[0:64, 32 + hh:33 + hh]), waits=[tmu])
            gstate["pv"][pvb] = tbv
            yield
        gstate["tmu_prev"] = tmu
        hargs[("stgfree", pairt % 2)] = sch.tok("sS")
        if odd:
            todd = dma("sync", "dX", sb_ap(attn_out, 64, 64, 2048 * (pairt % 2), 2048),
                       oddst[:], waits=[sch.tok("sS")])
            spill_tok[pairt % 2] = dma(
                "sync", "dX", spill_d[:, 2048 * pairt:2048 * pairt + 2048],
                attn_out[:, 2048 * (pairt % 2):2048 * (pairt % 2) + 2048],
                waits=[todd, sch.tok("sS")])
            gstate["vb"][pairt % 2] = sch.tok("sT")

    def run_interleaved(g1, g2, r1=1, r2=1):
        gens = [[g, r] for g, r in ((g1, r1), (g2, r2)) if g is not None]
        while gens:
            for item in list(gens):
                try:
                    for _ in range(item[1]):
                        next(item[0])
                except StopIteration:
                    gens.remove(item)

    def chainB(p):
        yield from gB(2 * p)
        yield from gB(2 * p + 1)

    bufs = {}
    prevB = None
    for p in range(8):
        run_interleaved(prevB, gA2(p), 3, 1)
        prevB = chainB(p)
    sfree("xnseq", "wqb", "wkb", "wvb", "sqscr", "nrmx", "cch")
    tok_g7 = [sch.tok("sT"), sch.tok("sV"), sch.tok("sS")]
    xnwin_pre = salloc("xnwin", [128, 8 * WIN], BF16)
    lc_state["xtm"] = salloc("xtm2a", [128, 1024], F32)
    lc_state["xntm"] = salloc("xntm2a", [128, 1024], F32)
    lc_state["xfree"] = list(tok_g7)
    lc_state["regfree"] = [None, None]
    lc_state["regpin"] = True

    def gen_lnwin():
        for i in range(28):
            ln_chunk("win", i, 32 + i, dst_bf=xnwin_pre, bfcol=128 * i, gate=tok_g7)
            yield
    run_interleaved(prevB, gen_lnwin(), 5, 1)
    lc_state["regpin"] = False
    bufs["xnwin"] = xnwin_pre
    tok_spill = spill_tok[1]
    sfree("attn_out", "oddst", "ptb", "repsG", "lnrecB",
          "qstg00", "qstg01", "qstg10", "qstg11", "kstg00", "kstg01", "kstg10", "kstg11",
          "vb0", "vb1", "qkvb", "Mb")


    def stream_rd_bf(t, w0, w, dstap_fn, waits):
        toks = []
        if w0 < SOWN:
            bb = min(w0 + w, SOWN)
            toks.append(vo(lambda e, a=w0, b=bb, t_=t, hh=bufs["halo"]: e.tensor_copy(
                dstap_fn(0, b - a),
                hh[:, NHALO * t_ + (a - X1AA):NHALO * t_ + (b - X1AA)]), waits=waits))
        if w0 + w > SOWN:
            aa = max(w0, SOWN)
            toks.append(so(lambda e, a=aa, n=w0 + w - aa, o=aa - w0, t_=t, ss=bufs["sown"]: e.activation(
                dstap_fn(o, n),
                ss[:, NSOWN * t_ + (a - SOWN):NSOWN * t_ + (a - SOWN) + n], AF.Copy),
                waits=waits))
        return toks

    def stream_slice(t, w0, w):
        if w0 < SOWN:
            assert w0 + w <= SOWN
            return bufs["halo"][:, NHALO * t + (w0 - X1AA):NHALO * t + (w0 - X1AA) + w]
        return bufs["sown"][:, NSOWN * t + (w0 - SOWN):NSOWN * t + (w0 - SOWN) + w]

    def ffn_layer(l, tok_prev):
        kv0, q0 = KV[l], Q0[l]
        nkv, nq, nqc, nkvc = NKV[l], NQ[l], NQC[l], NKVC[l]
        penlim = (PADTOP - kv0) // 128

        if l == 0:
            xnwin = bufs["xnwin"]
        else:
            xnwin = None
        wqkvb = salloc("wqkvb", [128, 8 * 1536], BF16)
        for kt in range(8):
            dma("gpsimd", "dW", wqkvb[:, 1536 * kt:1536 * kt + 1536],
                bass.AP(wqkv_d.tensor, l * D * 1536 + 128 * kt * 1536, [[1536, 128], [1, 1536]]),
                waits=tok_prev)
        twq = sch.tok("dW")
        qf = salloc("qf", [128, 4 * nq], BF16)
        kf = salloc("kf", [128, 4 * nkv], BF16)
        vbf = salloc("vbf", [128, nkvc * 520], BF16)
        if l == 1:
            xq2s = [salloc("xq2a", [128, 8 * 513], BF16),
                    salloc("xq2b", [128, 8 * 513], BF16)]

        pfree = [None, None, None, None]
        pbi = 0
        tmemset = vo(lambda e, n=nkvc: e.memset(
            bass.AP(vbf, 64, [[vbf.shape[1], 128], [520, n], [65, 8]]), 1.0),
            waits=tok_prev)
        xqtoks = []
        xq_mm_slot = [None, None]
        for cb, (c0, w) in enumerate(_blocks(nkv)):
            xq_mm_prev = xq_mm_slot[cb % 2]
            if l == 0:
                def rhs(kt, cc=c0, ww=w):
                    off = kv0 + cc - (1 if kt >= 4 else 0)
                    return xnwin[:, WIN * kt + off:WIN * kt + off + ww]
                rtoks = list(tok_prev)
            else:
                xq2 = xq2s[cb % 2]
                xqtoks = []
                for t in range(8):
                    xqtoks += stream_rd_bf(
                        t, kv0 + c0 - 1, w + 1,
                        (lambda o, n, t_=t, x2=xq2: x2[:, 513 * t_ + o:513 * t_ + o + n]),
                        waits=tok_prev + [xq_mm_prev])

                def rhs(kt, ww=w, x2=xq2):
                    off = 1 if kt < 4 else 0
                    return x2[:, 513 * kt + off:513 * kt + off + ww]
                rtoks = list(xqtoks)
            for t in range(4):
                pb = pbi % 4
                pbi += 1
                tmm = None
                for kt in range(8):
                    tmm = mm(bank(pb, 0, 128, 0, w),
                             wqkvb[:, 1536 * kt + 512 + 128 * t:1536 * kt + 512 + 128 * t + 128],
                             rhs(kt), start=(kt == 0), stop=(kt == 7),
                             waits=([twq, pfree[pb]] + rtoks) if kt == 0 else [])
                ev = so(lambda e, b=pb, t_=t, c=c0, ww=w: e.activation(
                    kf[:, nkv * t_ + c:nkv * t_ + c + ww], bank(b, 0, 128, 0, ww), AF.Copy),
                    waits=[tmm])
                pfree[pb] = ev
            if c0 >= 512:
                for t in range(4):
                    pb = pbi % 4
                    pbi += 1
                    tmm = None
                    for kt in range(8):
                        tmm = mm(bank(pb, 0, 128, 0, w),
                                 wqkvb[:, 1536 * kt + 128 * t:1536 * kt + 128 * t + 128],
                                 rhs(kt), start=(kt == 0), stop=(kt == 7),
                                 waits=([pfree[pb]] + rtoks) if kt == 0 else [])
                    ev = so(lambda e, b=pb, t_=t, c=c0 - 512, ww=w: e.activation(
                        qf[:, nq * t_ + c:nq * t_ + c + ww], bank(b, 0, 128, 0, ww), AF.Copy),
                        waits=[tmm])
                    pfree[pb] = ev
            for tci in range(w // 128):
                tc = c0 // 128 + tci
                pb = pbi % 4
                pbi += 1
                tmm = None
                for kt in range(8):
                    tmm = mm(bank(pb, 0, 128, 0, 512),
                             rhs(kt)[:, 128 * tci:128 * tci + 128],
                             wqkvb[:, 1536 * kt + 1024:1536 * kt + 1536],
                             start=(kt == 0), stop=(kt == 7),
                             waits=([pfree[pb], tmemset] + rtoks) if kt == 0 else [])
                ev = so(lambda e, b=pb, t=tc: e.activation(
                    bass.AP(vbf, 520 * t, [[vbf.shape[1], 128], [65, 8], [1, 64]]),
                    bass.AP(ps_t, 512 * b, [[4096, 128], [64, 8], [1, 64]]), AF.Copy),
                    waits=[tmm])
                pfree[pb] = ev
            xq_mm_slot[cb % 2] = sch.tok("sT")
        if l == 0:
            halo = bufs["halo"] = salloc("halo", [128, 8 * NHALO], BF16, high=True)
            for t in range(8):
                vo(lambda e, t_=t: e.tensor_copy(
                    halo[:, NHALO * t_:NHALO * t_ + NHALO],
                    xnwin[:, WIN * t_ + X1AA:WIN * t_ + SOWN]), waits=[sch.tok("sS")])
            sfree("xnwin")
        tok_qkv = [sch.tok("sS"), sch.tok("sV")]
        if l == 1:
            sfree("xq2a", "xq2b")
        sfree("wqkvb")

        rsnq = salloc("rsnq", [8, nq], BF16)
        rsnk = salloc("rsnk", [8, nkv], BF16)
        sq2 = salloc("sq2", [128, 2 * 512], BF16)
        m2s = salloc("m2s", [8, 512], F32)
        vmax_prev = [None, None]
        m2s_prev = None
        gcb = 0
        for (src, dst, ntot) in ((qf, rsnq, nq), (kf, rsnk, nkv)):
            for cb, (c0, w) in enumerate(_blocks(ntot)):
                lb = 2 + gcb % 2
                gcb += 1
                tmm = None
                sq_toks = [None, None]
                for t in range(4):
                    sb = t % 2
                    tq = so(lambda e, s=sb, t_=t, c=c0, ww=w, sr=src, nt=ntot: e.activation(
                        sq2[:, 512 * s:512 * s + ww], sr[:, nt * t_ + c:nt * t_ + c + ww],
                        AF.Square), waits=tok_qkv + [sq_toks[sb]])
                    tmm = mm(bank(lb, 0, 8, 0, w), c_l2o[:, 8 * t:8 * t + 8],
                             sq2[:, 512 * sb:512 * sb + w], start=(t == 0), stop=(t == 3),
                             waits=[tq] + ([vmax_prev[lb - 2]] if t == 0 else []))
                    sq_toks[sb] = tmm
                tv = vo(lambda e, b=lb, ww=w: e.tensor_scalar_max(
                    m2s[:, 0:ww], bank(b, 0, 8, 0, ww), 1e-24),
                    waits=[tmm, m2s_prev])
                vmax_prev[lb - 2] = tv
                ts1 = so(lambda e, ww=w: e.activation(m2s[:, 0:ww], m2s[:, 0:ww], AF.Ln), waits=[tv])
                m2s_prev = so(lambda e, d=dst, c=c0, ww=w: e.activation(
                    d[:, c:c + ww], m2s[:, 0:ww], AF.Exp, scale=-0.5), waits=[ts1])
        tok_rsn = [sch.tok("sS"), sch.tok("sV")]
        sfree("sq2", "m2s")

        swapb = salloc("swapb", [128, 4 * nkv], BF16)
        tb = salloc("tb", [128, 4 * 512], F32)
        sc1 = salloc("sc1", [128, 512], BF16)
        sc2 = salloc("sc2", [128, 512], BF16)
        rep_prev = [None, None]
        grt = 0
        for (buf, rsn, ntot, ti, qcol, woff) in (
                (qf, rsnq, nq, 0, 4 * l, q0), (kf, rsnk, nkv, 2, 4 * l + 2, kv0)):
            tok_sw = []
            swgate = [sch.tok("sV")]
            for t in range(4):
                for a in range(2):
                    for bb2 in range(2):
                        tok_sw.append(dma(
                            "sync", "dU",
                            sb_ap(swapb, 64 * a + 32 * (1 - bb2), 32, ntot * t, ntot),
                            sb_ap(buf, 64 * a + 32 * bb2, 32, ntot * t, ntot),
                            waits=tok_qkv + tok_rsn + swgate))
            tok_sw = [tok_sw[-1]]
            BART = [sch.tok("sT")] + tok_rsn
            tbfree = [None, None]
            for cb, (c0, w) in enumerate(_blocks(ntot)):
                rot = cb % 2
                dma("gpsimd", f"dT{rot}", tb[:, 1024 * rot:1024 * rot + w],
                    tab_d[ti][:, woff + c0:woff + c0 + w], waits=[tbfree[rot]] + BART)
                tt2 = dma("gpsimd", f"dT{rot}", tb[:, 1024 * rot + 512:1024 * rot + 512 + w],
                          tab_d[ti + 1][:, woff + c0:woff + c0 + w], waits=[])
                tt1 = tt2
                for t in range(4):
                    ta = vo(lambda e, t_=t, c=c0, ww=w, r=rot, b=buf, nt=ntot, qc=qcol:
                            e.scalar_tensor_tensor(
                                sc1[:, 0:ww], b[:, nt * t_ + c:nt * t_ + c + ww],
                                qspp[:, qc:qc + 1], tb[:, 1024 * r:1024 * r + ww],
                                ALU.mult, ALU.mult),
                            waits=[tt1] + tok_sw)
                    tbm = vo(lambda e, t_=t, c=c0, ww=w, r=rot, nt=ntot, qc=qcol:
                             e.scalar_tensor_tensor(
                                 sc2[:, 0:ww], swapb[:, nt * t_ + c:nt * t_ + c + ww],
                                 qspp[:, qc + 1:qc + 2], tb[:, 1024 * r + 512:1024 * r + 512 + ww],
                                 ALU.mult, ALU.mult),
                             waits=[tt2] + tok_sw)
                    tadd = vo(lambda e, t_=t, c=c0, ww=w, b=buf, nt=ntot: e.tensor_add(
                        b[:, nt * t_ + c:nt * t_ + c + ww], sc1[:, 0:ww], sc2[:, 0:ww]),
                        waits=[ta, tbm])
                    rb3 = 2 + grt % 2
                    grt += 1
                    trep = mm(bank(rb3, 0, 128, 0, w), c_rep8[0:8, 128 * t:128 * t + 128],
                              rsn[0:8, c0:c0 + w], start=True, stop=True,
                              waits=[rep_prev[rb3 - 2], tadd] + tok_rsn)
                    tmul = vo(lambda e, t_=t, c=c0, ww=w, b=buf, r=rb3, nt=ntot: e.tensor_mul(
                        b[:, nt * t_ + c:nt * t_ + c + ww],
                        b[:, nt * t_ + c:nt * t_ + c + ww], bank(r, 0, 128, 0, ww)),
                        waits=[trep])
                    rep_prev[rb3 - 2] = tmul
                tbfree[rot] = sch.tok("sV")
        tok_rot = sch.tok("sV")
        sfree("swapb", "tb", "sc1", "sc2", "rsnq", "rsnk")

        p1f_init = [None, None]

        o_fm = salloc("o_fm", [128, 4 * nq], BF16)
        odd2 = salloc("odd2", [64, nq], BF16)
        unno = salloc("unno", [65, 512], F32)
        recf2 = salloc("recf2", [1, 512], F32)
        qsg2 = [salloc(f"qsg{i}", [65, nq], BF16) for i in range(2)]
        ksg2 = [salloc(f"ksg{i}", [65, nkv], BF16) for i in range(2)]
        negM2s = salloc("negM2s", [1, nq], BF16)
        nMc2f = [salloc(f"nMcGf{i}", [128, 1], F32) for i in range(2)]
        negMc2f = [salloc(f"nMcLf{i}", [128, 1], F32) for i in range(2)]
        PW = (penlim - 1) * 128 + 640
        c_pen = salloc("c_pen", [1, PW], BF16)
        tok_pen = dma("gpsimd", "dW", c_pen[:], padpen_d[0:1, kv0:kv0 + PW],
                      waits=[sch.tok("sT")])
        pt2 = salloc("pt2", [128, 5 * 640], BF16)

        fst = {"p1": [p1f_init[0]], "st": [None, None], "pt": [None, None, None, None, None],
               "pv": [None, None], "rep": None, "mu": None}
        fargs = {}

        def fgA(h):
            par = h % 2
            pairt, odd = h // 2, h % 2
            qsg, ksg, negM2 = qsg2[par], ksg2[par], negM2s
            negMc_, negMcL = nMc2f[par], negMc2f[par]
            tq1 = dma("sync", "dU", qsg[0:64, :], sb_ap(qf, 64 * odd, 64, nq * pairt, nq),
                      waits=[tok_rot, fargs.get(("fstg", par))])
            tu2 = dma("gpsimd", "dUg" if par == 0 else "dUh", ksg[0:64, :],
                      sb_ap(kf, 64 * odd, 64, nkv * pairt, nkv),
                      waits=[tok_rot, fargs.get(("fstg", par))])
            tko = vo(lambda e, ks=ksg: e.memset(ks[64:65, :], 1.0), waits=[tu2])
            pend = None
            tneg = None
            for qc in range(nqc):
                tmm = mm(bank(0, 0, 128, 0, 512),
                         qsg[0:64, 128 * qc:128 * qc + 128],
                         ksg[0:64, 128 * qc:128 * qc + 512],
                         start=True, stop=(qc >= penlim),
                         waits=[tq1, tu2, tko, fst["p1"][0]])
                tmm = mm(bank(1, 0, 128, 0, 128),
                         qsg[0:64, 128 * qc:128 * qc + 128],
                         ksg[0:64, 128 * qc + 512:128 * qc + 640],
                         start=True, stop=(qc >= penlim), waits=[])
                if qc < penlim:
                    mm(bank(0, 0, 128, 0, 512), c_onesr[0:1, 0:128],
                       c_pen[0:1, 128 * qc:128 * qc + 512],
                       start=False, stop=True, waits=[tok_pen])
                    tmm = mm(bank(1, 0, 128, 0, 128), c_onesr[0:1, 0:128],
                             c_pen[0:1, 128 * qc + 512:128 * qc + 640],
                             start=False, stop=True, waits=[])
                tred = vo(lambda e, n=negMc_: e.tensor_reduce(
                    n[:], pscols(0, 640), AX.X, ALU.max, negate=True),
                    waits=[tmm, nmst.get("ftrp"), pend[1] if pend else None])
                fst["p1"][0] = tred
                if pend is not None:
                    q0, tn0 = pend
                    ttr = mm(bank(0, 0, 1, 0, 128), negMcL[:], c_iden[:],
                             start=True, stop=True, waits=[tn0, fst["p1"][0]])
                    nmst["ftrp"] = ttr
                    trev = so(lambda e, q=q0, nf=negM2: e.activation(
                        nf[0:1, 128 * q:128 * q + 128], bank(0, 0, 1, 0, 128), AF.Copy),
                        waits=[ttr, nmst.get("ftnmp")])
                    fst["p1"][0] = trev
                tcp = vo(lambda e, a=negMcL, b=negMc_: e.tensor_copy(a[:], b[:]),
                         waits=[tred, nmst.get("ftrp")])
                pend = (qc, tcp)
                yield
            q0, tn0 = pend
            ttr = mm(bank(0, 0, 1, 0, 128), negMc2f[par][:], c_iden[:],
                     start=True, stop=True, waits=[tn0, fst["p1"][0]])
            nmst["ftrp"] = ttr
            trev = so(lambda e, q=q0, nf=negM2: e.activation(
                nf[0:1, 128 * q:128 * q + 128], bank(0, 0, 1, 0, 128), AF.Copy),
                waits=[ttr, nmst.get("ftnmp")])
            fst["p1"][0] = trev
            tnm = dma("sync", "dSa" if par == 0 else "dSb", qsg[64:65, :], negM2[:], waits=[trev])
            nmst["ftnmp"] = tnm
            fargs[("tnm", h)] = tnm

        def fgB(h):
            par = h % 2
            pairt, odd = h // 2, h % 2
            qsg, ksg = qsg2[par], ksg2[par]
            tnm = fargs.pop(("tnm", h))
            pend = []
            for qc in range(nqc + 4):
                if qc < nqc:
                    stb = qc % 2
                    rb = 2 + 2 * stb
                    tmm = None
                    for jj in range(5):
                        edge = jj in (0, 4)
                        outap = (bank(rb, 0, 128, 128 * jj, 128) if jj < 4
                                 else bank(rb + 1, 0, 128, 0, 128))
                        tmm = mm(outap, ksg[0:65, 128 * (qc + jj):128 * (qc + jj) + 128],
                                 qsg[0:65, 128 * qc:128 * qc + 128],
                                 start=True, stop=(qc + jj >= penlim) and not edge,
                                 waits=([tnm, fst["st"][stb]] if jj == 0 else []))
                        if qc + jj < penlim:
                            tmm = mm(outap,
                                     c_pen[0:1, 128 * (qc + jj):128 * (qc + jj) + 128],
                                     c_onesr[0:1, 0:128], start=False, stop=not edge, waits=[])
                        if edge:
                            tmm = mm(outap, c_idenb[:],
                                     (c_maskf if jj == 0 else c_maskl)[:],
                                     start=False, stop=True, waits=[])
                    te2 = so(lambda e, a=rb, p=qc % 5: e.activation(
                        pt2[:, 640 * p:640 * p + 640], pscols(512 * a, 640), AF.Exp),
                        waits=[tmm, fst["pt"][qc % 5]])
                    fst["st"][stb] = te2
                    pend.append((qc, te2))
                if len(pend) > 4 or (qc >= nqc and pend):
                    q0, te0 = pend.pop(0)
                    pvb = q0 % 2
                    tpv = None
                    for jj in range(5):
                        tc = q0 + jj
                        tpv = mm(bank(6 + pvb, 0, 65, 0, 128),
                                 vbf[:, 520 * tc + 65 * h:520 * tc + 65 * h + 65],
                                 pt2[:, 640 * (q0 % 5) + 128 * jj:640 * (q0 % 5) + 128 * jj + 128],
                                 start=(jj == 0), stop=(jj == 4),
                                 waits=([te0] + ([fst["pv"][pvb]] if jj == 0 else [])))
                    fst["pt"][q0 % 5] = tpv
                    tevu = vo(lambda e, q=q0, b=pvb: e.tensor_copy(
                        unno[:, 128 * (q % 4):128 * (q % 4) + 128], bank(6 + b, 0, 65, 0, 128)),
                        waits=[tpv, fst["mu"]])
                    fst["pv"][pvb] = tevu
                    if q0 % 4 == 3 or q0 == nqc - 1:
                        w0 = 128 * (q0 % 4) + 128
                        c0 = 128 * (q0 - q0 % 4)
                        tall = sch.tok("sV")
                        tl1 = so(lambda e, ww=w0: e.activation(
                            recf2[0:1, 0:ww], sb_ap(unno, 64, 1, 0, ww), AF.Ln),
                            waits=[tall, fst["rep"]])
                        tl2 = so(lambda e, ww=w0: e.activation(
                            recf2[0:1, 0:ww], recf2[0:1, 0:ww],
                            AF.Exp, scale=-1.0), waits=[tl1])
                        rb2 = 6 + (q0 % 2)
                        trep = mm(bank(rb2, 0, 64, 0, w0), c_1f[0:1, 0:64],
                                  recf2[0:1, 0:w0],
                                  start=True, stop=True, waits=[tl2, fst["rep"]])
                        fst["rep"] = trep
                        dst = o_fm if odd == 0 else odd2
                        dc = (nq * pairt + c0) if odd == 0 else c0
                        tmu = vo(lambda e, d=dst, c=dc, r=rb2, ww=w0: e.tensor_mul(
                            sb_ap(d, 0, 64, c, ww), unno[0:64, 0:ww], bank(r, 0, 64, 0, ww)),
                            waits=[trep, fargs.get(("hd", h - 2)) if (odd and c0 == 0) else None])
                        fst["mu"] = tmu
                        fst["pv"][q0 % 2] = tmu
                yield
            fst["st"][0] = sch.tok("sV")
            if odd:
                fargs[("hd", h)] = dma("sync", "dX", sb_ap(o_fm, 64, 64, nq * pairt, nq),
                                       odd2[:], waits=[sch.tok("sV")])
            fargs[("fstg", par)] = sch.tok("sV")

        prevB = None
        for h in range(8):
            run_interleaved(prevB, fgA(h))
            prevB = fgB(h)
        run_interleaved(prevB, None)
        tok_attn = [sch.tok("sV"), sch.tok("sS"), sch.tok("sT"), sch.tok("dX")]
        sfree("qsg0", "qsg1", "ksg0", "ksg1", "negM2s", "pt2", "unno", "recf2",
              "odd2", "qf", "kf", "vbf", "c_pen",
              "nMcGf0", "nMcGf1", "nMcLf0", "nMcLf1")
        if l == 0:
            sown = bufs["sown"] = salloc("sown", [128, 8 * NSOWN], F32, high=True)
            gate = [sch.tok("sT"), sch.tok("sV"), sch.tok("sS"), sch.tok("dX")]
            lc_state["regfree"] = [None, None]
            lc_state["xfree"] = list(gate)
            for i in range(11, 28):
                ln_chunk("win", i, 32 + i, dst_f32=sown, f32col=128 * (i - 11), gate=gate)
            tok_stream = sch.all_toks()
            sfree("xtm2a", "xntm2a", "st_rs", "st_nb")
        else:
            tok_stream = tok_prev

        wof = salloc("wof", [128, 4 * 1024], BF16)
        for kt in range(4):
            dma("gpsimd", "dW", wof[:, 1024 * kt:1024 * kt + 1024],
                bass.AP(wof_d.tensor, l * 512 * D + 128 * kt * D, [[D, 128], [1, D]]),
                waits=tok_attn)
        twof = sch.tok("dW")
        wfree = [None, None, None, None]
        for (b0, w) in WO_BLKS[l]:
            for dmc in range(8):
                pb = dmc % 4
                tmm = None
                for kt in range(4):
                    tmm = mm(bank(pb, 0, 128, 0, w),
                             wof[:, 1024 * kt + 128 * dmc:1024 * kt + 128 * dmc + 128],
                             o_fm[:, nq * kt + (b0 - q0):nq * kt + (b0 - q0) + w],
                             start=(kt == 0), stop=(kt == 3),
                             waits=([twof, wfree[pb]] + tok_attn + list(tok_stream))
                             if kt == 0 else [])
                sl = stream_slice(dmc, b0, w)
                tadd = vo(lambda e, b=pb, s=sl, ww=w: e.tensor_add(
                    s, bank(b, 0, 128, 0, ww), s), waits=[tmm])
                wfree[pb] = tadd
        if l == 1:
            EDGE2 = 1536 - 1 - SOWN
            for t in range(8):
                vo(lambda e, t_=t: e.tensor_scalar_mul(
                    bufs["sown"][:, NSOWN * t_ + EDGE2:NSOWN * t_ + EDGE2 + 1],
                    bufs["sown"][:, NSOWN * t_ + EDGE2:NSOWN * t_ + EDGE2 + 1],
                    padedge[:, 0:1]), waits=[sch.tok("sV"), wfree[0], wfree[1]])
        tok_wo = [sch.tok("sV")]
        sfree("wof", "o_fm")

        w1b = salloc("w1b", [128, 8 * 5460], BF16)
        for kt in range(8):
            dma("gpsimd", "dW", w1b[:, 5460 * kt:5460 * kt + 5460],
                bass.AP(w1_d.tensor, l * D * 2 * HID + 128 * kt * 2 * HID, [[2 * HID, 128], [1, 2 * HID]]),
                waits=tok_attn + tok_wo)
        tw1 = sch.tok("dW")
        if l == 0:
            for lc in range(2):
                for dmc in range(8):
                    base = (lc * 8 + dmc) * 128 * 2816
                    dma("gpsimd", "dWc",
                        bass.AP(w2bf_d.tensor, base, [[2816, 128], [128, 21], [1, 128]]),
                        bass.AP(w2_d.tensor, lc * HID * D + 128 * dmc,
                                [[D, 128], [128 * D, 21], [1, 128]]), waits=[])
                    dma("gpsimd", "dWc",
                        bass.AP(w2bf_d.tensor, base + 2688, [[2816, 42], [1, 128]]),
                        bass.AP(w2_d.tensor, lc * HID * D + 2688 * D + 128 * dmc,
                                [[D, 42], [1, 128]]), waits=[])
            bufs["w2conv"] = sch.tok("dWc")
        w2s = salloc("w2s", [128, 3 * 2816], BF16)
        hbD = [salloc("hb0", [128, 8 * MLPW], BF16),
               salloc("hb1", [128, 8 * MLPW], BF16)]
        lnoD = hbD
        hidA = salloc("hidbA", [128, 11 * MLPW], BF16)
        hidB = salloc("hidbB", [128, 11 * MLPW], BF16)
        def hid_sl(hc, p1, c0, w):
            t = hidA if hc < 11 else hidB
            c = MLPW * (hc if hc < 11 else hc - 11)
            return t[0:p1, c + c0:c + c0 + w]
        sqh = salloc("sqh", [128, MLPW], BF16)
        statw = salloc("statw", [33, 2 * MLPW], F32)
        mu_r = statw[0:1, 0:MLPW]
        srow = statw[0:1, MLPW:2 * MLPW]
        ms_r = statw[32:33, 0:MLPW]
        vscr = statw[32:33, MLPW:2 * MLPW]
        sv = salloc("sv", [128, 8], F32)

        w2free = [tok_wo[0], tok_wo[0], tok_wo[0]]
        w2bank = [None, None]
        st = {"agfree": [None, None], "hbfree": [[], []], "svfree": [],
              "r1free": None, "r2free": None, "sqfree": None, "gglfree": None,
              "lnofree": [[], []], "prev_w": None, "lno_tok": {}, "sv_tok": {}}

        def gen_prep(bi, b0, w):
            par = bi % 2
            hb = hbD[par]
            lno = hb
            th = []
            for t in range(8):
                hbw = st["hbfree"][par] + st["lnofree"][par]
                if t < 4:
                    th += stream_rd_bf(t, b0, w,
                                       (lambda o, n, t_=t: hb[:, MLPW * t_ + o:MLPW * t_ + o + n]),
                                       waits=tok_wo + hbw)
                else:
                    if bi == 0:
                        th += stream_rd_bf(t, b0 - 1, w,
                                           (lambda o, n, t_=t: hb[:, MLPW * t_ + o:MLPW * t_ + o + n]),
                                           waits=tok_wo + hbw)
                    else:
                        th.append(vo(lambda e, t_=t: e.tensor_copy(
                            hb[:, MLPW * t_:MLPW * t_ + 1], sv[:, t_ - 4:t_ - 3]),
                            waits=st["svfree"] + hbw))
                        th += stream_rd_bf(t, b0, w - 1,
                                           (lambda o, n, t_=t: hb[:, MLPW * t_ + 1 + o:MLPW * t_ + 1 + o + n]),
                                           waits=tok_wo + hbw)
                if t % 2 == 1:
                    yield
            svfree = []
            svg = [sch.tok("sV")]
            for t in range(4, 8):
                svfree.append(vo(lambda e, t_=t, c=b0 + w - 1: e.tensor_copy(
                    sv[:, t_ - 4:t_ - 3], stream_slice(t_, c, 1)),
                    waits=tok_wo + [st["prev_w"]] + svg))
            st["svfree"] = svfree
            st["sv_tok"][bi] = svfree
            yield
            tmm_mu = tmm_ms = None
            for t in range(8):
                tsq = so(lambda e, t_=t, ww=w: e.activation(
                    sqh[:, 0:ww], hb[:, MLPW * t_:MLPW * t_ + ww], AF.Square),
                    waits=th + [st["sqfree"]])
                tmm_mu = mm(bank(4, 0, 1, 0, w), c_onesc[:], hb[:, MLPW * t:MLPW * t + w],
                            start=(t == 0), stop=(t == 7),
                            waits=(th + [st["r1free"]]) if t == 0 else [])
                tmm_ms = mm(bank(5, 0, 1, 0, w), c_onesc[:], sqh[:, 0:w],
                            start=(t == 0), stop=(t == 7),
                            waits=[tsq] + ([st["r2free"]] if t == 0 else []))
                st["sqfree"] = tmm_ms
                if t % 2 == 1:
                    yield
            tev = so(lambda e, ww=w: e.activation(mu_r[0:1, 0:ww], bank(4, 0, 1, 0, ww),
                                                  AF.Copy, scale=1.0 / 1024), waits=[tmm_mu])
            tev2 = so(lambda e, ww=w: e.activation(ms_r[0:1, 0:ww], bank(5, 0, 1, 0, ww),
                                                   AF.Copy, scale=1.0 / 1024), waits=[tmm_ms])
            tmu2 = so(lambda e, ww=w: e.activation(vscr[0:1, 0:ww], mu_r[0:1, 0:ww], AF.Square),
                      waits=[tev])
            tvar = vo(lambda e, ww=w: e.tensor_sub(vscr[0:1, 0:ww], ms_r[0:1, 0:ww],
                                                   vscr[0:1, 0:ww]), waits=[tmu2, tev2])
            tln = so(lambda e, ww=w: e.activation(vscr[0:1, 0:ww], vscr[0:1, 0:ww], AF.Ln,
                                                  bias=1.0e-5), waits=[tvar])
            texp = so(lambda e, ww=w: e.activation(srow[0:1, 0:ww], vscr[0:1, 0:ww], AF.Exp,
                                                   scale=-0.5), waits=[tln])
            yield
            tr1 = mm(bank(4, 0, 128, 0, w), c_1f[0:1, :], srow[0:1, 0:w],
                     start=True, stop=True, waits=[texp, st["r1free"]])
            tr2 = mm(bank(5, 0, 128, 0, w), c_1f[0:1, :], mu_r[0:1, 0:w],
                     start=True, stop=True, waits=[tev, st["r2free"]])
            for t in range(8):
                tv1 = vo(lambda e, t_=t, ww=w: e.scalar_tensor_tensor(
                    lno[:, MLPW * t_:MLPW * t_ + ww], hb[:, MLPW * t_:MLPW * t_ + ww], 1.0,
                    bank(5, 0, 128, 0, ww), ALU.mult, ALU.subtract),
                    waits=[tr2])
                st["r2free"] = tv1
                lastadd = vo(lambda e, t_=t, ww=w: e.scalar_tensor_tensor(
                    lno[:, MLPW * t_:MLPW * t_ + ww], lno[:, MLPW * t_:MLPW * t_ + ww], 1.0,
                    bank(4, 0, 128, 0, ww), ALU.mult, ALU.mult), waits=[tr1, tv1])
                st["r1free"] = lastadd
                so(lambda e, t_=t, ww=w: e.activation(
                    lno[:, MLPW * t_:MLPW * t_ + ww], lno[:, MLPW * t_:MLPW * t_ + ww],
                    AF.Identity, scale=lgpp[:, 8 * l + t_:8 * l + t_ + 1],
                    bias=lbpp[:, 8 * l + t_:8 * l + t_ + 1]), waits=[lastadd])
                if t % 2 == 1:
                    yield
            tok_lno = [sch.tok("sS"), sch.tok("sV")]
            st["lno_tok"][bi] = tok_lno
            st["hbfree"][par] = list(tok_lno)

        def gen_mlp(bi, b0, w):
            par = bi % 2
            lno = lnoD[par]
            tok_lno = st["lno_tok"].pop(bi)
            svfree_bi = st["sv_tok"].pop(bi)
            pendh = None
            for hc in range(NHC):
                whc = _hc_w(hc)
                ab, gb = (6, 7) if hc % 2 == 0 else (2, 3)
                ta = tg = None
                for kt in range(8):
                    ta = mm(bank(ab, 0, whc, 0, w),
                            w1b[:, 5460 * kt + 128 * hc:5460 * kt + 128 * hc + whc],
                            lno[:, MLPW * kt:MLPW * kt + w], start=(kt == 0), stop=(kt == 7),
                            waits=([tw1, st["agfree"][hc % 2]] + tok_lno if kt == 0 else []))
                    tg = mm(bank(gb, 0, whc, 0, w),
                            w1b[:, 5460 * kt + HID + 128 * hc:5460 * kt + HID + 128 * hc + whc],
                            lno[:, MLPW * kt:MLPW * kt + w], start=(kt == 0), stop=(kt == 7),
                            waits=[])
                if pendh is not None:
                    c0h, wh0, ab0, gb0, tg0, ta0 = pendh
                    tgg = so(lambda e, ww=w, wh=wh0, c=c0h, g=gb0: e.activation(
                        hid_sl(c, wh, 0, ww), bank(g, 0, wh, 0, ww), AF.Gelu,
                        bias=b1g[0:wh, 22 * l + c:22 * l + c + 1]),
                        waits=[tg0, st["gglfree"]])
                    gf = vo(lambda e, ww=w, wh=wh0, c=c0h, a=ab0: e.scalar_tensor_tensor(
                        hid_sl(c, wh, 0, ww), bank(a, 0, wh, 0, ww),
                        b1a[0:wh, 22 * l + c:22 * l + c + 1], hid_sl(c, wh, 0, ww),
                        ALU.add, ALU.mult), waits=[ta0, tgg])
                    st["gglfree"] = gf
                    st["agfree"][c0h % 2] = gf
                pendh = (hc, whc, ab, gb, tg, ta)
                yield
            c0h, wh0, ab0, gb0, tg0, ta0 = pendh
            tgg = so(lambda e, ww=w, wh=wh0, c=c0h, g=gb0: e.activation(
                hid_sl(c, wh, 0, ww), bank(g, 0, wh, 0, ww), AF.Gelu,
                bias=b1g[0:wh, 22 * l + c:22 * l + c + 1]), waits=[tg0, st["gglfree"]])
            gf = vo(lambda e, ww=w, wh=wh0, c=c0h, a=ab0: e.scalar_tensor_tensor(
                hid_sl(c, wh, 0, ww), bank(a, 0, wh, 0, ww),
                b1a[0:wh, 22 * l + c:22 * l + c + 1], hid_sl(c, wh, 0, ww),
                ALU.add, ALU.mult), waits=[ta0, tgg])
            st["gglfree"] = gf
            st["agfree"][c0h % 2] = gf
            st["lnofree"][par] = [sch.tok("sT")]
            tok_hid = sch.tok("sV")
            for dmc in range(8):
                slo = (8 * bi + dmc) % 3
                pbk = dmc % 2
                sname = ("dWa", "dWb", "dWd")[slo]
                dma("sync", sname,
                    bass.AP(w2s, 2816 * slo, [[3 * 2816, 128], [1, 2688]]),
                    bass.AP(w2bf_d.tensor, (l * 8 + dmc) * 128 * 2816,
                            [[2816, 128], [1, 2688]]),
                    waits=[w2free[slo], bufs["w2conv"]] +
                          (tok_wo if bi == 0 and dmc < 3 else []))
                tld2 = dma("sync", sname,
                           bass.AP(w2s, 2816 * slo + 2688, [[3 * 2816, 42], [1, 128]]),
                           bass.AP(w2bf_d.tensor, (l * 8 + dmc) * 128 * 2816 + 2688,
                                   [[2816, 42], [1, 128]]), waits=[])
                tld1 = tld2
                tmm = None
                for hc in range(NHC):
                    whc = _hc_w(hc)
                    tmm = mm(bank(pbk, 0, 128, 0, w),
                             w2s[0:whc, 2816 * slo + 128 * hc:2816 * slo + 128 * hc + 128],
                             hid_sl(hc, whc, 0, w),
                             start=(hc == 0), stop=(hc == NHC - 1),
                             waits=([tld1, tld2, tok_hid, w2bank[pbk]] if hc == 0 else []))
                sl2 = stream_slice(dmc, b0, w)
                tadd = vo(lambda e, b=pbk, s=sl2, ww=w, d=dmc: e.scalar_tensor_tensor(
                    s, bank(b, 0, 128, 0, ww), b2c[:, 8 * l + d:8 * l + d + 1], s,
                    ALU.add, ALU.add), waits=[tmm] + svfree_bi)
                w2free[slo] = tadd
                w2bank[pbk] = tadd
                yield
            st["prev_w"] = w2free[1] if w2free[1] else w2free[0]

        blks = MLP_BLKS[l]
        run_interleaved(gen_prep(0, *blks[0]), None)
        for bi, (b0, w) in enumerate(blks):
            nxt = gen_prep(bi + 1, *blks[bi + 1]) if bi + 1 < len(blks) else None
            run_interleaved(gen_mlp(bi, b0, w), nxt, 2, 1)
        tok_mlp = sch.all_toks()
        sfree("w1b", "w2s", "hb0", "hb1", "hidbA", "hidbB",
              "sqh", "statw", "sv")
        return tok_mlp

    toks = ffn_layer(0, [tok_spill] + sch.all_toks())
    EDGE = 1536 - 1 - SOWN
    for t in range(8):
        vo(lambda e, t_=t: e.tensor_scalar_mul(
            bufs["sown"][:, NSOWN * t_ + EDGE:NSOWN * t_ + EDGE + 1],
            bufs["sown"][:, NSOWN * t_ + EDGE:NSOWN * t_ + EDGE + 1],
            padedge[:, 0:1]), waits=toks[:])
    toks = sch.all_toks()
    toks = ffn_layer(1, toks)

    woab = salloc("woab", [128, 8 * 1024], BF16)
    bo_row = salloc("bo_row", [1, 1024], BF16)
    dma("gpsimd", "dW", bo_row[:], bo_d[None, :], waits=toks)
    for kt in range(8):
        dma("gpsimd", "dW", woab[:, 1024 * kt:1024 * kt + 1024],
            wo_a_d[128 * kt:128 * kt + 128, :], waits=toks)
    twoa = sch.tok("dW")
    attn2 = salloc("attn2", [128, 8 * NOWN], BF16)
    trl = None
    for t in range(8):
        trl = dma("gpsimd", "dS", attn2[:, 2048 * t:2048 * t + 2048],
                  spill_d[:, 2048 * t:2048 * t + 2048], waits=[tok_spill] + toks)
    xtm2 = salloc("xtm2", [128, 2 * 1024], F32)
    stg = salloc("stg", [128, 2 * 1024], F32)
    stg2 = salloc("stg2", [128, 2 * 1024], F32)
    tregf = [None, None]
    wregf = [None, None]
    stgf = [None, None]
    stg2f = [None, None]
    for c in range(16):
        rr = c % 2
        tl = dma("gpsimd", f"dT{rr}", xtm2[:, 1024 * rr:1024 * rr + 1024],
                 xseq[128 * c:128 * c + 128, :],
                 waits=[stgf[rr]] + (toks if c < 2 else []))
        tt = None
        for t in range(8):
            tt = tsp(pscols(1024 * rr + 128 * t, 128),
                     bufs["sown"][:, NSOWN * t + 128 + 128 * c:NSOWN * t + 128 + 128 * c + 128],
                     waits=toks + [tregf[rr]])
        tw = None
        for kt in range(8):
            for hh in range(2):
                tw = mm(pscols(2048 + 1024 * rr + 512 * hh, 512),
                        attn2[:, 2048 * kt + 128 * c:2048 * kt + 128 * c + 128],
                        woab[:, 1024 * kt + 512 * hh:1024 * kt + 512 * hh + 512],
                        start=(kt == 0), stop=False,
                        waits=([twoa, trl, wregf[rr]] if (kt == 0 and hh == 0) else []))
        for hh in range(2):
            tw = mm(pscols(2048 + 1024 * rr + 512 * hh, 512), c_onesr[0:1, 0:128],
                    bo_row[0:1, 512 * hh:512 * hh + 512], start=False, stop=True, waits=[])
        tv1 = vo(lambda e, r=rr: e.tensor_add(
            stg[:, 1024 * r:1024 * r + 1024], pscols(1024 * r, 1024),
            xtm2[:, 1024 * r:1024 * r + 1024]), waits=[tt, tl, stg2f[rr]])
        tregf[rr] = tv1
        tv2 = vo(lambda e, r=rr: e.tensor_add(
            stg2[:, 1024 * r:1024 * r + 1024], pscols(2048 + 1024 * r, 1024),
            stg[:, 1024 * r:1024 * r + 1024]), waits=[tw, tv1])
        wregf[rr] = tv2
        stg2f[rr] = tv2
        stgf[rr] = dma("sync", f"dO{rr}", out_d[128 * c:128 * c + 128, :],
                       stg2[:, 1024 * rr:1024 * rr + 1024], waits=[tv2])
    tok_out = (sch.tok("dO0"), sch.tok("dO1"))

    with nc.Block() as blk:
        def mkbody(name):
            def body(e):
                lastw = {}
                for waits, inc, incn, fn in sch.ops[name]:
                    for sem, val in waits:
                        if val > 0 and lastw.get(sem, -1) < val:
                            e.wait_ge(semh[sem], val)
                            lastw[sem] = val
                    ins = fn(e)
                    if inc is not None:
                        ins.then_inc(semh[inc], incn)
                if name == "sync":
                    e.wait_ge(semh["dO0"], tok_out[0][1])
                    e.wait_ge(semh["dO1"], tok_out[1][1])
            return body
        blk.sync(mkbody("sync"))
        blk.gpsimd(mkbody("gpsimd"))
        blk.tensor(mkbody("tensor"))
        blk.vector(mkbody("vector"))
        blk.scalar(mkbody("scalar"))
    if __name__ == "__main__":
        print({e: len(sch.ops[e]) for e in ENGS}, "sem:", sch.cnt,
              "sbuf free:", sorted(free), file=sys.stderr)
    return nc


def kernel(**inputs):
    x = np.ascontiguousarray(np.asarray(inputs["L0_attn_x"], dtype=np.float32))
    if _PROG[0] is None:
        _PROG[0] = build_program()
    nc = _PROG[0]
    shared = dict(
        wq=np.ascontiguousarray(inputs["L0_attn_wq"], dtype=np.float32),
        wk=np.ascontiguousarray(inputs["L0_attn_wk"], dtype=np.float32),
        wv=np.ascontiguousarray(inputs["L0_attn_wv"], dtype=np.float32),
        wo_a=np.ascontiguousarray(inputs["L0_attn_wo"], dtype=np.float32),
        bq=np.ascontiguousarray(inputs["L0_attn_bq"], dtype=np.float32),
        bk=np.ascontiguousarray(inputs["L0_attn_bk"], dtype=np.float32),
        bv=np.ascontiguousarray(inputs["L0_attn_bv"], dtype=np.float32),
        bo=np.ascontiguousarray(inputs["L0_attn_bo"], dtype=np.float32),
        wqkv=np.ascontiguousarray(inputs["L0_ffn_wqkv"], dtype=np.float32),
        qscale=np.ascontiguousarray(inputs["L0_ffn_q_scale"], dtype=np.float32),
        kscale=np.ascontiguousarray(inputs["L0_ffn_k_scale"], dtype=np.float32),
        wo_f=np.ascontiguousarray(inputs["L0_ffn_wo"], dtype=np.float32),
        ln_g=np.ascontiguousarray(inputs["L0_ffn_ln_g"], dtype=np.float32),
        ln_b=np.ascontiguousarray(inputs["L0_ffn_ln_b"], dtype=np.float32),
        w1=np.ascontiguousarray(inputs["L0_ffn_w1"], dtype=np.float32),
        b1=np.ascontiguousarray(inputs["L0_ffn_b1"], dtype=np.float32),
        w2=np.ascontiguousarray(inputs["L0_ffn_w2"], dtype=np.float32),
        b2=np.ascontiguousarray(inputs["L0_ffn_b2"], dtype=np.float32),
        iden=_CONST["iden"], ones_row=_CONST["ones_row"], ones_col=_CONST["ones_col"],
        mask_first=_CONST["mask_first"], mask_last=_CONST["mask_last"],
        l2ones=_CONST["l2ones"], rep8=_CONST["rep8"], ones1f=_CONST["ones1f"],
    )
    in_maps = []
    for c in range(8):
        b, h = c // 2, c % 2
        T0 = NOWN * h
        xs = np.ascontiguousarray(np.concatenate([x[b][T0:], x[b][:T0]], axis=0))
        if h == 1:
            xw = np.ascontiguousarray(x[b][T0 - 1536:T0 + NOWN])
            pen = np.zeros((1, WIN), dtype=ml_dtypes.bfloat16)
        else:
            xw = np.ascontiguousarray(
                np.concatenate([np.zeros((1536, D), np.float32), x[b][0:NOWN]], axis=0))
            penf = np.zeros((1, WIN), dtype=np.float32)
            penf[0, :PADTOP] = NEG
            pen = penf.astype(ml_dtypes.bfloat16)
        m = dict(shared)
        m["xseq"] = xs
        m["xwin"] = xw
        m["padpen"] = pen
        m["padedge"] = np.full((128, 1), float(h), dtype=np.float32)
        for i in range(4):
            m[f"tab{i}"] = _TABS[h][i]
        in_maps.append(m)
    res = run_bass_kernel_spmd(nc, in_maps, core_ids=list(range(8)))
    out = np.empty((B, S, D), dtype=np.float32)
    for c in range(8):
        b, h = c // 2, c % 2
        out[b, NOWN * h:NOWN * h + NOWN] = res.results[c]["out"]
    return out


if __name__ == "__main__":
    build_program()
    print("build ok")
